# version 01
# baseline (speedup 1.0000x reference)
import math
from contextlib import ExitStack

import numpy as np
import ml_dtypes
import concourse.bass as bass
import concourse.mybir as mybir
from concourse.bass_utils import run_bass_kernel_spmd

F32 = mybir.dt.float32
BF16 = mybir.dt.bfloat16
I8 = mybir.dt.int8
AF = mybir.ActivationFunctionType
ALU = mybir.AluOpType
AX = mybir.AxisListType

T = 2048
D = 2048
NCHUNK = D // 128
NTILE = T // 128
EPS = 1e-5
ENGINES = ("sync", "gpsimd", "vector", "scalar", "tensor")


class Sem:
    def __init__(self, h, name):
        self.h = h
        self.name = name
        self.n = 0


class K:

    def __init__(self, nc):
        self.nc = nc
        self.q = {e: [] for e in ENGINES}
        self.ctx = ExitStack()
        self._sems = []
        self.sV = self.sem("ordV")
        self.sS = self.sem("ordS")
        self.sG = self.sem("ordG")
        self.sT = self.sem("ordT")
        self._ord = {"vector": self.sV, "scalar": self.sS, "gpsimd": self.sG}

    def sem(self, name):
        h = self.ctx.enter_context(self.nc.semaphore(name))
        s = Sem(h, name)
        self._sems.append(s)
        return s

    def sb(self, name, shape, dtype):
        return self.ctx.enter_context(self.nc.sbuf_tensor(name, shape, dtype))

    def ps(self, name, shape, dtype=F32):
        return self.ctx.enter_context(self.nc.psum_tensor(name, shape, dtype))

    def _emit(self, eng, fn, waits, incs):
        wl = []
        for w in waits:
            if isinstance(w, Sem):
                wl.append((w.h, w.n))
            else:
                s, v = w
                if v > 0:
                    wl.append((s.h, v))
        best = {}
        for h, v in wl:
            key = id(h)
            if key not in best or best[key][1] < v:
                best[key] = (h, v)
        il = []
        for s, kk in incs:
            s.n += kk
            il.append((s.h, kk))
        self.q[eng].append((list(best.values()), il, fn))

    def on(self, eng, fn, waits=(), inc=None, inc_k=1):
        waits = list(waits)
        incs = []
        if eng in self._ord:
            o = self._ord[eng]
            waits.append((o, o.n))
            incs.append((o, 1))
            assert inc is None or inc is o, "V/S/G ops always inc their order sem"
        elif inc is not None:
            incs.append((inc, inc_k))
        self._emit(eng, fn, waits, incs)

    def dma(self, eng, out, in_, waits=(), inc=None, **kw):
        incs = [(inc, 16)] if inc is not None else []
        self._emit(eng, lambda e: e.dma_start(out=out, in_=in_, **kw),
                   list(waits), incs)

    def wait_all(self, eng):
        wl = [(s.h, s.n) for s in self._sems if s.n > 0]
        self.q[eng].append((wl, [], None))

    def barrier(self):
        for e in ENGINES:
            self.wait_all(e)

    def finalize(self):
        nc = self.nc
        with nc.Block() as blk:
            def runner(name):
                def go(e):
                    for wl, il, fn in self.q[name]:
                        for h, v in wl:
                            if v > 0:
                                e.wait_ge(h, v)
                        if fn is None:
                            continue
                        inst = fn(e)
                        for h, kk in il:
                            inst = inst.then_inc(h, kk)
                return go
            blk.sync(runner("sync"))
            blk.gpsimd(runner("gpsimd"))
            blk.vector(runner("vector"))
            blk.scalar(runner("scalar"))
            blk.tensor(runner("tensor"))
        self.ctx.close()


def _colmajor(v):
    v = np.asarray(v).reshape(-1)
    return np.ascontiguousarray(v.reshape(-1, 128).T)


def prep_weights(i):
    w = {}
    w["ident_bf"] = np.eye(128, dtype=ml_dtypes.bfloat16)
    w["ident_f32"] = np.eye(128, dtype=np.float32)
    w["ones_bf"] = np.ones((1, 128), dtype=ml_dtypes.bfloat16)
    w["ones_f32"] = np.ones((1, 128), dtype=np.float32)
    w["ones_colbf"] = np.ones((128, 1), dtype=ml_dtypes.bfloat16)
    w["maskU"] = np.triu(np.ones((128, 128))).astype(ml_dtypes.bfloat16)
    for o in range(4):
        kk = np.arange(128)[:, None] + o * 128
        qq = np.arange(512)[None, :]
        w[f"maskD{o}"] = (kk <= qq).astype(ml_dtypes.bfloat16)

    for nm in ["Wq", "Wk", "Wv", "Wg", "Wo"]:
        w[f"L0_{nm}T"] = np.ascontiguousarray(np.asarray(i[f"L0_attn_{nm}"]).T)
    w["L0_maa_w1"] = np.asarray(i["L0_attn_maa_w1"])
    w["L0_maa_w2"] = np.asarray(i["L0_attn_maa_w2"])
    w2p = np.zeros((5, 64, 2048), np.float32)
    mw2 = np.asarray(i["L0_attn_maa_w2"])
    for f in range(4):
        off = 32 * (f % 2)
        w2p[f, off:off + 32, :] = mw2[f]
    w2p[4, 0:32, :] = mw2[4]
    w["L0_w2p"] = w2p
    w["L0_decay_w1"] = np.asarray(i["L0_attn_decay_w1"])
    w["L0_decay_w2"] = np.asarray(i["L0_attn_decay_w2"])
    for nm in ["maa_x", "maa_r", "maa_k", "maa_v", "maa_w", "maa_g", "time_decay"]:
        w[f"L0_{nm}c"] = _colmajor(i[f"L0_attn_{nm}"])
    w["L0_bgc"] = _colmajor(i["L0_attn_bg"])
    w["L0_w_gate"] = np.asarray(i["L0_ffn_w_gate"])
    w["L0_w_up"] = np.asarray(i["L0_ffn_w_up"])
    w["L0_w_down"] = np.asarray(i["L0_ffn_w_down"])
    w["L0_sgc"] = _colmajor(i["L0_ffn_s_gate"])
    w["L0_suc"] = _colmajor(i["L0_ffn_s_up"])
    w["L0_sdc"] = _colmajor(i["L0_ffn_s_down"])
    w["L1_WQ"] = np.asarray(i["L1_attn_W_Q"])
    w["L1_WK"] = np.asarray(i["L1_attn_W_K"])
    w["L1_WV_flat"] = np.ascontiguousarray(
        np.asarray(i["L1_attn_W_V"]).transpose(1, 0, 2).reshape(2048, 2048))
    w["L1_WO"] = np.asarray(i["L1_attn_W_O"])
    w["L1_bQc"] = np.ascontiguousarray(np.asarray(i["L1_attn_b_Q"]).T)
    w["L1_bKc"] = np.ascontiguousarray(np.asarray(i["L1_attn_b_K"]).T)
    w["L1_bV_row"] = np.asarray(i["L1_attn_b_V"]).reshape(1, 2048)
    w["L1_bO_row"] = np.asarray(i["L1_attn_b_O"]).reshape(1, 2048)
    w["L1_Wg"] = np.asarray(i["L1_ffn_Wg"])
    w["L1_bg_row"] = np.ascontiguousarray(np.asarray(i["L1_ffn_bg"]).reshape(2, 1, 8))
    w["L1_Wup"] = np.asarray(i["L1_ffn_Wup"])
    w["L1_bupc"] = np.ascontiguousarray(
        np.asarray(i["L1_ffn_bup"]).reshape(6, 8, 128).transpose(0, 2, 1))
    w["L1_Wsw"] = np.asarray(i["L1_ffn_Wsw"])
    w["L1_bswc"] = np.ascontiguousarray(
        np.asarray(i["L1_ffn_bsw"]).reshape(6, 8, 128).transpose(0, 2, 1))
    w["L1_Wdn"] = np.asarray(i["L1_ffn_Wdn"])
    w["L1_bdn_row"] = np.ascontiguousarray(np.asarray(i["L1_ffn_bdn"]).reshape(6, 1, 2048))
    w["L1_Wsu"] = np.asarray(i["L1_ffn_Wsu"])
    w["L1_bsuc"] = _colmajor(i["L1_ffn_bsu"])
    w["L1_Wsa"] = np.asarray(i["L1_ffn_Wsa"])
    w["L1_bsac"] = _colmajor(i["L1_ffn_bsa"])
    w["L1_Wsd"] = np.asarray(i["L1_ffn_Wsd"])
    w["L1_bsd_row"] = np.asarray(i["L1_ffn_bsd"]).reshape(1, 2048)
    return {k: np.ascontiguousarray(v) for k, v in w.items()}


def declare_weight_inputs(nc, w):
    dt_map = {np.dtype(np.float32): F32, np.dtype(ml_dtypes.bfloat16): BF16,
              np.dtype(np.int8): I8}
    aps = {}
    for kk, v in w.items():
        aps[kk] = nc.dram_tensor(kk, list(v.shape), dt_map[v.dtype],
                                 kind="ExternalInput").ap()
    return aps


def emit_combine_ln(k, g, x_in_f32, adds, x_out_f32, xn_tok_bf, xn_f32, out_is_final=False):
    sX = k.sem("cl_x")
    sV = k.sV
    sS = k.sS
    sO = k.sem("cl_o")
    with ExitStack() as st:
        sb = lambda n, s, d: st.enter_context(k.nc.sbuf_tensor(n, s, d))
        xb = sb("cl_xb", [128, 2 * D], F32)
        ab = sb("cl_ab", [128, 2 * D], BF16)
        fb = sb("cl_fb", [128, 2 * D], BF16)
        sq = sb("cl_sq", [128, D], F32)
        x1 = sb("cl_x1", [128, 2 * D], F32)
        xn = sb("cl_xn", [128, D], BF16)
        xnf = sb("cl_xnf", [128, D], F32)
        cols = sb("cl_cols", [128, 8], F32)

        for t in range(NTILE):
            db = t % 2
            xs = xb[:, db * D:(db + 1) * D]
            x1s = x1[:, db * D:(db + 1) * D]
            war = [Sem.__new__(Sem)]
            war = ([(sV, sV.n), (sO, sO.n), (sS, sS.n)] if t >= 2 else [])
            k.dma("sync", xs, x_in_f32[t * 128:(t + 1) * 128, :], waits=war, inc=sX)
            srcs = []
            for idx, (ap, kind) in enumerate(adds):
                buf = (ab if idx == 0 else fb)[:, db * D:(db + 1) * D]
                if kind == "tok_bf":
                    k.dma("sync", buf, ap[t * 128:(t + 1) * 128, :], waits=war, inc=sX)
                else:
                    k.dma("sync", buf, ap[:, t * 128:(t + 1) * 128], waits=war, inc=sX,
                          transpose=True)
                srcs.append(buf)
            dmaw = [(sX, sX.n), (sO, sO.n), (sS, sS.n)]
            if not srcs:
                k.on("vector", lambda e, a=xs, o=x1s: e.tensor_copy(o, a), dmaw, sV)
            else:
                k.on("vector", lambda e, a=xs, b=srcs[0], o=x1s:
                     e.tensor_tensor(o, a, b, ALU.add), dmaw, sV)
                for b2 in srcs[1:]:
                    k.on("vector", lambda e, a=x1s, b=b2, o=x1s:
                         e.tensor_tensor(o, a, b, ALU.add), [], sV)
            if x_out_f32 is not None:
                k.dma("sync", x_out_f32[t * 128:(t + 1) * 128, :], x1s,
                      waits=[(sV, sV.n)], inc=sO)
            if xn_tok_bf is None and xn_f32 is None:
                continue
            k.on("scalar", lambda e, a=x1s: e.activation(sq[:], a, AF.Square,
                 accum_out=cols[:, 1:2]), [(sV, sV.n)], sS)
            k.on("vector", lambda e, a=x1s: e.tensor_reduce(cols[:, 0:1], a, AX.X,
                 ALU.add), [], sV)
            k.on("vector", lambda e: e.tensor_scalar_mul(cols[:, 2:3], cols[:, 0:1],
                 1.0 / D), [], sV)
            k.on("vector", lambda e: e.tensor_tensor(cols[:, 3:4], cols[:, 2:3],
                 cols[:, 2:3], ALU.mult), [], sV)
            k.on("vector", lambda e: e.tensor_scalar_mul(cols[:, 4:5], cols[:, 1:2],
                 1.0 / D), [(sS, sS.n)], sV)
            k.on("vector", lambda e: e.tensor_tensor(cols[:, 4:5], cols[:, 4:5],
                 cols[:, 3:4], ALU.subtract), [], sV)
            k.on("vector", lambda e: e.tensor_scalar_add(cols[:, 4:5], cols[:, 4:5],
                 float(EPS)), [], sV)
            k.on("scalar", lambda e: e.activation(cols[:, 5:6], cols[:, 4:5], AF.Sqrt),
                 [(sV, sV.n)], sS)
            k.on("vector", lambda e: e.reciprocal(cols[:, 6:7], cols[:, 5:6]),
                 [(sS, sS.n)], sV)
            if xn_tok_bf is not None:
                k.on("vector", lambda e, a=x1s: e.tensor_scalar(xn[:], a, cols[:, 2:3],
                     cols[:, 6:7], ALU.subtract, ALU.mult), [], sV)
                k.dma("sync", xn_tok_bf[t * 128:(t + 1) * 128, :], xn[:],
                      waits=[(sV, sV.n)], inc=sO)
            if xn_f32 is not None:
                k.on("vector", lambda e, a=x1s: e.tensor_scalar(xnf[:], a, cols[:, 2:3],
                     cols[:, 6:7], ALU.subtract, ALU.mult), [], sV)
                k.dma("sync", xn_f32[t * 128:(t + 1) * 128, :], xnf[:],
                      waits=[(sV, sV.n)], inc=sO)
        k.barrier()


def emit_load_fm(k, xnT, xn_tok_bf, sem, waits=()):
    for c in range(NCHUNK):
        k.dma("sync", xnT[:, c * T:(c + 1) * T],
              xn_tok_bf[:, c * 128:(c + 1) * 128], waits=waits, inc=sem,
              transpose=True)


def emit_l0_ffn(k, g, xnT, f0_fm):
    I = 8192
    NI = I // 128
    TG = 512
    sW = k.sem("ff_w")
    sWp = [k.sem("ff_wp0"), k.sem("ff_wp1")]
    sWd = [k.sem("ff_wd0"), k.sem("ff_wd1")]
    sOp = [k.sem("ff_op0"), k.sem("ff_op1")]
    sT = k.sT
    sV = k.sV
    sS = k.sS
    with ExitStack() as st:
        sb = lambda n, s, d: st.enter_context(k.nc.sbuf_tensor(n, s, d))
        ps = lambda n, s: st.enter_context(k.nc.psum_tensor(n, s, F32))
        wg = sb("ff_wg", [128, 2 * NCHUNK * 128], BF16)
        wu = sb("ff_wu", [128, 2 * NCHUNK * 128], BF16)
        wd = sb("ff_wd", [128, 2 * D], BF16)
        fuse = sb("ff_fuse", [128, NI * TG], BF16)
        gsb = sb("ff_gsb", [128, TG], BF16)
        outb = sb("ff_outb", [128, 8 * TG], BF16)
        scg = sb("ff_scg", [128, 64], F32)
        scu = sb("ff_scu", [128, 64], F32)
        scd = sb("ff_scd", [128, 16], F32)
        psg = ps("ff_psg", [128, TG])
        psu = ps("ff_psu", [128, TG])
        psd = [ps(f"ff_psd{j}", [128, TG]) for j in range(4)]

        k.dma("gpsimd", scg[:], g["L0_sgc"], inc=sW)
        k.dma("gpsimd", scu[:], g["L0_suc"], inc=sW)
        k.dma("gpsimd", scd[:], g["L0_sdc"], inc=sW)

        for tg in range(T // TG):
            t0 = tg * TG
            for i in range(NI):
                dbw = i % 2
                if i == 0:
                    for nm, dst in (("L0_w_gate", wg), ("L0_w_up", wu)):
                        src = g[nm][:, 0:128].rearrange("(c p) i -> p c i", p=128)
                        d3 = dst[:, 0:2048].rearrange("p (c i) -> p c i", c=NCHUNK)
                        k.dma("gpsimd", d3, src, waits=[(sT, sT.n)], inc=sWp[0])
                wmark = (sWp[dbw], sWp[dbw].n)
                if i + 1 < NI:
                    warw = [(sT, sT.n)]
                    for nm, dst in (("L0_w_gate", wg), ("L0_w_up", wu)):
                        src = g[nm][:, (i + 1) * 128:(i + 2) * 128].rearrange(
                            "(c p) i -> p c i", p=128)
                        d3 = dst[:, (1 - dbw) * 2048:(2 - dbw) * 2048].rearrange(
                            "p (c i) -> p c i", c=NCHUNK)
                        k.dma("gpsimd", d3, src, waits=warw, inc=sWp[1 - dbw])
                for e in range(NCHUNK):
                    k.on("tensor", lambda e_, dbw=dbw, ec=e, t0=t0: e_.matmul(
                        psg[:], wg[:, dbw * 2048 + ec * 128:dbw * 2048 + ec * 128 + 128],
                        xnT[:, ec * T + t0:ec * T + t0 + TG],
                        start=(ec == 0), stop=(ec == NCHUNK - 1)),
                        [wmark, (sW, sW.n), (sS, sS.n)] if e == 0 else [],
                        inc=sT if e == NCHUNK - 1 else None)
                for e in range(NCHUNK):
                    k.on("tensor", lambda e_, dbw=dbw, ec=e, t0=t0: e_.matmul(
                        psu[:], wu[:, dbw * 2048 + ec * 128:dbw * 2048 + ec * 128 + 128],
                        xnT[:, ec * T + t0:ec * T + t0 + TG],
                        start=(ec == 0), stop=(ec == NCHUNK - 1)),
                        [(sV, sV.n)] if e == 0 else [],
                        inc=sT if e == NCHUNK - 1 else None)
                if getattr(k, "_ffn_dbg2", None) is not None and tg == 0 and i == 0:
                    k.on("vector", lambda e_: e_.tensor_copy(k._ffn_dbg2_sb[:], psg[:]),
                         [(sT, sT.n - 1)])
                    k.dma("sync", k._ffn_dbg2, k._ffn_dbg2_sb[:], waits=[(sV, sV.n)], inc=sO)
                    k.dma("sync", k._ffn_dbg3, wg[:, 0:2048], waits=[(sT, sT.n)], inc=sO)
                    k.dma("sync", k._ffn_dbg4, xnT[:, 0:2048], inc=sO)
                k.on("scalar", lambda e_, i=i: e_.activation(gsb[:], psg[:], AF.Silu,
                     scale=scg[:, i:i + 1]), [(sT, sT.n - 1), (sV, sV.n)], sS)
                k.on("vector", lambda e_, i=i: e_.scalar_tensor_tensor(
                     fuse[:, i * TG:(i + 1) * TG], psu[:], scu[:, i:i + 1], gsb[:],
                     ALU.mult, ALU.mult), [(sT, sT.n), (sS, sS.n)], sV)
            if getattr(k, "_ffn_dbg", None) is not None and tg == 0:
                for i in range(NI):
                    k.dma("sync", k._ffn_dbg[i * 128:(i + 1) * 128, 0:TG],
                          fuse[:, i * TG:(i + 1) * TG], waits=[(sV, sV.n)], inc=sO)
            fuse_mark = sV.n
            for dh in range(4):
                for i in range(NI):
                    dbw = i % 2
                    if i == 0:
                        k.dma("gpsimd", wd[:, 0:D], g["L0_w_down"][0:128, :],
                              waits=[(sT, sT.n)], inc=sWd[0])
                    wmark = (sWd[dbw], sWd[dbw].n)
                    if i + 1 < NI:
                        k.dma("gpsimd", wd[:, (1 - dbw) * D:(2 - dbw) * D],
                              g["L0_w_down"][(i + 1) * 128:(i + 2) * 128, :],
                              waits=[(sT, sT.n)], inc=sWd[1 - dbw])
                    for j in range(4):
                        dc = dh * 4 + j
                        k.on("tensor", lambda e_, dbw=dbw, dc=dc, i=i, j=j: e_.matmul(
                            psd[j][:], wd[:, dbw * D + dc * 128:dbw * D + dc * 128 + 128],
                            fuse[:, i * TG:(i + 1) * TG],
                            start=(i == 0), stop=(i == NI - 1)),
                            [wmark, (sV, fuse_mark), (sS, sS.n)] if j == 0 and i == 0
                            else ([wmark] if j == 0 else []),
                            inc=sT if j == 3 else None)
                par = dh % 2
                for j in range(4):
                    dc = dh * 4 + j
                    slot = outb[:, (par * 4 + j) * TG:(par * 4 + j + 1) * TG]
                    k.on("scalar", lambda e_, dc=dc, j=j, slot=slot: e_.activation(
                         slot, psd[j][:], AF.Copy, scale=scd[:, dc:dc + 1]),
                         [(sT, sT.n), (sOp[par], sOp[par].n)], sS)
                    k.dma("sync", f0_fm[dc * 128:(dc + 1) * 128, t0:t0 + TG],
                          slot, waits=[(sS, sS.n)], inc=sOp[par])
        k.barrier()


def dram(nc, name, shape, dtype, kind="Internal"):
    return nc.dram_tensor(name, list(shape), dtype, kind=kind).ap()


def build_ln0():
    nc = bass.Bass("TRN2", target_bir_lowering=False)
    x0 = dram(nc, "x0", [T, D], F32, "ExternalInput")
    xn0 = dram(nc, "xn0", [T, D], BF16, "ExternalOutput")
    k = K(nc)
    emit_combine_ln(k, None, x0, [], None, xn0, None)
    k.finalize()
    return nc


def build_l0_ffn(w, debug=False):
    nc = bass.Bass("TRN2", target_bir_lowering=False)
    g = declare_weight_inputs(nc, {kk: w[kk] for kk in
        ["L0_w_gate", "L0_w_up", "L0_w_down", "L0_sgc", "L0_suc", "L0_sdc"]})
    xn0 = dram(nc, "xn0", [T, D], BF16, "ExternalInput")
    f0 = dram(nc, "f0_fm", [D, T], BF16, "ExternalOutput")
    k = K(nc)
    if debug:
        k._ffn_dbg = dram(nc, "fuse_dbg", [8192, 512], BF16, "ExternalOutput")
        k._ffn_dbg2 = dram(nc, "psg_dbg", [128, 512], F32, "ExternalOutput")
        k._ffn_dbg3 = dram(nc, "slab_dbg", [128, 2048], BF16, "ExternalOutput")
        k._ffn_dbg4 = dram(nc, "xnT_dbg", [128, 2048], BF16, "ExternalOutput")
        k._ffn_dbg2_sb = k.sb("dbg2sb", [128, 512], F32)
    sL = k.sem("ld_fm")
    xnT = k.sb("xnT", [128, NCHUNK * T], BF16)
    emit_load_fm(k, xnT, xn0, sL)
    k.barrier()
    emit_l0_ffn(k, g, xnT, f0)
    k.finalize()
    return nc


def build_combine0(w):
    nc = bass.Bass("TRN2", target_bir_lowering=False)
    x0 = dram(nc, "x0", [T, D], F32, "ExternalInput")
    a0 = dram(nc, "attn0_fm", [D, T], BF16, "ExternalInput")
    f0 = dram(nc, "f0_fm", [D, T], BF16, "ExternalInput")
    x1 = dram(nc, "x1", [T, D], F32, "ExternalOutput")
    xn1 = dram(nc, "xn1", [T, D], BF16, "ExternalOutput")
    xn1f = dram(nc, "xn1f", [T, D], F32, "ExternalOutput")
    k = K(nc)
    emit_combine_ln(k, None, x0, [(a0, "fm_bf"), (f0, "fm_bf")], x1, xn1, xn1f)
    k.finalize()
    return nc


def build_combine1():
    nc = bass.Bass("TRN2", target_bir_lowering=False)
    x1 = dram(nc, "x1", [T, D], F32, "ExternalInput")
    a1 = dram(nc, "attn1_tok", [T, D], BF16, "ExternalInput")
    f1 = dram(nc, "ffn1_tok", [T, D], BF16, "ExternalInput")
    out = dram(nc, "out", [T, D], F32, "ExternalOutput")
    k = K(nc)
    emit_combine_ln(k, None, x1, [(a1, "tok_bf"), (f1, "tok_bf")], out, None, None)
    k.finalize()
    return nc


_L0A_KEYS = ["L0_maa_w1", "L0_w2p", "L0_decay_w1", "L0_decay_w2", "maskU",
             "L0_maa_xc", "L0_maa_rc", "L0_maa_kc", "L0_maa_vc", "L0_maa_wc",
             "L0_maa_gc", "L0_time_decayc", "L0_bgc",
             "L0_WqT", "L0_WkT", "L0_WvT", "L0_WgT", "L0_WoT"]
_L0F_KEYS = ["L0_w_gate", "L0_w_up", "L0_w_down", "L0_sgc", "L0_suc", "L0_sdc"]
_L1A_KEYS = ["L1_WQ", "L1_WK", "L1_WV_flat", "L1_WO", "L1_bQc", "L1_bKc",
             "L1_bV_row", "L1_bO_row", "ones_bf", "ones_f32", "ones_colbf",
             "maskD0", "maskD1", "maskD2", "maskD3"]


def kernel(**inputs):
    NC = 8
    cores = list(range(NC))
    w = prep_weights(inputs)
    x_all = np.ascontiguousarray(np.asarray(inputs["L0_attn_hidden_states"],
                                            dtype=np.float32))

    def launch(nc, per_core_fn):
        res = run_bass_kernel_spmd(nc, [per_core_fn(c) for c in cores],
                                   core_ids=cores)
        return res.results

    r = launch(build_ln0(), lambda c: dict(x0=x_all[c]))
    xn0 = [r[c]["xn0"] for c in cores]
    wa = {kk: w[kk] for kk in _L0A_KEYS}
    r = launch(build_l0_attn(w), lambda c: dict(xn0=xn0[c], **wa))
    a0 = [r[c]["attn0_fm"] for c in cores]
    wf = {kk: w[kk] for kk in _L0F_KEYS}
    r = launch(build_l0_ffn(w), lambda c: dict(xn0=xn0[c], **wf))
    f0 = [r[c]["f0_fm"] for c in cores]
    r = launch(build_combine0(w), lambda c: dict(x0=x_all[c], attn0_fm=a0[c],
                                                 f0_fm=f0[c]))
    x1 = [r[c]["x1"] for c in cores]
    xn1 = [r[c]["xn1"] for c in cores]
    xn1f = [r[c]["xn1f"] for c in cores]
    wm = {kk: w[kk] for kk in _L1A_KEYS}
    r = launch(build_l1_attn(w), lambda c: dict(xn1=xn1[c], **wm))
    a1 = [r[c]["attn1_tok"] for c in cores]
    wo = {kk: w[kk] for kk in MOE_KEYS}
    r = launch(build_l1_moe(w), lambda c: dict(xn1=xn1[c], xn1f=xn1f[c], **wo))
    f1 = [r[c]["ffn1_tok"] for c in cores]
    r = launch(build_combine1(), lambda c: dict(x1=x1[c], attn1_tok=a1[c],
                                                ffn1_tok=f1[c]))
    out = np.stack([np.asarray(r[c]["out"], dtype=np.float32) for c in cores])
    return out


def emit_l0_attn(k, g, xn_tok, kend_dram, v_dram, attn0_fm, wbf, phases=99, ntau=None):
    TAU = 512
    NTAU = T // TAU
    sX = k.sem("ra_x")
    sW = k.sem("ra_w")
    sWs = [k.sem("ra_ws0"), k.sem("ra_ws1")]
    sKVp = [k.sem("ra_kv0"), k.sem("ra_kv1")]
    sAOp = [k.sem("ra_ao0"), k.sem("ra_ao1")]
    sT, sV, sS = k.sT, k.sV, k.sS
    with ExitStack() as st:
        sb = lambda n, s, d: st.enter_context(k.nc.sbuf_tensor(n, s, d))
        ps = lambda n, s: st.enter_context(k.nc.psum_tensor(n, s, F32))
        xnT = sb("ra_xnT", [128, NCHUNK * TAU], BF16)
        xnprev = sb("ra_xnprev", [128, NCHUNK], BF16)
        S = sb("ra_S", [128, NCHUNK * 64], F32)
        Sbf = sb("ra_Sbf", [128, NCHUNK * 64], BF16)
        expL = sb("ra_expL", [128, NCHUNK * TAU], BF16)
        qd = sb("ra_qd", [128, NCHUNK * TAU], BF16)
        kd = sb("ra_kd", [128, NCHUNK * TAU], BF16)
        gF = sb("ra_gF", [128, NCHUNK * TAU], BF16)
        xf = sb("ra_xf", [128, NCHUNK * TAU], BF16)
        work = sb("ra_work", [128, 8192], BF16)
        yg = sb("ra_yg", [128, NCHUNK * TAU], BF16)
        t501 = sb("ra_t501", [64, TAU], BF16)
        t523 = sb("ra_t523", [64, TAU], BF16)
        t54 = sb("ra_t54", [32, TAU], BF16)
        d1 = sb("ra_d1", [64, TAU], BF16)
        w1sb = sb("ra_w1sb", [128, NCHUNK * 160], BF16)
        w2sb = sb("ra_w2sb", [64, 5 * 2048], BF16)
        dw1sb = sb("ra_dw1sb", [128, NCHUNK * 64], BF16)
        dw2sb = sb("ra_dw2sb", [64, 2048], BF16)
        cvec = sb("ra_cvec", [128, 8 * NCHUNK], F32)
        wslab = sb("ra_wslab", [128, 2 * 2048], BF16)
        e1b = sb("ra_e1b", [128, TAU], F32)
        cumE = sb("ra_cumE", [128, TAU], F32)
        tmpR = sb("ra_tmpR", [128, TAU], F32)
        elogw = sb("ra_elogw", [128, TAU], F32)
        onem = sb("ra_onem", [128, TAU], BF16)
        expmL = sb("ra_expmL", [128, TAU], BF16)
        expR = sb("ra_expR", [128, TAU], BF16)
        k2b = sb("ra_k2b", [128, TAU], BF16)
        stg = sb("ra_stg", [128, 2 * TAU], BF16)
        mmf = sb("ra_mmf", [128, TAU], BF16)
        expLcs = sb("ra_expLcs", [128, 64], F32)
        Asb = sb("ra_Asb", [128, 256], BF16)
        masku = sb("ra_masku", [128, 128], BF16)
        outb = sb("ra_outb", [128, 2 * TAU], BF16)
        zeros = sb("ra_zeros", [128, 128], F32)
        pool = [ps(f"ra_p{j}", [128, TAU]) for j in range(4)]
        psAe = ps("ra_psAe", [128, 512])
        psAo = ps("ra_psAo", [128, 512])
        psY = ps("ra_psY", [128, 128])
        psDS = ps("ra_psDS", [128, 64])

        MAA = {"r": 1, "k": 2, "v": 3, "w": 4, "g": 5}
        T5F = {"r": 0, "k": 1, "v": 2, "w": 3, "g": 4}
        k.dma("gpsimd", w1sb[:].rearrange("p (c m) -> p c m", c=NCHUNK),
              g["L0_maa_w1"].rearrange("(c p) m -> p c m", p=128), inc=sW)
        for f in range(5):
            k.dma("gpsimd", w2sb[:, f * 2048:(f + 1) * 2048], g["L0_w2p"][f], inc=sW)
        k.dma("gpsimd", dw1sb[:].rearrange("p (c m) -> p c m", c=NCHUNK),
              g["L0_decay_w1"].rearrange("(c p) m -> p c m", p=128), inc=sW)
        k.dma("gpsimd", dw2sb[:], g["L0_decay_w2"], inc=sW)
        for j, nm in enumerate(["maa_x", "maa_r", "maa_k", "maa_v", "maa_w", "maa_g",
                                "time_decay"]):
            k.dma("gpsimd", cvec[:, j * NCHUNK:(j + 1) * NCHUNK], g[f"L0_{nm}c"], inc=sW)
        k.dma("gpsimd", cvec[:, 7 * NCHUNK:8 * NCHUNK], g["L0_bgc"], inc=sW)
        k.dma("gpsimd", masku[:], g["maskU"], inc=sW)
        k.on("vector", lambda e: e.memset(zeros[:], 0.0))
        k.on("vector", lambda e: e.memset(S[:], 0.0))
        k.on("vector", lambda e: e.memset(Sbf[:], 0.0))
        k.on("vector", lambda e: e.memset(xnprev[:], 0.0))
        wpre = sW.n

        def slabload(dst_slot, srcap, waits):
            d3 = wslab[:, dst_slot * 2048:(dst_slot + 1) * 2048].rearrange(
                "p (c i) -> p c i", c=NCHUNK)
            s3 = srcap.rearrange("(c p) i -> p c i", p=128)
            k.dma("gpsimd", d3, s3, waits=waits, inc=sWs[dst_slot])

        for tg in range(ntau if ntau is not None else NTAU):
            t0 = tg * TAU
            if tg > 0:
                k.on("vector", lambda e: e.tensor_copy(
                    xnprev[:], xnT[:, TAU - 1::TAU]))
            xw_mark = (sV, sV.n)
            for c in range(NCHUNK):
                k.dma("sync", xnT[:, c * TAU:(c + 1) * TAU],
                      xn_tok[t0:t0 + TAU, c * 128:(c + 1) * 128],
                      waits=[xw_mark, (sT, sT.n)], inc=sX, transpose=True)
            xn_mark = (sX, sX.n)
            dx = work[:, 0:NCHUNK * TAU]
            for c in range(NCHUNK):
                k.on("vector", lambda e, c=c: e.tensor_sub(
                    dx[:, c * TAU + 1:(c + 1) * TAU],
                    xnT[:, c * TAU:(c + 1) * TAU - 1],
                    xnT[:, c * TAU + 1:(c + 1) * TAU]),
                    [xn_mark, (sT, sT.n)])
                k.on("vector", lambda e, c=c: e.tensor_sub(
                    dx[:, c * TAU:c * TAU + 1], xnprev[:, c:c + 1],
                    xnT[:, c * TAU:c * TAU + 1]), [])

            def mix(f):
                if f == "x":
                    for c in range(NCHUNK):
                        k.on("vector", lambda e, c=c: e.scalar_tensor_tensor(
                            xf[:, c * TAU:(c + 1) * TAU], dx[:, c * TAU:(c + 1) * TAU],
                            cvec[:, 0 * NCHUNK + c:0 * NCHUNK + c + 1],
                            xnT[:, c * TAU:(c + 1) * TAU], ALU.mult, ALU.add),
                            [(sW, wpre)])
                    return
                fi = T5F[f]
                rhs = {0: t501, 1: t501, 2: t523, 3: t523, 4: t54}[fi]
                kdim = 32 if fi == 4 else 64
                for c in range(NCHUNK):
                    slot = pool[2 + c % 2]
                    k.on("tensor", lambda e, c=c, fi=fi, rhs=rhs, kdim=kdim, slot=slot:
                         e.matmul(slot[:], w2sb[0:kdim, fi * 2048 + c * 128:
                                                fi * 2048 + c * 128 + 128],
                                  rhs[0:kdim, :], start=True, stop=True),
                         [(sS, sS.n), (sV, sV.n)], inc=sT)
                    k.on("scalar", lambda e, c=c, f=f, slot=slot: e.activation(
                         mmf[:], slot[:], AF.Identity,
                         bias=cvec[:, MAA[f] * NCHUNK + c:MAA[f] * NCHUNK + c + 1]),
                         [(sT, sT.n), (sW, wpre)])
                    k.on("vector", lambda e, c=c: e.tensor_mul(
                         xf[:, c * TAU:(c + 1) * TAU], dx[:, c * TAU:(c + 1) * TAU],
                         mmf[:]), [(sS, sS.n)])
                    k.on("vector", lambda e, c=c: e.tensor_add(
                         xf[:, c * TAU:(c + 1) * TAU], xf[:, c * TAU:(c + 1) * TAU],
                         xnT[:, c * TAU:(c + 1) * TAU]), [])

            def proj16(c, slot, slab_slot):
                for e_ in range(NCHUNK):
                    k.on("tensor", lambda e, c=c, e_=e_, slot=slot, ss=slab_slot:
                         e.matmul(pool[slot][:],
                                  wslab[:, ss * 2048 + e_ * 128:ss * 2048 + e_ * 128 + 128],
                                  xf[:, e_ * TAU:(e_ + 1) * TAU],
                                  start=(e_ == 0), stop=(e_ == NCHUNK - 1)),
                         [(sWs[slab_slot], sWs[slab_slot].n), (sV, sV.n), (sS, sS.n)]
                         if e_ == 0 else [],
                         inc=sT if e_ == NCHUNK - 1 else None)

            mix("x")
            for e_ in range(NCHUNK):
                k.on("tensor", lambda e, e_=e_: e.matmul(
                    pool[0][:], w1sb[:, e_ * 160:e_ * 160 + 128],
                    xf[:, e_ * TAU:(e_ + 1) * TAU],
                    start=(e_ == 0), stop=(e_ == NCHUNK - 1)),
                    [(sW, wpre), (sV, sV.n), (sS, sS.n)] if e_ == 0 else [],
                    inc=sT if e_ == NCHUNK - 1 else None)
            for e_ in range(NCHUNK):
                k.on("tensor", lambda e, e_=e_: e.matmul(
                    pool[1][0:32, :], w1sb[:, e_ * 160 + 128:e_ * 160 + 160],
                    xf[:, e_ * TAU:(e_ + 1) * TAU],
                    start=(e_ == 0), stop=(e_ == NCHUNK - 1)),
                    inc=sT if e_ == NCHUNK - 1 else None)
            k.on("scalar", lambda e: e.activation(t501[:], pool[0][0:64, :], AF.Tanh),
                 [(sT, sT.n)])
            k.on("scalar", lambda e: e.activation(t523[:], pool[0][64:128, :], AF.Tanh), [])
            k.on("scalar", lambda e: e.activation(t54[:], pool[1][0:32, :], AF.Tanh), [])
            if phases < 2:
                continue
            mix("w")
            for e_ in range(NCHUNK):
                k.on("tensor", lambda e, e_=e_: e.matmul(
                    pool[0][0:64, :], dw1sb[:, e_ * 64:(e_ + 1) * 64],
                    xf[:, e_ * TAU:(e_ + 1) * TAU],
                    start=(e_ == 0), stop=(e_ == NCHUNK - 1)),
                    [(sS, sS.n), (sV, sV.n)] if e_ == 0 else [],
                    inc=sT if e_ == NCHUNK - 1 else None)
            k.on("scalar", lambda e: e.activation(d1[:], pool[0][0:64, :], AF.Tanh),
                 [(sT, sT.n)])
            if phases < 3:
                continue
            mix("k")
            slabload(0, wbf["k"][:, 0:128], [(sT, sT.n)])
            for c in range(NCHUNK):
                if c + 1 < NCHUNK:
                    slabload(1 - c % 2, wbf["k"][:, (c + 1) * 128:(c + 2) * 128],
                             [(sT, sT.n)])
                k.on("tensor", lambda e, c=c: e.matmul(
                    pool[0][:], dw2sb[:, c * 128:(c + 1) * 128], d1[:],
                    start=True, stop=True),
                    [(sS, sS.n), (sV, sV.n)], inc=sT)
                k.on("scalar", lambda e, c=c: e.activation(
                    e1b[:], pool[0][:], AF.Exp,
                    bias=cvec[:, 6 * NCHUNK + c:6 * NCHUNK + c + 1]),
                    [(sT, sT.n), (sW, wpre)])
                for q in range(4):
                    k.on("vector", lambda e, q=q: e.tensor_tensor_scan(
                        cumE[:, q * 128:(q + 1) * 128], e1b[:, q * 128:(q + 1) * 128],
                        zeros[:], 0.0, ALU.add, ALU.add), [(sS, sS.n)])
                k.on("scalar", lambda e, c=c: e.activation(
                    expLcs[:, c * 4:(c + 1) * 4],
                    cumE[:, 127:TAU:128], AF.Exp, scale=-1.0), [(sV, sV.n)])
                k.on("scalar", lambda e, c=c: e.activation(
                    expL[:, c * TAU:(c + 1) * TAU], cumE[:], AF.Exp, scale=-1.0), [])
                k.on("scalar", lambda e: e.activation(expmL[:], cumE[:], AF.Exp), [])
                for q in range(4):
                    k.on("vector", lambda e, q=q: e.tensor_single_scalar(
                        tmpR[:, q * 128:(q + 1) * 128], cumE[:, q * 128:(q + 1) * 128],
                        cumE[:, q * 128 + 127:q * 128 + 128], ALU.subtract), [(sS, sS.n)])
                k.on("scalar", lambda e: e.activation(expR[:], tmpR[:], AF.Exp),
                     [(sV, sV.n)])
                k.on("scalar", lambda e: e.activation(elogw[:], e1b[:], AF.Exp,
                     scale=-1.0), [])
                k.on("vector", lambda e: e.tensor_scalar(onem[:], elogw[:], -1.0, 1.0,
                     ALU.mult, ALU.add), [(sS, sS.n)])
                proj16(c, 2 + c % 2, c % 2)
                k.on("vector", lambda e, c=c: e.tensor_mul(k2b[:], pool[2 + c % 2][:],
                     onem[:]), [(sT, sT.n)])
                k.on("vector", lambda e, c=c: e.tensor_mul(kd[:, c * TAU:(c + 1) * TAU],
                     k2b[:], expmL[:]), [])
                k.on("vector", lambda e, c=c: e.tensor_mul(stg[:, (c % 2) * TAU:
                     (c % 2 + 1) * TAU], k2b[:], expR[:]),
                     [(sKVp[c % 2], sKVp[c % 2].n)])
                k.dma("sync", kend_dram[c * 128:(c + 1) * 128, t0:t0 + TAU],
                      stg[:, (c % 2) * TAU:(c % 2 + 1) * TAU],
                      waits=[(sV, sV.n)], inc=sKVp[c % 2])
            if phases < 4:
                continue
            mix("r")
            slabload(0, wbf["q"][:, 0:128], [(sT, sT.n)])
            for c in range(NCHUNK):
                if c + 1 < NCHUNK:
                    slabload(1 - c % 2, wbf["q"][:, (c + 1) * 128:(c + 2) * 128],
                             [(sT, sT.n)])
                proj16(c, 2 + c % 2, c % 2)
                k.on("vector", lambda e, c=c: e.tensor_mul(qd[:, c * TAU:(c + 1) * TAU],
                     pool[2 + c % 2][:], expL[:, c * TAU:(c + 1) * TAU]), [(sT, sT.n)])
            if phases < 5:
                continue
            mix("v")
            slabload(0, wbf["v"][:, 0:128], [(sT, sT.n)])
            for c in range(NCHUNK):
                if c + 1 < NCHUNK:
                    slabload(1 - c % 2, wbf["v"][:, (c + 1) * 128:(c + 2) * 128],
                             [(sT, sT.n)])
                proj16(c, 2 + c % 2, c % 2)
                k.on("vector", lambda e, c=c: e.tensor_copy(
                     stg[:, (c % 2) * TAU:(c % 2 + 1) * TAU], pool[2 + c % 2][:]),
                     [(sT, sT.n), (sKVp[c % 2], sKVp[c % 2].n)])
                k.dma("sync", v_dram[c * 128:(c + 1) * 128, t0:t0 + TAU],
                      stg[:, (c % 2) * TAU:(c % 2 + 1) * TAU],
                      waits=[(sV, sV.n)], inc=sKVp[c % 2])
            if phases < 6:
                continue
            mix("g")
            slabload(0, wbf["g"][:, 0:128], [(sT, sT.n)])
            for c in range(NCHUNK):
                if c + 1 < NCHUNK:
                    slabload(1 - c % 2, wbf["g"][:, (c + 1) * 128:(c + 2) * 128],
                             [(sT, sT.n)])
                proj16(c, 2 + c % 2, c % 2)
                k.on("scalar", lambda e, c=c: e.activation(
                     gF[:, c * TAU:(c + 1) * TAU], pool[2 + c % 2][:], AF.Silu,
                     bias=cvec[:, 7 * NCHUNK + c:7 * NCHUNK + c + 1]),
                     [(sT, sT.n), (sW, wpre)])
            if phases < 7:
                continue
            kv_mark = [(sKVp[0], sKVp[0].n), (sKVp[1], sKVp[1].n)]
            for cc in range(4):
                ket = work[:, (cc % 2) * 2048:(cc % 2) * 2048 + 2048]
                vt = work[:, 4096 + (cc % 2) * 2048:4096 + (cc % 2) * 2048 + 2048]
                k.dma("sync", ket, kend_dram[:, t0 + cc * 128:t0 + (cc + 1) * 128],
                      waits=kv_mark + [(sT, sT.n)], inc=sX, transpose=True)
                k.dma("sync", vt, v_dram[:, t0 + cc * 128:t0 + (cc + 1) * 128],
                      waits=kv_mark + [(sT, sT.n)], inc=sX, transpose=True)
                kvx = (sX, sX.n)
                for hp in range(NCHUNK):
                    cs = slice(hp * TAU + cc * 128, hp * TAU + cc * 128 + 128)
                    k.on("tensor", lambda e, cs=cs: e.matmul(
                        psAe[:, 0:128], kd[0:64, cs], qd[0:64, cs],
                        start=True, stop=True), [(sV, sV.n), kvx])
                    k.on("tensor", lambda e, cs=cs: e.matmul(
                        psAo[:, 0:128], kd[64:128, cs], qd[64:128, cs],
                        start=True, stop=True), [], inc=sT)
                    k.on("vector", lambda e: e.tensor_mul(Asb[:, 0:128], psAe[:, 0:128],
                         masku[:]), [(sT, sT.n)])
                    k.on("vector", lambda e: e.tensor_mul(Asb[:, 128:256],
                         psAo[:, 0:128], masku[:]), [])
                    k.on("tensor", lambda e, hp=hp, vt=vt: e.matmul(
                        psY[0:64, :], vt[:, hp * 128:hp * 128 + 64], Asb[:, 0:128],
                        start=True, stop=False), [(sV, sV.n)])
                    k.on("tensor", lambda e, hp=hp, cs=cs: e.matmul(
                        psY[0:64, :], Sbf[0:64, hp * 64:(hp + 1) * 64], qd[0:64, cs],
                        start=False, stop=True), [])
                    k.on("tensor", lambda e, hp=hp, vt=vt: e.matmul(
                        psY[64:128, :], vt[:, hp * 128 + 64:hp * 128 + 128],
                        Asb[:, 128:256], start=True, stop=False,
                        tile_position=(0, 64)), [])
                    k.on("tensor", lambda e, hp=hp, cs=cs: e.matmul(
                        psY[64:128, :], Sbf[64:128, hp * 64:(hp + 1) * 64],
                        qd[64:128, cs], start=False, stop=True,
                        tile_position=(64, 64)), [], inc=sT)
                    k.on("tensor", lambda e, hp=hp, ket=ket, vt=vt: e.matmul(
                        psDS[0:64, :], ket[:, hp * 128:hp * 128 + 64],
                        vt[:, hp * 128:hp * 128 + 64], start=True, stop=True), [])
                    k.on("tensor", lambda e, hp=hp, ket=ket, vt=vt: e.matmul(
                        psDS[64:128, :], ket[:, hp * 128 + 64:hp * 128 + 128],
                        vt[:, hp * 128 + 64:hp * 128 + 128], start=True, stop=True,
                        tile_position=(0, 64)), [], inc=sT)
                    k.on("vector", lambda e, hp=hp, cc=cc: e.tensor_mul(
                        yg[:, hp * TAU + cc * 128:hp * TAU + cc * 128 + 128], psY[:],
                        gF[:, hp * TAU + cc * 128:hp * TAU + cc * 128 + 128]),
                        [(sT, sT.n - 1), (sS, sS.n)])
                    k.on("vector", lambda e, hp=hp, cc=cc: e.scalar_tensor_tensor(
                        S[:, hp * 64:(hp + 1) * 64], S[:, hp * 64:(hp + 1) * 64],
                        expLcs[:, hp * 4 + cc:hp * 4 + cc + 1], psDS[:],
                        ALU.mult, ALU.add), [(sT, sT.n)])
                    k.on("vector", lambda e, hp=hp: e.tensor_copy(
                        Sbf[:, hp * 64:(hp + 1) * 64], S[:, hp * 64:(hp + 1) * 64]), [])
            if phases < 8:
                continue
            slabload(0, wbf["o"][:, 0:128], [(sT, sT.n)])
            for c in range(NCHUNK):
                if c + 1 < NCHUNK:
                    slabload(1 - c % 2, wbf["o"][:, (c + 1) * 128:(c + 2) * 128],
                             [(sT, sT.n)])
                slot = 2 + c % 2
                for e_ in range(NCHUNK):
                    k.on("tensor", lambda e, c=c, e_=e_, slot=slot: e.matmul(
                         pool[slot][:],
                         wslab[:, (c % 2) * 2048 + e_ * 128:(c % 2) * 2048 + e_ * 128 + 128],
                         yg[:, e_ * TAU:(e_ + 1) * TAU],
                         start=(e_ == 0), stop=(e_ == NCHUNK - 1)),
                         [(sWs[c % 2], sWs[c % 2].n), (sV, sV.n)] if e_ == 0 else [],
                         inc=sT if e_ == NCHUNK - 1 else None)
                k.on("vector", lambda e, c=c, slot=slot: e.tensor_copy(
                     outb[:, (c % 2) * TAU:(c % 2 + 1) * TAU], pool[slot][:]),
                     [(sT, sT.n), (sAOp[c % 2], sAOp[c % 2].n)])
                k.dma("sync", attn0_fm[c * 128:(c + 1) * 128, t0:t0 + TAU],
                      outb[:, (c % 2) * TAU:(c % 2 + 1) * TAU],
                      waits=[(sV, sV.n)], inc=sAOp[c % 2])
        k.barrier()


def build_l0_attn(w, phases=99, ntau=None):
    nc = bass.Bass("TRN2", target_bir_lowering=False)
    keys = ["L0_maa_w1", "L0_w2p", "L0_decay_w1", "L0_decay_w2", "maskU",
            "L0_maa_xc", "L0_maa_rc", "L0_maa_kc", "L0_maa_vc", "L0_maa_wc",
            "L0_maa_gc", "L0_time_decayc", "L0_bgc",
            "L0_WqT", "L0_WkT", "L0_WvT", "L0_WgT", "L0_WoT"]
    g = declare_weight_inputs(nc, {kk: w[kk] for kk in keys})
    xn0 = dram(nc, "xn0", [T, D], BF16, "ExternalInput")
    a0 = dram(nc, "attn0_fm", [D, T], BF16, "ExternalOutput")
    kend_dram = dram(nc, "kend_dram", [D, T], BF16)
    v_dram = dram(nc, "v_dram", [D, T], BF16)
    wbf = {}
    k = K(nc)
    sPre = k.sem("precast")
    for short, nm in [("q", "L0_WqT"), ("k", "L0_WkT"), ("v", "L0_WvT"),
                      ("g", "L0_WgT"), ("o", "L0_WoT")]:
        wbf[short] = dram(nc, nm + "_bf", [D, D], BF16)
        for r0 in range(0, D, 256):
            k.dma("gpsimd", wbf[short][r0:r0 + 256, :], g[nm][r0:r0 + 256, :], inc=sPre)
    k.barrier()
    emit_l0_attn(k, g, xn0, kend_dram, v_dram, a0, wbf, phases, ntau)
    k.finalize()
    return nc


def emit_l1_attn(k, g, xn_tok, qk_dram, v1_dram, attn1_tok):
    TAU = 512
    SC = 1.0 / math.sqrt(128.0)
    sX = k.sem("ma_x")
    sW = k.sem("ma_w")
    sWh = [k.sem("ma_wh0"), k.sem("ma_wh1")]
    sQp = [k.sem("ma_qp0"), k.sem("ma_qp1")]
    sHL = k.sem("ma_hl")
    sAO = [k.sem("ma_ao0"), k.sem("ma_ao1")]
    sT, sV, sS = k.sT, k.sV, k.sS
    with ExitStack() as st:
        sb = lambda n, s, d: st.enter_context(k.nc.sbuf_tensor(n, s, d))
        ps = lambda n, s: st.enter_context(k.nc.psum_tensor(n, s, F32))
        xnT = sb("ma_xnT", [128, NCHUNK * TAU], BF16)
        wqk = sb("ma_wqk", [128, 2 * 2048], BF16)
        wbig = sb("ma_wbig", [128, NCHUNK * 2048], BF16)
        bq = sb("ma_bq", [128, 16], F32)
        bk = sb("ma_bk", [128, 16], F32)
        bvrow = sb("ma_bvrow", [1, 2048], BF16)
        borow = sb("ma_borow", [1, 2048], BF16)
        onesb = sb("ma_onesb", [1, 128], BF16)
        onesf = sb("ma_onesf", [1, 128], F32)
        onescol = sb("ma_onescol", [128, 1], BF16)
        stq = sb("ma_stq", [128, 2 * TAU], BF16)
        vstg = sb("ma_vstg", [128, 2 * 2048], BF16)
        masks = sb("ma_masks", [128, 4 * TAU], BF16)
        qh = sb("ma_qh", [128, 2048], BF16)
        kh = sb("ma_kh", [128, 2048], BF16)
        vh = sb("ma_vh", [128, 2048], BF16)
        sqb = sb("ma_sqb", [128, 2048], BF16)
        ptile = sb("ma_ptile", [128, 2 * TAU], BF16)
        zT = sb("ma_zT", [128, 16 * 2048], BF16)
        ILsb = sb("ma_ILsb", [128, TAU], F32)
        nsc = sb("ma_nsc", [1, 16], F32)
        mhc = sb("ma_mhc", [128, 1], F32)
        aostg = sb("ma_aostg", [128, 2 * 2048], BF16)
        pool = [ps(f"ma_p{j}", [128, TAU]) for j in range(4)]
        psZ = ps("ma_psZ", [128, TAU])
        psL = ps("ma_psL", [128, TAU])
        psB = ps("ma_psB", [128, TAU])

        k.dma("gpsimd", bq[:], g["L1_bQc"], inc=sW)
        k.dma("gpsimd", bk[:], g["L1_bKc"], inc=sW)
        k.dma("gpsimd", bvrow[:], g["L1_bV_row"], inc=sW)
        k.dma("gpsimd", borow[:], g["L1_bO_row"], inc=sW)
        k.dma("gpsimd", onesb[:], g["ones_bf"], inc=sW)
        k.dma("gpsimd", onesf[:], g["ones_f32"], inc=sW)
        k.dma("gpsimd", onescol[:], g["ones_colbf"], inc=sW)
        for o in range(4):
            k.dma("gpsimd", masks[:, o * TAU:(o + 1) * TAU], g[f"maskD{o}"], inc=sW)
        for e_ in range(NCHUNK):
            k.dma("gpsimd", wbig[:, e_ * 2048:(e_ + 1) * 2048],
                  g["L1_WV_flat"][e_ * 128:(e_ + 1) * 128, :], inc=sW)
        wpre = sW.n

        for tg in range(T // TAU):
            t0 = tg * TAU
            for c in range(NCHUNK):
                k.dma("sync", xnT[:, c * TAU:(c + 1) * TAU],
                      xn_tok[t0:t0 + TAU, c * 128:(c + 1) * 128],
                      waits=[(sV, sV.n), (sT, sT.n), (sS, sS.n)], inc=sX,
                      transpose=True)
            xmark = (sX, sX.n)
            for (wname, brow, off) in [("L1_WQ", bq, 0), ("L1_WK", bk, D)]:
                k.dma("gpsimd", wqk[:, 0:2048].rearrange("p (c m) -> p c m", c=NCHUNK),
                      g[wname][0].rearrange("(c p) m -> p c m", p=128),
                      waits=[(sT, sT.n)], inc=sWh[0])
                for h in range(16):
                    par = h % 2
                    if h + 1 < 16:
                        k.dma("gpsimd",
                              wqk[:, (1 - par) * 2048:(2 - par) * 2048].rearrange(
                                  "p (c m) -> p c m", c=NCHUNK),
                              g[wname][h + 1].rearrange("(c p) m -> p c m", p=128),
                              waits=[(sT, sT.n)], inc=sWh[1 - par])
                    slot = pool[2 + par]
                    for e_ in range(NCHUNK):
                        k.on("tensor", lambda e, e_=e_, par=par, slot=slot: e.matmul(
                             slot[:],
                             wqk[:, par * 2048 + e_ * 128:par * 2048 + e_ * 128 + 128],
                             xnT[:, e_ * TAU:(e_ + 1) * TAU],
                             start=(e_ == 0), stop=(e_ == NCHUNK - 1)),
                             [(sWh[par], sWh[par].n), xmark, (sS, sS.n)]
                             if e_ == 0 else [], inc=sT if e_ == NCHUNK - 1 else None)
                    k.on("scalar", lambda e, h=h, par=par, brow=brow, slot=slot:
                         e.activation(stq[:, par * TAU:(par + 1) * TAU], slot[:],
                                      AF.Identity, bias=brow[:, h:h + 1]),
                         [(sT, sT.n), (sQp[par], sQp[par].n), (sW, wpre)])
                    k.dma("sync", qk_dram[off + h * 128:off + (h + 1) * 128,
                          t0:t0 + TAU], stq[:, par * TAU:(par + 1) * TAU],
                          waits=[(sS, sS.n)], inc=sQp[par])
            for tt in range(4):
                trel = tt * 128
                par = tt % 2
                for e_ in range(NCHUNK):
                    for d4 in range(4):
                        k.on("tensor", lambda e, e_=e_, d4=d4, trel=trel: e.matmul(
                             pool[d4][:],
                             xnT[:, e_ * TAU + trel:e_ * TAU + trel + 128],
                             wbig[:, e_ * 2048 + d4 * 512:e_ * 2048 + d4 * 512 + 512],
                             start=(e_ == 0), stop=False),
                             [(sW, wpre), (sV, sV.n), (sS, sS.n)]
                             if e_ == 0 and d4 == 0 else [])
                for d4 in range(4):
                    k.on("tensor", lambda e, d4=d4: e.matmul(
                         pool[d4][:], onesb[:], bvrow[:, d4 * 512:(d4 + 1) * 512],
                         start=False, stop=True), [],
                         inc=sT if d4 == 3 else None)
                for d4 in range(4):
                    k.on("vector", lambda e, d4=d4, par=par: e.tensor_copy(
                         vstg[:, par * 2048 + d4 * 512:par * 2048 + (d4 + 1) * 512],
                         pool[d4][:]),
                         [(sT, sT.n), (sQp[par], sQp[par].n)] if d4 == 0 else [])
                k.dma("sync", v1_dram[t0 + trel:t0 + trel + 128, :],
                      vstg[:, par * 2048:(par + 1) * 2048],
                      waits=[(sV, sV.n)], inc=sQp[par])
        k.barrier()
        for h in range(16):
            k.dma("gpsimd", wbig[:, h * 2048:(h + 1) * 2048], g["L1_WO"][h], inc=sW)

        for h in range(16):
            k.dma("sync", qh[:], qk_dram[h * 128:(h + 1) * 128, :],
                  waits=[(sT, sT.n), (sS, sS.n)], inc=sHL)
            k.dma("sync", kh[:], qk_dram[D + h * 128:D + (h + 1) * 128, :],
                  waits=[(sT, sT.n)], inc=sHL)
            k.dma("sync", vh[:].rearrange("p (tt hd) -> p tt hd", tt=NTILE),
                  v1_dram[:, h * 128:(h + 1) * 128].rearrange(
                      "(tt p) hd -> p tt hd", p=128),
                  waits=[(sT, sT.n)], inc=sHL)
            hmark = (sHL, sHL.n)
            for j, src in [(0, qh), (1, kh)]:
                k.on("scalar", lambda e, src=src: e.activation(sqb[:], src[:],
                     AF.Square), [hmark, (sV, sV.n)])
                for q4 in range(4):
                    k.on("tensor", lambda e, q4=q4: e.matmul(
                         psB[0:1, :], onescol[:], sqb[:, q4 * 512:(q4 + 1) * 512],
                         start=True, stop=True), [(sS, sS.n), (sV, sV.n)],
                         inc=sT)
                    k.on("vector", lambda e, j=j, q4=q4: e.tensor_reduce(
                         nsc[0:1, j * 4 + q4:j * 4 + q4 + 1], psB[0:1, :], AX.X,
                         ALU.max), [(sT, sT.n)])
            k.on("vector", lambda e: e.tensor_reduce(nsc[0:1, 8:9], nsc[0:1, 0:4],
                 AX.X, ALU.max), [])
            k.on("vector", lambda e: e.tensor_reduce(nsc[0:1, 9:10], nsc[0:1, 4:8],
                 AX.X, ALU.max), [])
            k.on("vector", lambda e: e.tensor_tensor(nsc[0:1, 10:11], nsc[0:1, 8:9],
                 nsc[0:1, 9:10], ALU.mult), [])
            k.on("scalar", lambda e: e.activation(nsc[0:1, 11:12], nsc[0:1, 10:11],
                 AF.Sqrt), [(sV, sV.n)])
            k.on("vector", lambda e: e.tensor_scalar_mul(nsc[0:1, 12:13],
                 nsc[0:1, 11:12], -SC), [(sS, sS.n)])
            k.on("tensor", lambda e: e.matmul(psB[:, 0:1], onesf[:],
                 nsc[0:1, 12:13], start=True, stop=True), [(sV, sV.n)], inc=sT)
            k.on("vector", lambda e: e.tensor_copy(mhc[:], psB[:, 0:1]),
                 [(sT, sT.n)])
            for qg in range(4):
                nkt = 4 * qg + 4
                for kt in range(nkt):
                    par = kt % 2
                    k.on("tensor", lambda e, kt=kt, qg=qg, par=par: e.matmul(
                         pool[par][:], kh[:, kt * 128:(kt + 1) * 128],
                         qh[:, qg * 512:(qg + 1) * 512], start=True, stop=True),
                         [(sV, sV.n), (sS, sS.n)], inc=sT)
                    k.on("scalar", lambda e, par=par: e.activation(
                         ptile[:, par * TAU:(par + 1) * TAU], pool[par][:], AF.Exp,
                         bias=mhc[:], scale=SC), [(sT, sT.n)])
                    o = kt - 4 * qg
                    if o >= 0:
                        k.on("vector", lambda e, par=par, o=o: e.tensor_mul(
                             ptile[:, par * TAU:(par + 1) * TAU],
                             ptile[:, par * TAU:(par + 1) * TAU],
                             masks[:, o * TAU:(o + 1) * TAU]),
                             [(sS, sS.n), (sW, wpre)])
                    pw = [(sS, sS.n), (sV, sV.n)]
                    k.on("tensor", lambda e, kt=kt, par=par, nkt=nkt: e.matmul(
                         psZ[:], vh[:, kt * 128:(kt + 1) * 128],
                         ptile[:, par * TAU:(par + 1) * TAU],
                         start=(kt == 0), stop=(kt == nkt - 1)), pw)
                    k.on("tensor", lambda e, kt=kt, par=par, nkt=nkt: e.matmul(
                         psL[0:1, :], onescol[:],
                         ptile[:, par * TAU:(par + 1) * TAU],
                         start=(kt == 0), stop=(kt == nkt - 1)), [],
                         inc=sT if kt == nkt - 1 else None)
                k.on("vector", lambda e: e.reciprocal(ILsb[0:1, :], psL[0:1, :]),
                     [(sT, sT.n)])
                k.on("tensor", lambda e: e.matmul(psB[:], onesf[:], ILsb[0:1, :],
                     start=True, stop=True), [(sV, sV.n)], inc=sT)
                k.on("vector", lambda e: e.tensor_copy(ILsb[:], psB[:]),
                     [(sT, sT.n)])
                k.on("vector", lambda e, h=h, qg=qg: e.tensor_mul(
                     zT[:, h * 2048 + qg * 512:h * 2048 + (qg + 1) * 512],
                     psZ[:], ILsb[:]), [])
        k.barrier()
        for tt in range(NTILE):
            trel = tt * 128
            par = tt % 2
            for h in range(16):
                for d4 in range(4):
                    k.on("tensor", lambda e, h=h, d4=d4, trel=trel: e.matmul(
                         pool[d4][:], zT[:, h * 2048 + trel:h * 2048 + trel + 128],
                         wbig[:, h * 2048 + d4 * 512:h * 2048 + d4 * 512 + 512],
                         start=(h == 0), stop=False),
                         [(sW, sW.n), (sV, sV.n)] if h == 0 and d4 == 0 else [])
            for d4 in range(4):
                k.on("tensor", lambda e, d4=d4: e.matmul(
                     pool[d4][:], onesb[:], borow[:, d4 * 512:(d4 + 1) * 512],
                     start=False, stop=True), [], inc=sT if d4 == 3 else None)
            for d4 in range(4):
                k.on("vector", lambda e, d4=d4, par=par: e.tensor_copy(
                     aostg[:, par * 2048 + d4 * 512:par * 2048 + (d4 + 1) * 512],
                     pool[d4][:]),
                     [(sT, sT.n), (sAO[par], sAO[par].n)] if d4 == 0 else [])
            k.dma("sync", attn1_tok[trel:trel + 128, :],
                  aostg[:, par * 2048:(par + 1) * 2048],
                  waits=[(sV, sV.n)], inc=sAO[par])
        k.barrier()


def build_l1_attn(w):
    nc = bass.Bass("TRN2", target_bir_lowering=False)
    keys = ["L1_WQ", "L1_WK", "L1_WV_flat", "L1_WO", "L1_bQc", "L1_bKc",
            "L1_bV_row", "L1_bO_row", "ones_bf", "ones_f32", "ones_colbf",
            "maskD0", "maskD1", "maskD2", "maskD3"]
    g = declare_weight_inputs(nc, {kk: w[kk] for kk in keys})
    xn1 = dram(nc, "xn1", [T, D], BF16, "ExternalInput")
    a1 = dram(nc, "attn1_tok", [T, D], BF16, "ExternalOutput")
    qk_dram = dram(nc, "qk_dram", [2 * D, T], BF16)
    v1_dram = dram(nc, "v1_dram", [T, D], BF16)
    k = K(nc)
    emit_l1_attn(k, g, xn1, qk_dram, v1_dram, a1)
    k.finalize()
    return nc


def emit_l1_moe(k, g, wbf, xn1_tok, xn1_f32, xt1_tok, ffn1_tok):
    TAU = 512
    NE = 6
    BIG = 1e9
    sX = k.sem("mo_x")
    sW = k.sem("mo_w")
    sWs = [k.sem("mo_ws0"), k.sem("mo_ws1")]
    sWd = k.sem("mo_wd")
    sR = k.sem("mo_r")
    sOp = [k.sem("mo_op0"), k.sem("mo_op1")]
    sT, sV, sS = k.sT, k.sV, k.sS
    with ExitStack() as st:
        sb = lambda n, s, d: st.enter_context(k.nc.sbuf_tensor(n, s, d))
        ps = lambda n, s: st.enter_context(k.nc.psum_tensor(n, s, F32))
        xtT = sb("mo_xtT", [128, NCHUNK * TAU], BF16)
        h1 = sb("mo_h1", [128, 8 * TAU], BF16)
        h2 = sb("mo_h2", [128, 8 * TAU], BF16)
        sh = sb("mo_sh", [128, NCHUNK * TAU], BF16)
        acc = sb("mo_acc", [128, 4 * 2048], F32)
        wdn = sb("mo_wdn", [128, 8 * 2048], BF16)
        wslab = sb("mo_wslab", [128, 2 * 2048], BF16)
        wgsb = sb("mo_wgsb", [128, NCHUNK * 8], BF16)
        swsb = sb("mo_swsb", [128, TAU], BF16)
        gate = sb("mo_gate", [128, 8], F32)
        tmp8 = sb("mo_tmp8", [128, 8], F32)
        m8 = sb("mo_m8", [128, 8], F32)
        p8 = sb("mo_p8", [128, 8], F32)
        gcols = sb("mo_gcols", [128, 8], F32)
        wcols = sb("mo_wcols", [128, 4 * 8], F32)
        bup = sb("mo_bup", [128, 8], F32)
        bsw = sb("mo_bsw", [128, 8], F32)
        bsu = sb("mo_bsu", [128, 16], F32)
        bsa = sb("mo_bsa", [128, 8], F32)
        bgrow = sb("mo_bgrow", [1, 8], BF16)
        bdnrow = sb("mo_bdnrow", [1, 6 * 2048], BF16)
        bsdrow = sb("mo_bsdrow", [1, 2048], BF16)
        onesb = sb("mo_onesb", [1, 128], BF16)
        res = sb("mo_res", [128, 2048], F32)
        stg = sb("mo_stg", [128, 2 * 2048], BF16)
        pool = [ps(f"mo_p{j}", [128, TAU]) for j in range(4)]
        psG = ps("mo_psG", [128, TAU])

        k.dma("gpsimd", onesb[:], g["ones_bf"], inc=sW)
        k.dma("gpsimd", bsu[:], g["L1_bsuc"], inc=sW)
        k.dma("gpsimd", bsa[:], g["L1_bsac"], inc=sW)
        k.dma("gpsimd", bsdrow[:], g["L1_bsd_row"], inc=sW)
        for n in range(NE):
            k.dma("gpsimd", bdnrow[:, n * 2048:(n + 1) * 2048],
                  g["L1_bdn_row"][n], inc=sW)

        def colslab(dst_slot, srcap, nch, waits):
            d3 = wslab[:, dst_slot * 2048:dst_slot * 2048 + nch * 128].rearrange(
                "p (c i) -> p c i", c=nch)
            s3 = srcap.rearrange("(c p) i -> p c i", p=128)
            k.dma("gpsimd", d3, s3, waits=waits, inc=sWs[dst_slot])

        def proj_slab(nch_k, slab_slot, dst_ps, rhs_fn, first_waits):
            for e_ in range(nch_k):
                k.on("tensor", lambda e, e_=e_, ss=slab_slot, dst=dst_ps,
                     rhs=rhs_fn(e_): e.matmul(
                     dst[:], wslab[:, ss * 2048 + e_ * 128:ss * 2048 + e_ * 128 + 128],
                     rhs, start=(e_ == 0), stop=(e_ == nch_k - 1)),
                     first_waits if e_ == 0 else [],
                     inc=sT if e_ == nch_k - 1 else None)

        for j in range(2):
            xin = xn1_tok if j == 0 else xt1_tok
            xout = xt1_tok if j == 0 else ffn1_tok
            k.dma("gpsimd", wgsb[:].rearrange("p (c m) -> p c m", c=NCHUNK),
                  g["L1_Wg"][j].rearrange("(c p) m -> p c m", p=128),
                  waits=[(sT, sT.n)], inc=sW)
            k.dma("gpsimd", bgrow[:], g["L1_bg_row"][j], waits=[(sT, sT.n)], inc=sW)
            wpre = sW.n
            for tg in range(T // TAU):
                t0 = tg * TAU
                for c in range(NCHUNK):
                    k.dma("sync", xtT[:, c * TAU:(c + 1) * TAU],
                          xin[t0:t0 + TAU, c * 128:(c + 1) * 128],
                          waits=[(sV, sV.n), (sT, sT.n), (sS, sS.n)], inc=sX,
                          transpose=True)
                xmark = (sX, sX.n)
                for tt in range(4):
                    trel = tt * 128
                    for e_ in range(NCHUNK):
                        k.on("tensor", lambda e, e_=e_, trel=trel: e.matmul(
                             psG[:, 0:8], xtT[:, e_ * TAU + trel:e_ * TAU + trel + 128],
                             wgsb[:, e_ * 8:(e_ + 1) * 8],
                             start=(e_ == 0), stop=False),
                             [(sW, wpre), xmark, (sV, sV.n)] if e_ == 0 else [])
                    k.on("tensor", lambda e: e.matmul(psG[:, 0:8], onesb[:], bgrow[:],
                         start=False, stop=True), [], inc=sT)
                    k.on("vector", lambda e: e.tensor_copy(gate[:], psG[:, 0:8]),
                         [(sT, sT.n)])
                    k.on("vector", lambda e: e.tensor_reduce(gcols[:, 0:1], gate[:],
                         AX.X, ALU.min), [])
                    k.on("vector", lambda e: e.tensor_single_scalar(tmp8[:], gate[:],
                         gcols[:, 0:1], ALU.is_equal), [])
                    k.on("vector", lambda e: e.scalar_tensor_tensor(tmp8[:], tmp8[:],
                         BIG, gate[:], ALU.mult, ALU.add), [])
                    k.on("vector", lambda e: e.tensor_reduce(gcols[:, 1:2], tmp8[:],
                         AX.X, ALU.min), [])
                    k.on("vector", lambda e: e.tensor_single_scalar(m8[:], gate[:],
                         gcols[:, 1:2], ALU.is_gt), [])
                    k.on("vector", lambda e: e.tensor_reduce(gcols[:, 2:3], gate[:],
                         AX.X, ALU.max), [])
                    k.on("vector", lambda e: e.tensor_scalar_mul(gcols[:, 3:4],
                         gcols[:, 2:3], -1.0), [])
                    k.on("scalar", lambda e: e.activation(p8[:], gate[:], AF.Exp,
                         bias=gcols[:, 3:4]), [(sV, sV.n)])
                    k.on("vector", lambda e: e.tensor_mul(p8[:], p8[:], m8[:]),
                         [(sS, sS.n)])
                    k.on("vector", lambda e: e.tensor_reduce(gcols[:, 4:5], p8[:],
                         AX.X, ALU.add), [])
                    k.on("vector", lambda e: e.reciprocal(gcols[:, 5:6],
                         gcols[:, 4:5]), [])
                    k.on("vector", lambda e, tt=tt: e.tensor_single_scalar(
                         wcols[:, tt * 8:(tt + 1) * 8], p8[:], gcols[:, 5:6],
                         ALU.mult), [])
                if getattr(k, "_mo_dbg", None) and j == 0 and tg == 0:
                    k.dma("sync", k._mo_dbg["wcols"], wcols[:], waits=[(sV, sV.n)],
                          inc=sOp[0])
                for n in range(NE):
                    k.dma("gpsimd", bup[:], g["L1_bupc"][n], waits=[(sS, sS.n)],
                          inc=sW)
                    k.dma("gpsimd", bsw[:], g["L1_bswc"][n], waits=[(sS, sS.n)],
                          inc=sW)
                    for e_ in range(8):
                        k.dma("gpsimd", wdn[:, e_ * 2048:(e_ + 1) * 2048],
                              wbf["dn"][n][e_ * 128:(e_ + 1) * 128, :],
                              waits=[(sT, sT.n)] if e_ == 0 else [], inc=sWd)
                    wdmark = (sWd, sWd.n)
                    colslab(0, wbf["up"][n][:, 0:128], NCHUNK, [(sT, sT.n)])
                    for i in range(8):
                        if i + 1 < 8:
                            colslab(1 - i % 2, wbf["up"][n][:, (i + 1) * 128:(i + 2) * 128],
                                    NCHUNK, [(sT, sT.n)])
                        slot = pool[2 + i % 2]
                        proj_slab(NCHUNK, i % 2, slot,
                                  lambda e_: xtT[:, e_ * TAU:(e_ + 1) * TAU],
                                  [(sWs[i % 2], sWs[i % 2].n), (sS, sS.n), (sV, sV.n)])
                        k.on("scalar", lambda e, i=i, n=n, slot=slot: e.activation(
                             h1[:, i * TAU:(i + 1) * TAU], slot[:], AF.Identity,
                             bias=bup[:, i:i + 1]), [(sT, sT.n), (sW, sW.n)])
                    colslab(0, wbf["sw"][n][:, 0:128], 8, [(sT, sT.n)])
                    for f in range(8):
                        if f + 1 < 8:
                            colslab(1 - f % 2, wbf["sw"][n][:, (f + 1) * 128:(f + 2) * 128],
                                    8, [(sT, sT.n)])
                        slot = pool[2 + f % 2]
                        proj_slab(8, f % 2, slot,
                                  lambda e_: h1[:, e_ * TAU:(e_ + 1) * TAU],
                                  [(sWs[f % 2], sWs[f % 2].n), (sS, sS.n), (sV, sV.n)])
                        k.on("scalar", lambda e, f=f, n=n, slot=slot: e.activation(
                             swsb[:], slot[:], AF.Silu, bias=bsw[:, f:f + 1]),
                             [(sT, sT.n), (sV, sV.n), (sW, sW.n)])
                        k.on("vector", lambda e, f=f: e.tensor_mul(
                             h2[:, f * TAU:(f + 1) * TAU], swsb[:],
                             h1[:, f * TAU:(f + 1) * TAU]), [(sS, sS.n)])
                    for tt in range(4):
                        trel = tt * 128
                        for e_ in range(8):
                            for d4 in range(4):
                                k.on("tensor", lambda e, e_=e_, d4=d4, trel=trel:
                                     e.matmul(pool[d4][:],
                                     h2[:, e_ * TAU + trel:e_ * TAU + trel + 128],
                                     wdn[:, e_ * 2048 + d4 * 512:e_ * 2048 + d4 * 512 + 512],
                                     start=(e_ == 0), stop=False),
                                     [wdmark, (sV, sV.n), (sS, sS.n)]
                                     if e_ == 0 and d4 == 0 else [])
                        for d4 in range(4):
                            k.on("tensor", lambda e, d4=d4, n=n: e.matmul(
                                 pool[d4][:], onesb[:],
                                 bdnrow[:, n * 2048 + d4 * 512:n * 2048 + (d4 + 1) * 512],
                                 start=False, stop=True), [],
                                 inc=sT if d4 == 3 else None)
                        for d4 in range(4):
                            a = acc[:, tt * 2048 + d4 * 512:tt * 2048 + (d4 + 1) * 512]
                            wc = wcols[:, tt * 8 + n:tt * 8 + n + 1]
                            if n == 0:
                                k.on("vector", lambda e, a=a, wc=wc, d4=d4:
                                     e.tensor_single_scalar(a, pool[d4][:], wc,
                                     ALU.mult), [(sT, sT.n)] if d4 == 0 else [])
                            else:
                                k.on("vector", lambda e, a=a, wc=wc, d4=d4:
                                     e.scalar_tensor_tensor(a, pool[d4][:], wc, a,
                                     ALU.mult, ALU.add), [(sT, sT.n)] if d4 == 0 else [])
                if getattr(k, "_mo_dbg", None) and j == 0 and tg == 0:
                    k.dma("sync", k._mo_dbg["h1"], h1[:], waits=[(sS, sS.n), (sV, sV.n)], inc=sOp[0])
                    k.dma("sync", k._mo_dbg["h2"], h2[:], waits=[(sV, sV.n)], inc=sOp[0])
                    k.dma("sync", k._mo_dbg["acc1"], acc[:], waits=[(sV, sV.n)], inc=sOp[0])
                    k.barrier()
                colslab(0, wbf["su"][:, 0:128], NCHUNK, [(sT, sT.n)])
                for c in range(NCHUNK):
                    if c + 1 < NCHUNK:
                        colslab(1 - c % 2, wbf["su"][:, (c + 1) * 128:(c + 2) * 128],
                                NCHUNK, [(sT, sT.n)])
                    slot = pool[2 + c % 2]
                    proj_slab(NCHUNK, c % 2, slot,
                              lambda e_: xtT[:, e_ * TAU:(e_ + 1) * TAU],
                              [(sWs[c % 2], sWs[c % 2].n), (sS, sS.n), (sV, sV.n)])
                    k.on("scalar", lambda e, c=c, slot=slot: e.activation(
                         sh[:, c * TAU:(c + 1) * TAU], slot[:], AF.Identity,
                         bias=bsu[:, c:c + 1]), [(sT, sT.n), (sW, sW.n)])
                for s2 in range(2):
                    colslab(0, wbf["sa"][:, 0:128], 8, [(sT, sT.n)])
                    for f in range(8):
                        if f + 1 < 8:
                            colslab(1 - f % 2, wbf["sa"][:, (f + 1) * 128:(f + 2) * 128],
                                    8, [(sT, sT.n)])
                        slot = pool[2 + f % 2]
                        proj_slab(8, f % 2, slot,
                                  lambda e_, s2=s2: sh[:, (s2 * 8 + e_) * TAU:
                                                       (s2 * 8 + e_ + 1) * TAU],
                                  [(sWs[f % 2], sWs[f % 2].n), (sS, sS.n), (sV, sV.n)])
                        k.on("scalar", lambda e, f=f, slot=slot: e.activation(
                             swsb[:], slot[:], AF.Silu, bias=bsa[:, f:f + 1]),
                             [(sT, sT.n), (sV, sV.n), (sW, sW.n)])
                        fi = s2 * 8 + f
                        dstb = h1 if fi < 8 else h2
                        di = fi % 8
                        k.on("vector", lambda e, fi=fi, dstb=dstb, di=di: e.tensor_mul(
                             dstb[:, di * TAU:(di + 1) * TAU], swsb[:],
                             sh[:, fi * TAU:(fi + 1) * TAU]), [(sS, sS.n)])
                for hh in range(2):
                    for e_ in range(8):
                        k.dma("gpsimd", wdn[:, e_ * 2048:(e_ + 1) * 2048],
                              wbf["sd"][(hh * 8 + e_) * 128:(hh * 8 + e_ + 1) * 128, :],
                              waits=[(sT, sT.n)] if e_ == 0 else [], inc=sWd)
                    wdmark = (sWd, sWd.n)
                    for tt in range(4):
                        trel = tt * 128
                        for e_ in range(8):
                            su = hh * 8 + e_
                            srcb = h1 if su < 8 else h2
                            si = su % 8
                            last = (hh == 0 and e_ == 7)
                            for d4 in range(4):
                                k.on("tensor", lambda e, si=si, srcb=srcb, e_=e_,
                                     d4=d4, trel=trel, last=last: e.matmul(pool[d4][:],
                                     srcb[:, si * TAU + trel:si * TAU + trel + 128],
                                     wdn[:, e_ * 2048 + d4 * 512:e_ * 2048 + d4 * 512 + 512],
                                     start=(e_ == 0), stop=last),
                                     [wdmark, (sV, sV.n), (sS, sS.n)]
                                     if e_ == 0 and d4 == 0 else [],
                                     inc=sT if (last and d4 == 3) else None)
                        if hh == 1:
                            for d4 in range(4):
                                k.on("tensor", lambda e, d4=d4: e.matmul(
                                     pool[d4][:], onesb[:],
                                     bsdrow[:, d4 * 512:(d4 + 1) * 512],
                                     start=False, stop=True), [],
                                     inc=sT if d4 == 3 else None)
                        for d4 in range(4):
                            a = acc[:, tt * 2048 + d4 * 512:tt * 2048 + (d4 + 1) * 512]
                            k.on("vector", lambda e, a=a, d4=d4: e.tensor_tensor(
                                 a, a, pool[d4][:], ALU.add),
                                 [(sT, sT.n)] if d4 == 0 else [])
                if getattr(k, "_mo_dbg", None) and j == 0 and tg == 0:
                    k.dma("sync", k._mo_dbg["acc2"], acc[:], waits=[(sV, sV.n)], inc=sOp[0])
                    k.dma("sync", k._mo_dbg["sh"], sh[:], waits=[(sV, sV.n)], inc=sOp[0])
                    k.barrier()
                for tt in range(4):
                    trel = tt * 128
                    par = tt % 2
                    k.dma("sync", res[:], xn1_f32[t0 + trel:t0 + trel + 128, :],
                          waits=[(sV, sV.n)], inc=sR)
                    k.on("vector", lambda e, tt=tt, par=par: e.tensor_add(
                         stg[:, par * 2048:(par + 1) * 2048],
                         acc[:, tt * 2048:(tt + 1) * 2048], res[:]),
                         [(sR, sR.n), (sOp[par], sOp[par].n)])
                    k.dma("sync", xout[t0 + trel:t0 + trel + 128, :],
                          stg[:, par * 2048:(par + 1) * 2048],
                          waits=[(sV, sV.n)], inc=sOp[par])
            k.barrier()
        k.barrier()


def precast_moe_weights(k, g, nc):
    sPre = k.sem("mo_pre")
    wbf = {"up": [], "sw": [], "dn": []}
    for n in range(6):
        for nm, shp in [("up", (2048, 1024)), ("sw", (1024, 1024)),
                        ("dn", (1024, 2048))]:
            t = dram(nc, f"mo_{nm}{n}_bf", list(shp), BF16)
            src = {"up": "L1_Wup", "sw": "L1_Wsw", "dn": "L1_Wdn"}[nm]
            for r0 in range(0, shp[0], 512):
                k.dma("gpsimd", t[r0:r0 + 512, :], g[src][n][r0:r0 + 512, :],
                      inc=sPre)
            wbf[nm].append(t)
    for nm, src, shp in [("su", "L1_Wsu", (2048, 2048)), ("sa", "L1_Wsa", (1024, 1024)),
                         ("sd", "L1_Wsd", (2048, 2048))]:
        t = dram(nc, f"mo_{nm}_bf", list(shp), BF16)
        for r0 in range(0, shp[0], 512):
            k.dma("gpsimd", t[r0:r0 + 512, :], g[src][r0:r0 + 512, :], inc=sPre)
        wbf[nm] = t
    return wbf


MOE_KEYS = ["L1_Wg", "L1_bg_row", "L1_Wup", "L1_bupc", "L1_Wsw", "L1_bswc",
            "L1_Wdn", "L1_bdn_row", "L1_Wsu", "L1_bsuc", "L1_Wsa", "L1_bsac",
            "L1_Wsd", "L1_bsd_row", "ones_bf"]


def build_l1_moe(w, debug=False):
    nc = bass.Bass("TRN2", target_bir_lowering=False)
    g = declare_weight_inputs(nc, {kk: w[kk] for kk in MOE_KEYS})
    xn1 = dram(nc, "xn1", [T, D], BF16, "ExternalInput")
    xn1f = dram(nc, "xn1f", [T, D], F32, "ExternalInput")
    f1 = dram(nc, "ffn1_tok", [T, D], BF16, "ExternalOutput")
    xt1 = dram(nc, "xt1_tok", [T, D], BF16)
    k = K(nc)
    if debug:
        k._mo_dbg = {
            "wcols": dram(nc, "dbg_wcols", [128, 32], F32, "ExternalOutput"),
            "h1": dram(nc, "dbg_h1", [128, 8 * 512], BF16, "ExternalOutput"),
            "h2": dram(nc, "dbg_h2", [128, 8 * 512], BF16, "ExternalOutput"),
            "acc1": dram(nc, "dbg_acc1", [128, 4 * 2048], F32, "ExternalOutput"),
            "acc2": dram(nc, "dbg_acc2", [128, 4 * 2048], F32, "ExternalOutput"),
            "sh": dram(nc, "dbg_sh", [128, 16 * 512], BF16, "ExternalOutput"),
        }
    wbf = precast_moe_weights(k, g, nc)
    k.barrier()
    emit_l1_moe(k, g, wbf, xn1, xn1f, xt1, f1)
    k.finalize()
    return nc


# version 03
# speedup vs baseline: 1.3738x; 1.3738x over previous
import math
from contextlib import ExitStack

import numpy as np
import ml_dtypes
import concourse.bass as bass
import concourse.mybir as mybir
from concourse.bass_utils import run_bass_kernel_spmd

F32 = mybir.dt.float32
BF16 = mybir.dt.bfloat16
I8 = mybir.dt.int8
AF = mybir.ActivationFunctionType
ALU = mybir.AluOpType
AX = mybir.AxisListType

T = 2048
D = 2048
NCHUNK = D // 128
NTILE = T // 128
EPS = 1e-5
ENGINES = ("sync", "gpsimd", "vector", "scalar", "tensor")


class Sem:
    def __init__(self, h, name):
        self.h = h
        self.name = name
        self.n = 0


class K:

    def __init__(self, nc):
        self.nc = nc
        self.q = {e: [] for e in ENGINES}
        self.ctx = ExitStack()
        self._sems = []
        self.sV = self.sem("ordV")
        self.sS = self.sem("ordS")
        self.sG = self.sem("ordG")
        self.sT = self.sem("ordT")
        self._ord = {"vector": self.sV, "scalar": self.sS, "gpsimd": self.sG}

    def sem(self, name):
        name = self._uname(name)
        h = self.ctx.enter_context(self.nc.semaphore(name))
        s = Sem(h, name)
        self._sems.append(s)
        return s

    def _uname(self, name):
        self._nid = getattr(self, "_nid", 0) + 1
        return f"{name}_u{self._nid}"

    def sb(self, name, shape, dtype):
        return self.ctx.enter_context(self.nc.sbuf_tensor(self._uname(name), shape,
                                                          dtype))

    def ps(self, name, shape, dtype=F32):
        return self.ctx.enter_context(self.nc.psum_tensor(self._uname(name), shape,
                                                          dtype))

    def _emit(self, eng, fn, waits, incs):
        wl = []
        for w in waits:
            if isinstance(w, Sem):
                wl.append((w.h, w.n))
            else:
                s, v = w
                if v > 0:
                    wl.append((s.h, v))
        best = {}
        for h, v in wl:
            key = id(h)
            if key not in best or best[key][1] < v:
                best[key] = (h, v)
        il = []
        for s, kk in incs:
            s.n += kk
            il.append((s.h, kk))
        self.q[eng].append((list(best.values()), il, fn))

    def on(self, eng, fn, waits=(), inc=None, inc_k=1):
        waits = list(waits)
        incs = []
        if eng in self._ord:
            o = self._ord[eng]
            waits.append((o, o.n))
            incs.append((o, 1))
            assert inc is None or inc is o, "V/S/G ops always inc their order sem"
        elif inc is not None:
            incs.append((inc, inc_k))
        self._emit(eng, fn, waits, incs)

    def dma(self, eng, out, in_, waits=(), inc=None, **kw):
        incs = [(inc, 16)] if inc is not None else []
        self._emit(eng, lambda e: e.dma_start(out=out, in_=in_, **kw),
                   list(waits), incs)

    def mark(self, label):
        if not hasattr(self, "marks"):
            self.marks = []
        self.marks.append((label, {e: len(self.q[e]) for e in ENGINES}))

    def wait_all(self, eng):
        wl = [(s.h, s.n) for s in self._sems if s.n > 0]
        self.q[eng].append((wl, [], None))

    def barrier(self):
        for e in ENGINES:
            self.wait_all(e)

    def finalize(self):
        nc = self.nc
        with nc.Block() as blk:
            def runner(name):
                def go(e):
                    for wl, il, fn in self.q[name]:
                        for h, v in wl:
                            if v > 0:
                                e.wait_ge(h, v)
                        if fn is None:
                            continue
                        inst = fn(e)
                        for h, kk in il:
                            inst = inst.then_inc(h, kk)
                return go
            blk.sync(runner("sync"))
            blk.gpsimd(runner("gpsimd"))
            blk.vector(runner("vector"))
            blk.scalar(runner("scalar"))
            blk.tensor(runner("tensor"))
        self.ctx.close()


def _colmajor(v):
    v = np.asarray(v).reshape(-1)
    return np.ascontiguousarray(v.reshape(-1, 128).T)


def prep_weights(i):
    w = {}
    w["ident_bf"] = np.eye(128, dtype=ml_dtypes.bfloat16)
    w["ident_f32"] = np.eye(128, dtype=np.float32)
    w["ones_bf"] = np.ones((1, 128), dtype=ml_dtypes.bfloat16)
    w["ones_f32"] = np.ones((1, 128), dtype=np.float32)
    w["ones_colbf"] = np.ones((128, 1), dtype=ml_dtypes.bfloat16)
    w["maskU"] = np.triu(np.ones((128, 128))).astype(ml_dtypes.bfloat16)
    for o in range(4):
        kk = np.arange(128)[:, None] + o * 128
        qq = np.arange(512)[None, :]
        w[f"maskD{o}"] = (kk <= qq).astype(ml_dtypes.bfloat16)

    for nm in ["Wq", "Wk", "Wv", "Wg", "Wo"]:
        w[f"L0_{nm}T"] = np.ascontiguousarray(np.asarray(i[f"L0_attn_{nm}"]).T)
    w["L0_maa_w1"] = np.asarray(i["L0_attn_maa_w1"])
    w["L0_maa_w2"] = np.asarray(i["L0_attn_maa_w2"])
    w2p = np.zeros((5, 64, 2048), np.float32)
    mw2 = np.asarray(i["L0_attn_maa_w2"])
    for f in range(4):
        off = 32 * (f % 2)
        w2p[f, off:off + 32, :] = mw2[f]
    w2p[4, 0:32, :] = mw2[4]
    w["L0_w2p"] = w2p
    w["L0_decay_w1"] = np.asarray(i["L0_attn_decay_w1"])
    w["L0_decay_w2"] = np.asarray(i["L0_attn_decay_w2"])
    for nm in ["maa_x", "maa_r", "maa_k", "maa_v", "maa_w", "maa_g", "time_decay"]:
        w[f"L0_{nm}c"] = _colmajor(i[f"L0_attn_{nm}"])
    w["L0_bgc"] = _colmajor(i["L0_attn_bg"])
    w["L0_w_gate"] = np.asarray(i["L0_ffn_w_gate"])
    w["L0_w_up"] = np.asarray(i["L0_ffn_w_up"])
    w["L0_w_down"] = np.asarray(i["L0_ffn_w_down"])
    w["L0_sgc"] = _colmajor(i["L0_ffn_s_gate"])
    w["L0_suc"] = _colmajor(i["L0_ffn_s_up"])
    w["L0_sdc"] = _colmajor(i["L0_ffn_s_down"])
    w["L1_WQ"] = np.asarray(i["L1_attn_W_Q"])
    w["L1_WK"] = np.asarray(i["L1_attn_W_K"])
    w["L1_WV_flat"] = np.ascontiguousarray(
        np.asarray(i["L1_attn_W_V"]).transpose(1, 0, 2).reshape(2048, 2048))
    w["L1_WO"] = np.asarray(i["L1_attn_W_O"])
    w["L1_bQc"] = np.ascontiguousarray(np.asarray(i["L1_attn_b_Q"]).T)
    w["L1_bKc"] = np.ascontiguousarray(np.asarray(i["L1_attn_b_K"]).T)
    w["L1_bV_row"] = np.asarray(i["L1_attn_b_V"]).reshape(1, 2048)
    w["L1_bO_row"] = np.asarray(i["L1_attn_b_O"]).reshape(1, 2048)
    w["L1_Wg"] = np.asarray(i["L1_ffn_Wg"])
    w["L1_bg_row"] = np.ascontiguousarray(np.asarray(i["L1_ffn_bg"]).reshape(2, 1, 8))
    w["L1_Wup"] = np.asarray(i["L1_ffn_Wup"])
    w["L1_bupc"] = np.ascontiguousarray(
        np.asarray(i["L1_ffn_bup"]).reshape(6, 8, 128).transpose(0, 2, 1))
    w["L1_Wsw"] = np.asarray(i["L1_ffn_Wsw"])
    w["L1_bswc"] = np.ascontiguousarray(
        np.asarray(i["L1_ffn_bsw"]).reshape(6, 8, 128).transpose(0, 2, 1))
    w["L1_Wdn"] = np.asarray(i["L1_ffn_Wdn"])
    w["L1_bdn_row"] = np.ascontiguousarray(np.asarray(i["L1_ffn_bdn"]).reshape(6, 1, 2048))
    w["L1_Wsu"] = np.asarray(i["L1_ffn_Wsu"])
    w["L1_bsuc"] = _colmajor(i["L1_ffn_bsu"])
    w["L1_Wsa"] = np.asarray(i["L1_ffn_Wsa"])
    w["L1_bsac"] = _colmajor(i["L1_ffn_bsa"])
    w["L1_Wsd"] = np.asarray(i["L1_ffn_Wsd"])
    w["L1_bsd_row"] = np.asarray(i["L1_ffn_bsd"]).reshape(1, 2048)
    return {k: np.ascontiguousarray(v) for k, v in w.items()}


def declare_weight_inputs(nc, w):
    dt_map = {np.dtype(np.float32): F32, np.dtype(ml_dtypes.bfloat16): BF16,
              np.dtype(np.int8): I8}
    aps = {}
    for kk, v in w.items():
        aps[kk] = nc.dram_tensor(kk, list(v.shape), dt_map[v.dtype],
                                 kind="ExternalInput").ap()
    return aps


def emit_combine_ln(k, g, x_in_f32, adds, x_out_f32, xn_tok_bf, xn_f32, out_is_final=False):
    sX = k.sem("cl_x")
    sV = k.sV
    sS = k.sS
    sO = k.sem("cl_o")
    with ExitStack() as st:
        sb = lambda n, s, d: st.enter_context(k.nc.sbuf_tensor(k._uname(n), s, d))
        xb = sb("cl_xb", [128, 2 * D], F32)
        ab = sb("cl_ab", [128, 2 * D], BF16)
        fb = sb("cl_fb", [128, 2 * D], BF16)
        sq = sb("cl_sq", [128, D], F32)
        x1 = sb("cl_x1", [128, 2 * D], F32)
        xn = sb("cl_xn", [128, D], BF16)
        xnf = sb("cl_xnf", [128, D], F32)
        cols = sb("cl_cols", [128, 8], F32)

        for t in range(NTILE):
            db = t % 2
            xs = xb[:, db * D:(db + 1) * D]
            x1s = x1[:, db * D:(db + 1) * D]
            war = [Sem.__new__(Sem)]
            war = ([(sV, sV.n), (sO, sO.n), (sS, sS.n)] if t >= 2 else [])
            k.dma("sync", xs, x_in_f32[t * 128:(t + 1) * 128, :], waits=war, inc=sX)
            srcs = []
            for idx, (ap, kind) in enumerate(adds):
                buf = (ab if idx == 0 else fb)[:, db * D:(db + 1) * D]
                if kind == "tok_bf":
                    k.dma("sync", buf, ap[t * 128:(t + 1) * 128, :], waits=war, inc=sX)
                else:
                    k.dma("sync", buf, ap[:, t * 128:(t + 1) * 128], waits=war, inc=sX,
                          transpose=True)
                srcs.append(buf)
            dmaw = [(sX, sX.n), (sO, sO.n), (sS, sS.n)]
            if not srcs:
                k.on("vector", lambda e, a=xs, o=x1s: e.tensor_copy(o, a), dmaw, sV)
            else:
                k.on("vector", lambda e, a=xs, b=srcs[0], o=x1s:
                     e.tensor_tensor(o, a, b, ALU.add), dmaw, sV)
                for b2 in srcs[1:]:
                    k.on("vector", lambda e, a=x1s, b=b2, o=x1s:
                         e.tensor_tensor(o, a, b, ALU.add), [], sV)
            if x_out_f32 is not None:
                k.dma("sync", x_out_f32[t * 128:(t + 1) * 128, :], x1s,
                      waits=[(sV, sV.n)], inc=sO)
            if xn_tok_bf is None and xn_f32 is None:
                continue
            k.on("scalar", lambda e, a=x1s: e.activation(sq[:], a, AF.Square,
                 accum_out=cols[:, 1:2]), [(sV, sV.n)], sS)
            k.on("vector", lambda e, a=x1s: e.tensor_reduce(cols[:, 0:1], a, AX.X,
                 ALU.add), [], sV)
            k.on("vector", lambda e: e.tensor_scalar_mul(cols[:, 2:3], cols[:, 0:1],
                 1.0 / D), [], sV)
            k.on("vector", lambda e: e.tensor_tensor(cols[:, 3:4], cols[:, 2:3],
                 cols[:, 2:3], ALU.mult), [], sV)
            k.on("vector", lambda e: e.tensor_scalar_mul(cols[:, 4:5], cols[:, 1:2],
                 1.0 / D), [(sS, sS.n)], sV)
            k.on("vector", lambda e: e.tensor_tensor(cols[:, 4:5], cols[:, 4:5],
                 cols[:, 3:4], ALU.subtract), [], sV)
            k.on("vector", lambda e: e.tensor_scalar_add(cols[:, 4:5], cols[:, 4:5],
                 float(EPS)), [], sV)
            k.on("scalar", lambda e: e.activation(cols[:, 5:6], cols[:, 4:5], AF.Sqrt),
                 [(sV, sV.n)], sS)
            k.on("vector", lambda e: e.reciprocal(cols[:, 6:7], cols[:, 5:6]),
                 [(sS, sS.n)], sV)
            if xn_tok_bf is not None:
                k.on("vector", lambda e, a=x1s: e.tensor_scalar(xn[:], a, cols[:, 2:3],
                     cols[:, 6:7], ALU.subtract, ALU.mult), [], sV)
                k.dma("sync", xn_tok_bf[t * 128:(t + 1) * 128, :], xn[:],
                      waits=[(sV, sV.n)], inc=sO)
            if xn_f32 is not None:
                k.on("vector", lambda e, a=x1s: e.tensor_scalar(xnf[:], a, cols[:, 2:3],
                     cols[:, 6:7], ALU.subtract, ALU.mult), [], sV)
                k.dma("sync", xn_f32[t * 128:(t + 1) * 128, :], xnf[:],
                      waits=[(sV, sV.n)], inc=sO)
        k.barrier()


def emit_load_fm(k, xnT, xn_tok_bf, sem, waits=()):
    for c in range(NCHUNK):
        k.dma("sync", xnT[:, c * T:(c + 1) * T],
              xn_tok_bf[:, c * 128:(c + 1) * 128], waits=waits, inc=sem,
              transpose=True)


def emit_l0_ffn(k, g, xnT, f0_fm, wbf=None):
    I = 8192
    NI = I // 128
    TG = 512
    sW = k.sem("ff_w")
    sWp = [k.sem(f"ff_wp{i}") for i in range(4)]
    sWd = [k.sem(f"ff_wd{i}") for i in range(4)]
    sOp = [k.sem("ff_op0"), k.sem("ff_op1")]
    sT = k.sT
    sV = k.sV
    sS = k.sS
    with ExitStack() as st:
        sb = lambda n, s, d: st.enter_context(k.nc.sbuf_tensor(k._uname(n), s, d))
        ps = lambda n, s: st.enter_context(k.nc.psum_tensor(k._uname(n), s, F32))
        wg = sb("ff_wg", [128, 4 * NCHUNK * 128], BF16)
        wu = sb("ff_wu", [128, 4 * NCHUNK * 128], BF16)
        wd = sb("ff_wd", [128, 4 * D], BF16)
        fuse = sb("ff_fuse", [128, NI * TG], BF16)
        gsb = sb("ff_gsb", [128, 2 * TG], BF16)
        outb = sb("ff_outb", [128, 8 * TG], BF16)
        scg = sb("ff_scg", [128, 64], F32)
        scu = sb("ff_scu", [128, 64], F32)
        scd = sb("ff_scd", [128, 16], F32)
        pool8 = [ps(f"ff_pb{j}", [128, TG]) for j in range(8)]
        psg = [pool8[0], pool8[1]]
        psu = [pool8[2], pool8[3]]

        k.dma("gpsimd", scg[:], g["L0_sgc"], inc=sW)
        k.dma("gpsimd", scu[:], g["L0_suc"], inc=sW)
        k.dma("gpsimd", scd[:], g["L0_sdc"], inc=sW)
        gum = {}
        gm = {}
        wdm = {}

        def gw(key):
            return [gm.get(key, (sT, 0))]

        def load_gu(i, sl):
            for nm, dst in ((wbf["gate"], wg), (wbf["up"], wu)):
                src = nm[:, i * 128:(i + 1) * 128].rearrange("(c p) i -> p c i", p=128)
                d3 = dst[:, sl * 2048:(sl + 1) * 2048].rearrange(
                    "p (c i) -> p c i", c=NCHUNK)
                k.dma("sync", d3, src, waits=[(sT, gum.get(sl, 0))], inc=sWp[sl])

        def load_wd(i, sl):
            k.dma("sync", wd[:, sl * D:(sl + 1) * D],
                  wbf["down"][i * 128:(i + 1) * 128, :],
                  waits=[(sT, wdm.get(sl, 0))], inc=sWd[sl])

        for tg in range(T // TG):
            t0 = tg * TG
            for pre in range(3):
                load_gu(pre, pre)
            for i in range(NI):
                sl = i % 4
                pp = i % 2
                if i + 3 < NI:
                    load_gu(i + 3, (i + 3) % 4)
                for e in range(NCHUNK):
                    k.on("tensor", lambda e_, sl=sl, pp=pp, ec=e, t0=t0: e_.matmul(
                        psg[pp][:], wg[:, sl * 2048 + ec * 128:sl * 2048 + ec * 128 + 128],
                        xnT[:, ec * T + t0:ec * T + t0 + TG],
                        start=(ec == 0), stop=(ec == NCHUNK - 1)),
                        ([(sWp[sl], sWp[sl].n), (sW, sW.n)] + gw(("b", pp)))
                        if e == 0 else [],
                        inc=sT if e == NCHUNK - 1 else None)
                for e in range(NCHUNK):
                    k.on("tensor", lambda e_, sl=sl, pp=pp, ec=e, t0=t0: e_.matmul(
                        psu[pp][:], wu[:, sl * 2048 + ec * 128:sl * 2048 + ec * 128 + 128],
                        xnT[:, ec * T + t0:ec * T + t0 + TG],
                        start=(ec == 0), stop=(ec == NCHUNK - 1)),
                        gw(("b", 2 + pp)) if e == 0 else [],
                        inc=sT if e == NCHUNK - 1 else None)
                gum[sl] = sT.n
                k.on("scalar", lambda e_, i=i, pp=pp: e_.activation(
                     gsb[:, pp * TG:(pp + 1) * TG], psg[pp][:], AF.Silu,
                     scale=scg[:, i:i + 1]), [(sT, sT.n - 1), (sV, sV.n)])
                gm[("b", pp)] = (sS, sS.n)
                k.on("vector", lambda e_, i=i, pp=pp: e_.scalar_tensor_tensor(
                     fuse[:, i * TG:(i + 1) * TG], psu[pp][:], scu[:, i:i + 1],
                     gsb[:, pp * TG:(pp + 1) * TG], ALU.mult, ALU.mult),
                     [(sT, sT.n), (sS, sS.n)])
                gm[("b", 2 + pp)] = (sV, sV.n)
            fuse_mark = sV.n
            for pre in range(3):
                load_wd(pre, pre)
            for dh in range(2):
                for i in range(NI):
                    sl4 = i % 4
                    if i + 3 < NI:
                        load_wd(i + 3, (i + 3) % 4)
                    wmark = (sWd[sl4], sWd[sl4].n)
                    for j in range(8):
                        dc = dh * 8 + j
                        k.on("tensor", lambda e_, sl4=sl4, dc=dc, i=i, j=j, t0=t0:
                             e_.matmul(pool8[j][:],
                             wd[:, sl4 * D + dc * 128:sl4 * D + dc * 128 + 128],
                             fuse[:, i * TG:(i + 1) * TG],
                             start=(i == 0), stop=(i == NI - 1)),
                             ([wmark, (sV, fuse_mark)]
                              + [w for jj in range(8) for w in gw(("b", jj))])
                             if j == 0 and i == 0 else ([wmark] if j == 0 else []),
                             inc=sT if j == 7 else None)
                    wdm[sl4] = sT.n
                if dh + 1 < 2:
                    for pre in range(3):
                        load_wd(pre, pre)
                par = dh % 2
                for j in range(8):
                    dc = dh * 8 + j
                    slot = outb[:, j * TG:(j + 1) * TG]
                    k.on("scalar", lambda e_, dc=dc, j=j, slot=slot: e_.activation(
                         slot, pool8[j][:], AF.Copy, scale=scd[:, dc:dc + 1]),
                         [(sT, sT.n), (sOp[0], sOp[0].n), (sOp[1], sOp[1].n),
                          (sW, sW.n)])
                    gm[("b", j)] = (sS, sS.n)
                    k.dma("sync", f0_fm[dc * 128:(dc + 1) * 128, t0:t0 + TG],
                          slot, waits=[(sS, sS.n)], inc=sOp[par])
        k.barrier()


def precast_ffn_weights(k, g, nc):
    sPre = k.sem("ff_pre")
    wbf = {}
    for nm, src, shp in [("gate", "L0_w_gate", (2048, 8192)),
                         ("up", "L0_w_up", (2048, 8192)),
                         ("down", "L0_w_down", (8192, 2048))]:
        t = dram(nc, f"ff_{nm}_bf", list(shp), BF16)
        for r0 in range(0, shp[0], 256):
            k.dma("gpsimd", t[r0:r0 + 256, :], g[src][r0:r0 + 256, :], inc=sPre)
        wbf[nm] = t
    return wbf


def dram(nc, name, shape, dtype, kind="Internal"):
    return nc.dram_tensor(name, list(shape), dtype, kind=kind).ap()


def build_ln0():
    nc = bass.Bass("TRN2", target_bir_lowering=False)
    x0 = dram(nc, "x0", [T, D], F32, "ExternalInput")
    xn0 = dram(nc, "xn0", [T, D], BF16, "ExternalOutput")
    k = K(nc)
    emit_combine_ln(k, None, x0, [], None, xn0, None)
    k.finalize()
    return nc


def build_l0_ffn(w, debug=False):
    nc = bass.Bass("TRN2", target_bir_lowering=False)
    g = declare_weight_inputs(nc, {kk: w[kk] for kk in
        ["L0_w_gate", "L0_w_up", "L0_w_down", "L0_sgc", "L0_suc", "L0_sdc"]})
    xn0 = dram(nc, "xn0", [T, D], BF16, "ExternalInput")
    f0 = dram(nc, "f0_fm", [D, T], BF16, "ExternalOutput")
    k = K(nc)
    wbf_f = precast_ffn_weights(k, g, nc)
    if debug:
        k._ffn_dbg = dram(nc, "fuse_dbg", [8192, 512], BF16, "ExternalOutput")
        k._ffn_dbg2 = dram(nc, "psg_dbg", [128, 512], F32, "ExternalOutput")
        k._ffn_dbg3 = dram(nc, "slab_dbg", [128, 2048], BF16, "ExternalOutput")
        k._ffn_dbg4 = dram(nc, "xnT_dbg", [128, 2048], BF16, "ExternalOutput")
        k._ffn_dbg2_sb = k.sb("dbg2sb", [128, 512], F32)
    sL = k.sem("ld_fm")
    xnT = k.sb("xnT", [128, NCHUNK * T], BF16)
    emit_load_fm(k, xnT, xn0, sL)
    k.barrier()
    emit_l0_ffn(k, g, xnT, f0, wbf_f)
    k.finalize()
    return nc


def build_combine0(w):
    nc = bass.Bass("TRN2", target_bir_lowering=False)
    x0 = dram(nc, "x0", [T, D], F32, "ExternalInput")
    a0 = dram(nc, "attn0_fm", [D, T], BF16, "ExternalInput")
    f0 = dram(nc, "f0_fm", [D, T], BF16, "ExternalInput")
    x1 = dram(nc, "x1", [T, D], F32, "ExternalOutput")
    xn1 = dram(nc, "xn1", [T, D], BF16, "ExternalOutput")
    xn1f = dram(nc, "xn1f", [T, D], F32, "ExternalOutput")
    k = K(nc)
    emit_combine_ln(k, None, x0, [(a0, "fm_bf"), (f0, "fm_bf")], x1, xn1, xn1f)
    k.finalize()
    return nc


def build_combine1():
    nc = bass.Bass("TRN2", target_bir_lowering=False)
    x1 = dram(nc, "x1", [T, D], F32, "ExternalInput")
    a1 = dram(nc, "attn1_tok", [T, D], BF16, "ExternalInput")
    f1 = dram(nc, "ffn1_tok", [T, D], BF16, "ExternalInput")
    out = dram(nc, "out", [T, D], F32, "ExternalOutput")
    k = K(nc)
    emit_combine_ln(k, None, x1, [(a1, "tok_bf"), (f1, "tok_bf")], out, None, None)
    k.finalize()
    return nc


_L0A_KEYS = ["L0_maa_w1", "L0_w2p", "L0_decay_w1", "L0_decay_w2", "maskU",
             "L0_maa_xc", "L0_maa_rc", "L0_maa_kc", "L0_maa_vc", "L0_maa_wc",
             "L0_maa_gc", "L0_time_decayc", "L0_bgc",
             "L0_WqT", "L0_WkT", "L0_WvT", "L0_WgT", "L0_WoT"]
_L0F_KEYS = ["L0_w_gate", "L0_w_up", "L0_w_down", "L0_sgc", "L0_suc", "L0_sdc"]
_L1A_KEYS = ["L1_WQ", "L1_WK", "L1_WV_flat", "L1_WO", "L1_bQc", "L1_bKc",
             "L1_bV_row", "L1_bO_row", "ones_bf", "ones_f32", "ones_colbf",
             "maskD0", "maskD1", "maskD2", "maskD3"]


def _kernel_multi(**inputs):
    NC = 8
    cores = list(range(NC))
    w = prep_weights(inputs)
    x_all = np.ascontiguousarray(np.asarray(inputs["L0_attn_hidden_states"],
                                            dtype=np.float32))

    def launch(nc, per_core_fn):
        res = run_bass_kernel_spmd(nc, [per_core_fn(c) for c in cores],
                                   core_ids=cores)
        return res.results

    r = launch(build_ln0(), lambda c: dict(x0=x_all[c]))
    xn0 = [r[c]["xn0"] for c in cores]
    wa = {kk: w[kk] for kk in _L0A_KEYS}
    r = launch(build_l0_attn(w), lambda c: dict(xn0=xn0[c], **wa))
    a0 = [r[c]["attn0_fm"] for c in cores]
    wf = {kk: w[kk] for kk in _L0F_KEYS}
    r = launch(build_l0_ffn(w), lambda c: dict(xn0=xn0[c], **wf))
    f0 = [r[c]["f0_fm"] for c in cores]
    r = launch(build_combine0(w), lambda c: dict(x0=x_all[c], attn0_fm=a0[c],
                                                 f0_fm=f0[c]))
    x1 = [r[c]["x1"] for c in cores]
    xn1 = [r[c]["xn1"] for c in cores]
    xn1f = [r[c]["xn1f"] for c in cores]
    wm = {kk: w[kk] for kk in _L1A_KEYS}
    r = launch(build_l1_attn(w), lambda c: dict(xn1=xn1[c], **wm))
    a1 = [r[c]["attn1_tok"] for c in cores]
    wo = {kk: w[kk] for kk in MOE_KEYS}
    r = launch(build_l1_moe(w), lambda c: dict(xn1=xn1[c], xn1f=xn1f[c], **wo))
    f1 = [r[c]["ffn1_tok"] for c in cores]
    r = launch(build_combine1(), lambda c: dict(x1=x1[c], attn1_tok=a1[c],
                                                ffn1_tok=f1[c]))
    out = np.stack([np.asarray(r[c]["out"], dtype=np.float32) for c in cores])
    return out


def emit_l0_attn(k, g, xn_tok, kend_dram, v_dram, attn0_fm, wbf, phases=99, ntau=None):
    TAU = 512
    NTAU = T // TAU
    sX = k.sem("ra_x")
    sW = k.sem("ra_w")
    sWs = [k.sem("ra_ws0"), k.sem("ra_ws1")]
    sKVp = [k.sem("ra_kv0"), k.sem("ra_kv1")]
    sAOp = [k.sem("ra_ao0"), k.sem("ra_ao1")]
    sT, sV, sS = k.sT, k.sV, k.sS
    with ExitStack() as st:
        sb = lambda n, s, d: st.enter_context(k.nc.sbuf_tensor(k._uname(n), s, d))
        ps = lambda n, s: st.enter_context(k.nc.psum_tensor(k._uname(n), s, F32))
        xnT = sb("ra_xnT", [128, NCHUNK * TAU], BF16)
        xnprev = sb("ra_xnprev", [128, NCHUNK], BF16)
        S = sb("ra_S", [128, NCHUNK * 64], F32)
        Sbf = sb("ra_Sbf", [128, NCHUNK * 64], BF16)
        expL = sb("ra_expL", [128, NCHUNK * TAU], BF16)
        qd = sb("ra_qd", [128, NCHUNK * TAU], BF16)
        kd = sb("ra_kd", [128, NCHUNK * TAU], BF16)
        gF = sb("ra_gF", [128, NCHUNK * TAU], BF16)
        xf = sb("ra_xf", [128, NCHUNK * TAU], BF16)
        work = sb("ra_work", [128, 8192], BF16)
        yg = sb("ra_yg", [128, NCHUNK * TAU], BF16)
        t501 = sb("ra_t501", [64, TAU], BF16)
        t523 = sb("ra_t523", [64, TAU], BF16)
        t54 = sb("ra_t54", [32, TAU], BF16)
        d1 = sb("ra_d1", [64, TAU], BF16)
        w1sb = sb("ra_w1sb", [128, NCHUNK * 160], BF16)
        w2sb = sb("ra_w2sb", [64, 5 * 2048], BF16)
        dw1sb = sb("ra_dw1sb", [128, NCHUNK * 64], BF16)
        dw2sb = sb("ra_dw2sb", [64, 2048], BF16)
        cvec = sb("ra_cvec", [128, 8 * NCHUNK], F32)
        wslab = sb("ra_wslab", [128, 2 * 2048], BF16)
        e1b = sb("ra_e1b", [128, TAU], F32)
        cumE = sb("ra_cumE", [128, TAU], F32)
        tmpR = sb("ra_tmpR", [128, TAU], F32)
        elogw = sb("ra_elogw", [128, TAU], F32)
        onem = sb("ra_onem", [128, TAU], BF16)
        expmL = sb("ra_expmL", [128, TAU], BF16)
        expR = sb("ra_expR", [128, TAU], BF16)
        k2b = sb("ra_k2b", [128, TAU], BF16)
        stg = sb("ra_stg", [128, 2 * TAU], BF16)
        mmf = sb("ra_mmf", [128, 2 * TAU], BF16)
        expLcs = sb("ra_expLcs", [128, 64], F32)
        Asb = sb("ra_Asb", [128, 256], BF16)
        masku = sb("ra_masku", [128, 128], BF16)
        outb = sb("ra_outb", [128, 2 * TAU], BF16)
        zeros = sb("ra_zeros", [128, 128], F32)
        pool = [ps(f"ra_p{j}", [128, TAU]) for j in range(4)]
        psAe = ps("ra_psAe", [128, 512])
        psAo = ps("ra_psAo", [128, 512])
        psY = ps("ra_psY", [128, 128])
        psDS = ps("ra_psDS", [128, 64])

        k._ra_gm = {}
        MAA = {"r": 1, "k": 2, "v": 3, "w": 4, "g": 5}
        T5F = {"r": 0, "k": 1, "v": 2, "w": 3, "g": 4}
        k.dma("gpsimd", w1sb[:].rearrange("p (c m) -> p c m", c=NCHUNK),
              g["L0_maa_w1"].rearrange("(c p) m -> p c m", p=128), inc=sW)
        for f in range(5):
            k.dma("gpsimd", w2sb[:, f * 2048:(f + 1) * 2048], g["L0_w2p"][f], inc=sW)
        k.dma("gpsimd", dw1sb[:].rearrange("p (c m) -> p c m", c=NCHUNK),
              g["L0_decay_w1"].rearrange("(c p) m -> p c m", p=128), inc=sW)
        k.dma("gpsimd", dw2sb[:], g["L0_decay_w2"], inc=sW)
        for j, nm in enumerate(["maa_x", "maa_r", "maa_k", "maa_v", "maa_w", "maa_g",
                                "time_decay"]):
            k.dma("gpsimd", cvec[:, j * NCHUNK:(j + 1) * NCHUNK], g[f"L0_{nm}c"], inc=sW)
        k.dma("gpsimd", cvec[:, 7 * NCHUNK:8 * NCHUNK], g["L0_bgc"], inc=sW)
        k.dma("gpsimd", masku[:], g["maskU"], inc=sW)
        k.on("vector", lambda e: e.memset(zeros[:], 0.0))
        k.on("vector", lambda e: e.memset(S[:], 0.0))
        k.on("vector", lambda e: e.memset(Sbf[:], 0.0))
        k.on("vector", lambda e: e.memset(xnprev[:], 0.0))
        wpre = sW.n

        def slabload(dst_slot, srcap, waits):
            d3 = wslab[:, dst_slot * 2048:(dst_slot + 1) * 2048].rearrange(
                "p (c i) -> p c i", c=NCHUNK)
            s3 = srcap.rearrange("(c p) i -> p c i", p=128)
            k.dma("gpsimd", d3, s3, waits=waits, inc=sWs[dst_slot])

        for tg in range(ntau if ntau is not None else NTAU):
            t0 = tg * TAU
            if tg > 0:
                k.on("vector", lambda e: e.tensor_copy(
                    xnprev[:], xnT[:, TAU - 1::TAU]))
            xw_mark = (sV, sV.n)
            for c in range(NCHUNK):
                k.dma("sync", xnT[:, c * TAU:(c + 1) * TAU],
                      xn_tok[t0:t0 + TAU, c * 128:(c + 1) * 128],
                      waits=[xw_mark, (sT, sT.n)], inc=sX, transpose=True)
            xn_mark = (sX, sX.n)
            dx = work[:, 0:NCHUNK * TAU]
            for c in range(NCHUNK):
                k.on("vector", lambda e, c=c: e.tensor_sub(
                    dx[:, c * TAU + 1:(c + 1) * TAU],
                    xnT[:, c * TAU:(c + 1) * TAU - 1],
                    xnT[:, c * TAU + 1:(c + 1) * TAU]),
                    [xn_mark, (sT, sT.n)])
                k.on("vector", lambda e, c=c: e.tensor_sub(
                    dx[:, c * TAU:c * TAU + 1], xnprev[:, c:c + 1],
                    xnT[:, c * TAU:c * TAU + 1]), [])

            def mix(f):
                if f == "x":
                    for c in range(NCHUNK):
                        k.on("vector", lambda e, c=c: e.scalar_tensor_tensor(
                            xf[:, c * TAU:(c + 1) * TAU], dx[:, c * TAU:(c + 1) * TAU],
                            cvec[:, 0 * NCHUNK + c:0 * NCHUNK + c + 1],
                            xnT[:, c * TAU:(c + 1) * TAU], ALU.mult, ALU.add),
                            [(sW, wpre)])
                    k._ra_xfmark = (sV, sV.n)
                    return
                fi = T5F[f]
                rhs = {0: t501, 1: t501, 2: t523, 3: t523, 4: t54}[fi]
                kdim = 32 if fi == 4 else 64
                gm = k._ra_gm
                for c in range(NCHUNK):
                    par = c % 2
                    slot = pool[2 + par]
                    k.on("tensor", lambda e, c=c, fi=fi, rhs=rhs, kdim=kdim, slot=slot:
                         e.matmul(slot[:], w2sb[0:kdim, fi * 2048 + c * 128:
                                                fi * 2048 + c * 128 + 128],
                                  rhs[0:kdim, :], start=True, stop=True),
                         [gm.get(("mx", par), (sT, 0)),
                          gm.get(("pj", 2 + par), (sT, 0)),
                          gm.get(("t5", 0), (sV, 0))], inc=sT)
                    k.on("scalar", lambda e, c=c, f=f, slot=slot, par=par: e.activation(
                         mmf[:, par * TAU:(par + 1) * TAU], slot[:], AF.Identity,
                         bias=cvec[:, MAA[f] * NCHUNK + c:MAA[f] * NCHUNK + c + 1]),
                         [(sT, sT.n), (sW, wpre), gm.get(("mf", par), (sV, 0))])
                    gm[("mx", par)] = (sS, sS.n)
                    k.on("vector", lambda e, c=c, par=par: e.tensor_mul(
                         xf[:, c * TAU:(c + 1) * TAU], dx[:, c * TAU:(c + 1) * TAU],
                         mmf[:, par * TAU:(par + 1) * TAU]), [(sS, sS.n)])
                    k.on("vector", lambda e, c=c: e.tensor_add(
                         xf[:, c * TAU:(c + 1) * TAU], xf[:, c * TAU:(c + 1) * TAU],
                         xnT[:, c * TAU:(c + 1) * TAU]), [])
                    gm[("mf", par)] = (sV, sV.n)
                k._ra_xfmark = (sV, sV.n)

            def proj16(c, slot, slab_slot):
                gm = k._ra_gm
                for e_ in range(NCHUNK):
                    k.on("tensor", lambda e, c=c, e_=e_, slot=slot, ss=slab_slot:
                         e.matmul(pool[slot][:],
                                  wslab[:, ss * 2048 + e_ * 128:ss * 2048 + e_ * 128 + 128],
                                  xf[:, e_ * TAU:(e_ + 1) * TAU],
                                  start=(e_ == 0), stop=(e_ == NCHUNK - 1)),
                         [(sWs[slab_slot], sWs[slab_slot].n), k._ra_xfmark,
                          gm.get(("pj", slot), (sT, 0)),
                          gm.get(("mx", slot - 2), (sT, 0))]
                         if e_ == 0 else [],
                         inc=sT if e_ == NCHUNK - 1 else None)

            mix("x")
            for e_ in range(NCHUNK):
                k.on("tensor", lambda e, e_=e_: e.matmul(
                    pool[0][:], w1sb[:, e_ * 160:e_ * 160 + 128],
                    xf[:, e_ * TAU:(e_ + 1) * TAU],
                    start=(e_ == 0), stop=(e_ == NCHUNK - 1)),
                    [(sW, wpre), (sV, sV.n), (sS, sS.n)] if e_ == 0 else [],
                    inc=sT if e_ == NCHUNK - 1 else None)
            for e_ in range(NCHUNK):
                k.on("tensor", lambda e, e_=e_: e.matmul(
                    pool[1][0:32, :], w1sb[:, e_ * 160 + 128:e_ * 160 + 160],
                    xf[:, e_ * TAU:(e_ + 1) * TAU],
                    start=(e_ == 0), stop=(e_ == NCHUNK - 1)),
                    inc=sT if e_ == NCHUNK - 1 else None)
            k.on("scalar", lambda e: e.activation(t501[:], pool[0][0:64, :], AF.Tanh),
                 [(sT, sT.n)])
            k.on("scalar", lambda e: e.activation(t523[:], pool[0][64:128, :], AF.Tanh), [])
            k.on("scalar", lambda e: e.activation(t54[:], pool[1][0:32, :], AF.Tanh), [])
            k._ra_gm[("t5", 0)] = (sS, sS.n)
            if phases < 2:
                continue
            mix("w")
            for e_ in range(NCHUNK):
                k.on("tensor", lambda e, e_=e_: e.matmul(
                    pool[0][0:64, :], dw1sb[:, e_ * 64:(e_ + 1) * 64],
                    xf[:, e_ * TAU:(e_ + 1) * TAU],
                    start=(e_ == 0), stop=(e_ == NCHUNK - 1)),
                    [(sS, sS.n), (sV, sV.n)] if e_ == 0 else [],
                    inc=sT if e_ == NCHUNK - 1 else None)
            k.on("scalar", lambda e: e.activation(d1[:], pool[0][0:64, :], AF.Tanh),
                 [(sT, sT.n)])
            if phases < 3:
                continue
            mix("k")
            slabload(0, wbf["k"][:, 0:128], [(sT, sT.n)])
            for c in range(NCHUNK):
                if c + 1 < NCHUNK:
                    slabload(1 - c % 2, wbf["k"][:, (c + 1) * 128:(c + 2) * 128],
                             [(sT, sT.n)])
                k.on("tensor", lambda e, c=c: e.matmul(
                    pool[0][:], dw2sb[:, c * 128:(c + 1) * 128], d1[:],
                    start=True, stop=True),
                    [(sS, sS.n), (sV, sV.n)], inc=sT)
                k.on("scalar", lambda e, c=c: e.activation(
                    e1b[:], pool[0][:], AF.Exp,
                    bias=cvec[:, 6 * NCHUNK + c:6 * NCHUNK + c + 1]),
                    [(sT, sT.n), (sW, wpre)])
                for q in range(4):
                    k.on("vector", lambda e, q=q: e.tensor_tensor_scan(
                        cumE[:, q * 128:(q + 1) * 128], e1b[:, q * 128:(q + 1) * 128],
                        zeros[:], 0.0, ALU.add, ALU.add), [(sS, sS.n)])
                k.on("scalar", lambda e, c=c: e.activation(
                    expLcs[:, c * 4:(c + 1) * 4],
                    cumE[:, 127:TAU:128], AF.Exp, scale=-1.0), [(sV, sV.n)])
                k.on("scalar", lambda e, c=c: e.activation(
                    expL[:, c * TAU:(c + 1) * TAU], cumE[:], AF.Exp, scale=-1.0), [])
                k.on("scalar", lambda e: e.activation(expmL[:], cumE[:], AF.Exp), [])
                for q in range(4):
                    k.on("vector", lambda e, q=q: e.tensor_single_scalar(
                        tmpR[:, q * 128:(q + 1) * 128], cumE[:, q * 128:(q + 1) * 128],
                        cumE[:, q * 128 + 127:q * 128 + 128], ALU.subtract), [(sS, sS.n)])
                k.on("scalar", lambda e: e.activation(expR[:], tmpR[:], AF.Exp),
                     [(sV, sV.n)])
                k.on("scalar", lambda e: e.activation(elogw[:], e1b[:], AF.Exp,
                     scale=-1.0), [])
                k.on("vector", lambda e: e.tensor_scalar(onem[:], elogw[:], -1.0, 1.0,
                     ALU.mult, ALU.add), [(sS, sS.n)])
                proj16(c, 2 + c % 2, c % 2)
                k.on("vector", lambda e, c=c: e.tensor_mul(k2b[:], pool[2 + c % 2][:],
                     onem[:]), [(sT, sT.n)])
                k._ra_gm[("pj", 2 + c % 2)] = (sV, sV.n)
                k.on("vector", lambda e, c=c: e.tensor_mul(kd[:, c * TAU:(c + 1) * TAU],
                     k2b[:], expmL[:]), [])
                k.on("vector", lambda e, c=c: e.tensor_mul(stg[:, (c % 2) * TAU:
                     (c % 2 + 1) * TAU], k2b[:], expR[:]),
                     [(sKVp[c % 2], sKVp[c % 2].n)])
                k.dma("sync", kend_dram[c * 128:(c + 1) * 128, t0:t0 + TAU],
                      stg[:, (c % 2) * TAU:(c % 2 + 1) * TAU],
                      waits=[(sV, sV.n)], inc=sKVp[c % 2])
            if phases < 4:
                continue
            mix("r")
            slabload(0, wbf["q"][:, 0:128], [(sT, sT.n)])
            for c in range(NCHUNK):
                if c + 1 < NCHUNK:
                    slabload(1 - c % 2, wbf["q"][:, (c + 1) * 128:(c + 2) * 128],
                             [(sT, sT.n)])
                proj16(c, 2 + c % 2, c % 2)
                k.on("vector", lambda e, c=c: e.tensor_mul(qd[:, c * TAU:(c + 1) * TAU],
                     pool[2 + c % 2][:], expL[:, c * TAU:(c + 1) * TAU]), [(sT, sT.n)])
                k._ra_gm[("pj", 2 + c % 2)] = (sV, sV.n)
            if phases < 5:
                continue
            mix("v")
            slabload(0, wbf["v"][:, 0:128], [(sT, sT.n)])
            for c in range(NCHUNK):
                if c + 1 < NCHUNK:
                    slabload(1 - c % 2, wbf["v"][:, (c + 1) * 128:(c + 2) * 128],
                             [(sT, sT.n)])
                proj16(c, 2 + c % 2, c % 2)
                k.on("vector", lambda e, c=c: e.tensor_copy(
                     stg[:, (c % 2) * TAU:(c % 2 + 1) * TAU], pool[2 + c % 2][:]),
                     [(sT, sT.n), (sKVp[c % 2], sKVp[c % 2].n)])
                k._ra_gm[("pj", 2 + c % 2)] = (sV, sV.n)
                k.dma("sync", v_dram[c * 128:(c + 1) * 128, t0:t0 + TAU],
                      stg[:, (c % 2) * TAU:(c % 2 + 1) * TAU],
                      waits=[(sV, sV.n)], inc=sKVp[c % 2])
            if phases < 6:
                continue
            mix("g")
            slabload(0, wbf["g"][:, 0:128], [(sT, sT.n)])
            for c in range(NCHUNK):
                if c + 1 < NCHUNK:
                    slabload(1 - c % 2, wbf["g"][:, (c + 1) * 128:(c + 2) * 128],
                             [(sT, sT.n)])
                proj16(c, 2 + c % 2, c % 2)
                k.on("scalar", lambda e, c=c: e.activation(
                     gF[:, c * TAU:(c + 1) * TAU], pool[2 + c % 2][:], AF.Silu,
                     bias=cvec[:, 7 * NCHUNK + c:7 * NCHUNK + c + 1]),
                     [(sT, sT.n), (sW, wpre)])
                k._ra_gm[("pj", 2 + c % 2)] = (sS, sS.n)
            if phases < 7:
                continue
            kv_mark = [(sKVp[0], sKVp[0].n), (sKVp[1], sKVp[1].n)]
            for cc in range(4):
                ket = work[:, (cc % 2) * 2048:(cc % 2) * 2048 + 2048]
                vt = work[:, 4096 + (cc % 2) * 2048:4096 + (cc % 2) * 2048 + 2048]
                k.dma("sync", ket, kend_dram[:, t0 + cc * 128:t0 + (cc + 1) * 128],
                      waits=kv_mark + [(sT, sT.n)], inc=sX, transpose=True)
                k.dma("sync", vt, v_dram[:, t0 + cc * 128:t0 + (cc + 1) * 128],
                      waits=kv_mark + [(sT, sT.n)], inc=sX, transpose=True)
                kvx = (sX, sX.n)
                for hp in range(NCHUNK):
                    cs = slice(hp * TAU + cc * 128, hp * TAU + cc * 128 + 128)
                    k.on("tensor", lambda e, cs=cs: e.matmul(
                        psAe[:, 0:128], kd[0:64, cs], qd[0:64, cs],
                        start=True, stop=True), [(sV, sV.n), kvx])
                    k.on("tensor", lambda e, cs=cs: e.matmul(
                        psAo[:, 0:128], kd[64:128, cs], qd[64:128, cs],
                        start=True, stop=True), [], inc=sT)
                    k.on("vector", lambda e: e.tensor_mul(Asb[:, 0:128], psAe[:, 0:128],
                         masku[:]), [(sT, sT.n)])
                    k.on("vector", lambda e: e.tensor_mul(Asb[:, 128:256],
                         psAo[:, 0:128], masku[:]), [])
                    k.on("tensor", lambda e, hp=hp, vt=vt: e.matmul(
                        psY[0:64, :], vt[:, hp * 128:hp * 128 + 64], Asb[:, 0:128],
                        start=True, stop=False), [(sV, sV.n)])
                    k.on("tensor", lambda e, hp=hp, cs=cs: e.matmul(
                        psY[0:64, :], Sbf[0:64, hp * 64:(hp + 1) * 64], qd[0:64, cs],
                        start=False, stop=True), [])
                    k.on("tensor", lambda e, hp=hp, vt=vt: e.matmul(
                        psY[64:128, :], vt[:, hp * 128 + 64:hp * 128 + 128],
                        Asb[:, 128:256], start=True, stop=False,
                        tile_position=(0, 64)), [])
                    k.on("tensor", lambda e, hp=hp, cs=cs: e.matmul(
                        psY[64:128, :], Sbf[64:128, hp * 64:(hp + 1) * 64],
                        qd[64:128, cs], start=False, stop=True,
                        tile_position=(64, 64)), [], inc=sT)
                    k.on("tensor", lambda e, hp=hp, ket=ket, vt=vt: e.matmul(
                        psDS[0:64, :], ket[:, hp * 128:hp * 128 + 64],
                        vt[:, hp * 128:hp * 128 + 64], start=True, stop=True), [])
                    k.on("tensor", lambda e, hp=hp, ket=ket, vt=vt: e.matmul(
                        psDS[64:128, :], ket[:, hp * 128 + 64:hp * 128 + 128],
                        vt[:, hp * 128 + 64:hp * 128 + 128], start=True, stop=True,
                        tile_position=(0, 64)), [], inc=sT)
                    k.on("vector", lambda e, hp=hp, cc=cc: e.tensor_mul(
                        yg[:, hp * TAU + cc * 128:hp * TAU + cc * 128 + 128], psY[:],
                        gF[:, hp * TAU + cc * 128:hp * TAU + cc * 128 + 128]),
                        [(sT, sT.n - 1), (sS, sS.n)])
                    k.on("vector", lambda e, hp=hp, cc=cc: e.scalar_tensor_tensor(
                        S[:, hp * 64:(hp + 1) * 64], S[:, hp * 64:(hp + 1) * 64],
                        expLcs[:, hp * 4 + cc:hp * 4 + cc + 1], psDS[:],
                        ALU.mult, ALU.add), [(sT, sT.n)])
                    k.on("vector", lambda e, hp=hp: e.tensor_copy(
                        Sbf[:, hp * 64:(hp + 1) * 64], S[:, hp * 64:(hp + 1) * 64]), [])
            if phases < 8:
                continue
            slabload(0, wbf["o"][:, 0:128], [(sT, sT.n)])
            for c in range(NCHUNK):
                if c + 1 < NCHUNK:
                    slabload(1 - c % 2, wbf["o"][:, (c + 1) * 128:(c + 2) * 128],
                             [(sT, sT.n)])
                slot = 2 + c % 2
                for e_ in range(NCHUNK):
                    k.on("tensor", lambda e, c=c, e_=e_, slot=slot: e.matmul(
                         pool[slot][:],
                         wslab[:, (c % 2) * 2048 + e_ * 128:(c % 2) * 2048 + e_ * 128 + 128],
                         yg[:, e_ * TAU:(e_ + 1) * TAU],
                         start=(e_ == 0), stop=(e_ == NCHUNK - 1)),
                         [(sWs[c % 2], sWs[c % 2].n), (sV, sV.n)] if e_ == 0 else [],
                         inc=sT if e_ == NCHUNK - 1 else None)
                k.on("vector", lambda e, c=c, slot=slot: e.tensor_copy(
                     outb[:, (c % 2) * TAU:(c % 2 + 1) * TAU], pool[slot][:]),
                     [(sT, sT.n), (sAOp[c % 2], sAOp[c % 2].n)])
                k.dma("sync", attn0_fm[c * 128:(c + 1) * 128, t0:t0 + TAU],
                      outb[:, (c % 2) * TAU:(c % 2 + 1) * TAU],
                      waits=[(sV, sV.n)], inc=sAOp[c % 2])
        k.barrier()


def build_l0_attn(w, phases=99, ntau=None):
    nc = bass.Bass("TRN2", target_bir_lowering=False)
    keys = ["L0_maa_w1", "L0_w2p", "L0_decay_w1", "L0_decay_w2", "maskU",
            "L0_maa_xc", "L0_maa_rc", "L0_maa_kc", "L0_maa_vc", "L0_maa_wc",
            "L0_maa_gc", "L0_time_decayc", "L0_bgc",
            "L0_WqT", "L0_WkT", "L0_WvT", "L0_WgT", "L0_WoT"]
    g = declare_weight_inputs(nc, {kk: w[kk] for kk in keys})
    xn0 = dram(nc, "xn0", [T, D], BF16, "ExternalInput")
    a0 = dram(nc, "attn0_fm", [D, T], BF16, "ExternalOutput")
    kend_dram = dram(nc, "kend_dram", [D, T], BF16)
    v_dram = dram(nc, "v_dram", [D, T], BF16)
    wbf = {}
    k = K(nc)
    sPre = k.sem("precast")
    for short, nm in [("q", "L0_WqT"), ("k", "L0_WkT"), ("v", "L0_WvT"),
                      ("g", "L0_WgT"), ("o", "L0_WoT")]:
        wbf[short] = dram(nc, nm + "_bf", [D, D], BF16)
        for r0 in range(0, D, 256):
            k.dma("gpsimd", wbf[short][r0:r0 + 256, :], g[nm][r0:r0 + 256, :], inc=sPre)
    k.barrier()
    emit_l0_attn(k, g, xn0, kend_dram, v_dram, a0, wbf, phases, ntau)
    k.finalize()
    return nc


def emit_l1_attn(k, g, xn_tok, qk_dram, v1_dram, attn1_tok):
    TAU = 512
    SC = 1.0 / math.sqrt(128.0)
    sX = k.sem("ma_x")
    sW = k.sem("ma_w")
    sWh = [k.sem("ma_wh0"), k.sem("ma_wh1")]
    sQp = [k.sem("ma_qp0"), k.sem("ma_qp1")]
    sHL = k.sem("ma_hl")
    sAO = [k.sem("ma_ao0"), k.sem("ma_ao1")]
    sT, sV, sS = k.sT, k.sV, k.sS
    with ExitStack() as st:
        sb = lambda n, s, d: st.enter_context(k.nc.sbuf_tensor(k._uname(n), s, d))
        ps = lambda n, s: st.enter_context(k.nc.psum_tensor(k._uname(n), s, F32))
        xnT = sb("ma_xnT", [128, NCHUNK * TAU], BF16)
        wqk = sb("ma_wqk", [128, 2 * 2048], BF16)
        wbig = sb("ma_wbig", [128, NCHUNK * 2048], BF16)
        bq = sb("ma_bq", [128, 16], F32)
        bk = sb("ma_bk", [128, 16], F32)
        bvrow = sb("ma_bvrow", [1, 2048], BF16)
        borow = sb("ma_borow", [1, 2048], BF16)
        onesb = sb("ma_onesb", [1, 128], BF16)
        onesf = sb("ma_onesf", [1, 128], F32)
        onescol = sb("ma_onescol", [128, 1], BF16)
        stq = sb("ma_stq", [128, 2 * TAU], BF16)
        vstg = sb("ma_vstg", [128, 2 * 2048], BF16)
        masks = sb("ma_masks", [128, 4 * TAU], BF16)
        qh = sb("ma_qh", [128, 2048], BF16)
        kh = sb("ma_kh", [128, 2048], BF16)
        vh = sb("ma_vh", [128, 2048], BF16)
        sqb = sb("ma_sqb", [128, 2048], BF16)
        ptile = sb("ma_ptile", [128, 2 * TAU], BF16)
        zT = sb("ma_zT", [128, 16 * 2048], BF16)
        ILsb = sb("ma_ILsb", [128, TAU], F32)
        nsc = sb("ma_nsc", [1, 16], F32)
        mhc = sb("ma_mhc", [128, 1], F32)
        aostg = sb("ma_aostg", [128, 2 * 2048], BF16)
        pool = [ps(f"ma_p{j}", [128, TAU]) for j in range(4)]
        psZ = ps("ma_psZ", [128, TAU])
        psL = ps("ma_psL", [128, TAU])
        psB = ps("ma_psB", [128, TAU])

        k._ma_gm = {}
        k.dma("gpsimd", bq[:], g["L1_bQc"], inc=sW)
        k.dma("gpsimd", bk[:], g["L1_bKc"], inc=sW)
        k.dma("gpsimd", bvrow[:], g["L1_bV_row"], inc=sW)
        k.dma("gpsimd", borow[:], g["L1_bO_row"], inc=sW)
        k.dma("gpsimd", onesb[:], g["ones_bf"], inc=sW)
        k.dma("gpsimd", onesf[:], g["ones_f32"], inc=sW)
        k.dma("gpsimd", onescol[:], g["ones_colbf"], inc=sW)
        for o in range(4):
            k.dma("gpsimd", masks[:, o * TAU:(o + 1) * TAU], g[f"maskD{o}"], inc=sW)
        for e_ in range(NCHUNK):
            k.dma("gpsimd", wbig[:, e_ * 2048:(e_ + 1) * 2048],
                  g["L1_WV_flat"][e_ * 128:(e_ + 1) * 128, :], inc=sW)
        wpre = sW.n

        for tg in range(T // TAU):
            t0 = tg * TAU
            for c in range(NCHUNK):
                k.dma("sync", xnT[:, c * TAU:(c + 1) * TAU],
                      xn_tok[t0:t0 + TAU, c * 128:(c + 1) * 128],
                      waits=[(sV, sV.n), (sT, sT.n), (sS, sS.n)], inc=sX,
                      transpose=True)
            xmark = (sX, sX.n)
            for (wname, brow, off) in [("L1_WQ", bq, 0), ("L1_WK", bk, D)]:
                k.dma("gpsimd", wqk[:, 0:2048].rearrange("p (c m) -> p c m", c=NCHUNK),
                      g[wname][0].rearrange("(c p) m -> p c m", p=128),
                      waits=[(sT, sT.n)], inc=sWh[0])
                for h in range(16):
                    par = h % 2
                    if h + 1 < 16:
                        k.dma("gpsimd",
                              wqk[:, (1 - par) * 2048:(2 - par) * 2048].rearrange(
                                  "p (c m) -> p c m", c=NCHUNK),
                              g[wname][h + 1].rearrange("(c p) m -> p c m", p=128),
                              waits=[(sT, sT.n)], inc=sWh[1 - par])
                    slot = pool[2 + par]
                    for e_ in range(NCHUNK):
                        k.on("tensor", lambda e, e_=e_, par=par, slot=slot: e.matmul(
                             slot[:],
                             wqk[:, par * 2048 + e_ * 128:par * 2048 + e_ * 128 + 128],
                             xnT[:, e_ * TAU:(e_ + 1) * TAU],
                             start=(e_ == 0), stop=(e_ == NCHUNK - 1)),
                             [(sWh[par], sWh[par].n), xmark,
                              k._ma_gm.get(("qk", par), (sT, 0))]
                             if e_ == 0 else [], inc=sT if e_ == NCHUNK - 1 else None)
                    k.on("scalar", lambda e, h=h, par=par, brow=brow, slot=slot:
                         e.activation(stq[:, par * TAU:(par + 1) * TAU], slot[:],
                                      AF.Identity, bias=brow[:, h:h + 1]),
                         [(sT, sT.n), (sQp[par], sQp[par].n), (sW, wpre)])
                    k._ma_gm[("qk", par)] = (sS, sS.n)
                    k.dma("sync", qk_dram[off + h * 128:off + (h + 1) * 128,
                          t0:t0 + TAU], stq[:, par * TAU:(par + 1) * TAU],
                          waits=[(sS, sS.n)], inc=sQp[par])
            for tt in range(4):
                trel = tt * 128
                par = tt % 2
                for e_ in range(NCHUNK):
                    for d4 in range(4):
                        k.on("tensor", lambda e, e_=e_, d4=d4, trel=trel: e.matmul(
                             pool[d4][:],
                             xnT[:, e_ * TAU + trel:e_ * TAU + trel + 128],
                             wbig[:, e_ * 2048 + d4 * 512:e_ * 2048 + d4 * 512 + 512],
                             start=(e_ == 0), stop=False),
                             [(sW, wpre), k._ma_gm.get(("v", 0), (sT, 0)),
                              k._ma_gm.get(("qk", 0), (sT, 0)),
                              k._ma_gm.get(("qk", 1), (sT, 0))]
                             if e_ == 0 and d4 == 0 else [])
                for d4 in range(4):
                    k.on("tensor", lambda e, d4=d4: e.matmul(
                         pool[d4][:], onesb[:], bvrow[:, d4 * 512:(d4 + 1) * 512],
                         start=False, stop=True), [],
                         inc=sT if d4 == 3 else None)
                for d4 in range(4):
                    k.on("vector", lambda e, d4=d4, par=par: e.tensor_copy(
                         vstg[:, par * 2048 + d4 * 512:par * 2048 + (d4 + 1) * 512],
                         pool[d4][:]),
                         [(sT, sT.n), (sQp[par], sQp[par].n)] if d4 == 0 else [])
                k._ma_gm[("v", 0)] = (sV, sV.n)
                k.dma("sync", v1_dram[t0 + trel:t0 + trel + 128, :],
                      vstg[:, par * 2048:(par + 1) * 2048],
                      waits=[(sV, sV.n)], inc=sQp[par])
        k.barrier()
        for h in range(16):
            k.dma("gpsimd", wbig[:, h * 2048:(h + 1) * 2048], g["L1_WO"][h], inc=sW)

        for h in range(16):
            k.dma("sync", qh[:], qk_dram[h * 128:(h + 1) * 128, :],
                  waits=[(sT, sT.n), (sS, sS.n)], inc=sHL)
            k.dma("sync", kh[:], qk_dram[D + h * 128:D + (h + 1) * 128, :],
                  waits=[(sT, sT.n)], inc=sHL)
            k.dma("sync", vh[:].rearrange("p (tt hd) -> p tt hd", tt=NTILE),
                  v1_dram[:, h * 128:(h + 1) * 128].rearrange(
                      "(tt p) hd -> p tt hd", p=128),
                  waits=[(sT, sT.n)], inc=sHL)
            hmark = (sHL, sHL.n)
            for j, src in [(0, qh), (1, kh)]:
                k.on("scalar", lambda e, src=src: e.activation(sqb[:], src[:],
                     AF.Square), [hmark, (sV, sV.n)])
                for q4 in range(4):
                    k.on("tensor", lambda e, q4=q4: e.matmul(
                         psB[0:1, :], onescol[:], sqb[:, q4 * 512:(q4 + 1) * 512],
                         start=True, stop=True), [(sS, sS.n), (sV, sV.n)],
                         inc=sT)
                    k.on("vector", lambda e, j=j, q4=q4: e.tensor_reduce(
                         nsc[0:1, j * 4 + q4:j * 4 + q4 + 1], psB[0:1, :], AX.X,
                         ALU.max), [(sT, sT.n)])
            k.on("vector", lambda e: e.tensor_reduce(nsc[0:1, 8:9], nsc[0:1, 0:4],
                 AX.X, ALU.max), [])
            k.on("vector", lambda e: e.tensor_reduce(nsc[0:1, 9:10], nsc[0:1, 4:8],
                 AX.X, ALU.max), [])
            k.on("vector", lambda e: e.tensor_tensor(nsc[0:1, 10:11], nsc[0:1, 8:9],
                 nsc[0:1, 9:10], ALU.mult), [])
            k.on("scalar", lambda e: e.activation(nsc[0:1, 11:12], nsc[0:1, 10:11],
                 AF.Sqrt), [(sV, sV.n)])
            k.on("vector", lambda e: e.tensor_scalar_mul(nsc[0:1, 12:13],
                 nsc[0:1, 11:12], -SC), [(sS, sS.n)])
            k.on("tensor", lambda e: e.matmul(psB[:, 0:1], onesf[:],
                 nsc[0:1, 12:13], start=True, stop=True), [(sV, sV.n)], inc=sT)
            k.on("vector", lambda e: e.tensor_copy(mhc[:], psB[:, 0:1]),
                 [(sT, sT.n)])
            k._ma_gm[("mh",)] = (sV, sV.n)
            for qg in range(4):
                nkt = 4 * qg + 4
                pm = {}

                def emit_zl(kt, nkt=nkt, qg=qg):
                    par = kt % 2
                    zw = list(pm[kt])
                    if kt == 0:
                        zw.append(k._ma_gm.get(("zn",), (sT, 0)))
                    k.on("tensor", lambda e, kt=kt, par=par, nkt=nkt: e.matmul(
                         psZ[:], vh[:, kt * 128:(kt + 1) * 128],
                         ptile[:, par * TAU:(par + 1) * TAU],
                         start=(kt == 0), stop=(kt == nkt - 1)), zw)
                    k.on("tensor", lambda e, kt=kt, par=par, nkt=nkt: e.matmul(
                         psL[0:1, :], onescol[:],
                         ptile[:, par * TAU:(par + 1) * TAU],
                         start=(kt == 0), stop=(kt == nkt - 1)), [],
                         inc=sT if kt == nkt - 1 else None)

                for kt in range(nkt):
                    par = kt % 2
                    k.on("tensor", lambda e, kt=kt, qg=qg, par=par: e.matmul(
                         pool[par][:], kh[:, kt * 128:(kt + 1) * 128],
                         qh[:, qg * 512:(qg + 1) * 512], start=True, stop=True),
                         [k._ma_gm.get(("sc", par), (sT, 0)),
                          k._ma_gm.get(("v", 0), (sT, 0))], inc=sT)
                    k.on("scalar", lambda e, par=par: e.activation(
                         ptile[:, par * TAU:(par + 1) * TAU], pool[par][:], AF.Exp,
                         bias=mhc[:], scale=SC),
                         [(sT, sT.n), k._ma_gm[("mh",)]])
                    k._ma_gm[("sc", par)] = (sS, sS.n)
                    o = kt - 4 * qg
                    if o >= 0:
                        k.on("vector", lambda e, par=par, o=o: e.tensor_mul(
                             ptile[:, par * TAU:(par + 1) * TAU],
                             ptile[:, par * TAU:(par + 1) * TAU],
                             masks[:, o * TAU:(o + 1) * TAU]),
                             [(sS, sS.n), (sW, wpre)])
                        pm[kt] = [(sV, sV.n)]
                    else:
                        pm[kt] = [(sS, sS.n)]
                    if kt >= 1:
                        emit_zl(kt - 1)
                emit_zl(nkt - 1)
                k.on("vector", lambda e: e.reciprocal(ILsb[0:1, :], psL[0:1, :]),
                     [(sT, sT.n)])
                k.on("tensor", lambda e: e.matmul(psB[:], onesf[:], ILsb[0:1, :],
                     start=True, stop=True), [(sV, sV.n)], inc=sT)
                k.on("vector", lambda e: e.tensor_copy(ILsb[:], psB[:]),
                     [(sT, sT.n)])
                k.on("vector", lambda e, h=h, qg=qg: e.tensor_mul(
                     zT[:, h * 2048 + qg * 512:h * 2048 + (qg + 1) * 512],
                     psZ[:], ILsb[:]), [])
                k._ma_gm[("zn",)] = (sV, sV.n)
        k.barrier()
        for tt in range(NTILE):
            trel = tt * 128
            par = tt % 2
            for h in range(16):
                for d4 in range(4):
                    k.on("tensor", lambda e, h=h, d4=d4, trel=trel: e.matmul(
                         pool[d4][:], zT[:, h * 2048 + trel:h * 2048 + trel + 128],
                         wbig[:, h * 2048 + d4 * 512:h * 2048 + d4 * 512 + 512],
                         start=(h == 0), stop=False),
                         [(sW, sW.n), (sV, sV.n)] if h == 0 and d4 == 0 else [])
            for d4 in range(4):
                k.on("tensor", lambda e, d4=d4: e.matmul(
                     pool[d4][:], onesb[:], borow[:, d4 * 512:(d4 + 1) * 512],
                     start=False, stop=True), [], inc=sT if d4 == 3 else None)
            for d4 in range(4):
                k.on("vector", lambda e, d4=d4, par=par: e.tensor_copy(
                     aostg[:, par * 2048 + d4 * 512:par * 2048 + (d4 + 1) * 512],
                     pool[d4][:]),
                     [(sT, sT.n), (sAO[par], sAO[par].n)] if d4 == 0 else [])
            k.dma("sync", attn1_tok[trel:trel + 128, :],
                  aostg[:, par * 2048:(par + 1) * 2048],
                  waits=[(sV, sV.n)], inc=sAO[par])
        k.barrier()


def build_l1_attn(w):
    nc = bass.Bass("TRN2", target_bir_lowering=False)
    keys = ["L1_WQ", "L1_WK", "L1_WV_flat", "L1_WO", "L1_bQc", "L1_bKc",
            "L1_bV_row", "L1_bO_row", "ones_bf", "ones_f32", "ones_colbf",
            "maskD0", "maskD1", "maskD2", "maskD3"]
    g = declare_weight_inputs(nc, {kk: w[kk] for kk in keys})
    xn1 = dram(nc, "xn1", [T, D], BF16, "ExternalInput")
    a1 = dram(nc, "attn1_tok", [T, D], BF16, "ExternalOutput")
    qk_dram = dram(nc, "qk_dram", [2 * D, T], BF16)
    v1_dram = dram(nc, "v1_dram", [T, D], BF16)
    k = K(nc)
    emit_l1_attn(k, g, xn1, qk_dram, v1_dram, a1)
    k.finalize()
    return nc


def emit_l1_moe(k, g, wbf, xn1_tok, xn1_f32, xt1_tok, ffn1_tok):
    TAU = 512
    NE = 6
    BIG = 1e9
    sX = k.sem("mo_x")
    sW = k.sem("mo_w")
    sWs = [k.sem(f"mo_ws{i}") for i in range(4)]
    sWd = [k.sem("mo_wd0"), k.sem("mo_wd1")]
    sR = k.sem("mo_r")
    sOp = [k.sem("mo_op0"), k.sem("mo_op1")]
    sT, sV, sS = k.sT, k.sV, k.sS
    with ExitStack() as st:
        sb = lambda n, s, d: st.enter_context(k.nc.sbuf_tensor(k._uname(n), s, d))
        ps = lambda n, s: st.enter_context(k.nc.psum_tensor(k._uname(n), s, F32))
        xtT = sb("mo_xtT", [128, NCHUNK * TAU], BF16)
        h1 = sb("mo_h1", [128, 8 * TAU], BF16)
        h2 = sb("mo_h2", [128, 8 * TAU], BF16)
        sh = sb("mo_sh", [128, NCHUNK * TAU], BF16)
        acc = sb("mo_acc", [128, 4 * 2048], F32)
        wdn2 = [sb("mo_wdnA", [128, 8 * 2048], BF16),
                sb("mo_wdnB", [128, 8 * 2048], BF16)]
        wslab = sb("mo_wslab", [128, 4 * 2048], BF16)
        wgsb = sb("mo_wgsb", [128, NCHUNK * 8], BF16)
        swsb = sb("mo_swsb", [128, 2 * TAU], BF16)
        gate = sb("mo_gate", [128, 8], F32)
        tmp8 = sb("mo_tmp8", [128, 8], F32)
        m8 = sb("mo_m8", [128, 8], F32)
        p8 = sb("mo_p8", [128, 8], F32)
        gcols = sb("mo_gcols", [128, 8], F32)
        wcols = sb("mo_wcols", [128, 4 * 8], F32)
        bup = sb("mo_bup", [128, 2 * 8], F32)
        bsw = sb("mo_bsw", [128, 2 * 8], F32)
        bsu = sb("mo_bsu", [128, 16], F32)
        bsa = sb("mo_bsa", [128, 8], F32)
        bgrow = sb("mo_bgrow", [1, 8], BF16)
        bdnrow = sb("mo_bdnrow", [1, 6 * 2048], BF16)
        bsdrow = sb("mo_bsdrow", [1, 2048], BF16)
        onesb = sb("mo_onesb", [1, 128], BF16)
        res = sb("mo_res", [128, 2048], F32)
        stg = sb("mo_stg", [128, 2 * 2048], BF16)
        pool = [ps(f"mo_p{j}", [128, TAU]) for j in range(8)]

        k.dma("gpsimd", onesb[:], g["ones_bf"], inc=sW)
        k.dma("gpsimd", bsu[:], g["L1_bsuc"], inc=sW)
        k.dma("gpsimd", bsa[:], g["L1_bsac"], inc=sW)
        k.dma("gpsimd", bsdrow[:], g["L1_bsd_row"], inc=sW)
        for n in range(NE):
            k.dma("gpsimd", bdnrow[:, n * 2048:(n + 1) * 2048],
                  g["L1_bdn_row"][n], inc=sW)

        slabm = [0, 0]
        projm = {}

        def pw(slot):
            return [projm.get(slot, (sT, 0))]

        def dn_guards(bs):
            gw = pw(("dn", bs))
            if bs == 0:
                gw += pw(id(pool[2])) + pw(id(pool[3]))
            else:
                gw += pw(id(pool[6]))
            return gw

        def colslab(dst_slot, srcap, nch, tmark):
            d3 = wslab[:, dst_slot * 2048:dst_slot * 2048 + nch * 128].rearrange(
                "p (c i) -> p c i", c=nch)
            s3 = srcap.rearrange("(c p) i -> p c i", p=128)
            k.dma("sync", d3, s3, waits=[(sT, tmark)], inc=sWs[dst_slot])

        ring_tmarks = {}

        def ring_proj(srcs, nch, rhs_fn, epi, slotbase=2):
            ng = len(srcs)
            tmarks = ring_tmarks
            for pre in range(min(4 - 1, ng)):
                colslab(pre % 4, srcs[pre], nch, tmarks.get(pre % 4, 0))
            for i in range(ng):
                isl = i % 4
                if i + 3 < ng:
                    colslab((i + 3) % 4, srcs[i + 3], nch,
                            tmarks.get((i + 3) % 4, 0))
                slot = pool[2 + i % 2]
                proj_slab(nch, isl, slot, rhs_fn)
                tmarks[isl] = sT.n
                epi(i, slot)

        def proj_slab(nch_k, slab_slot, dst_ps, rhs_fn, extra=()):
            for e_ in range(nch_k):
                k.on("tensor", lambda e, e_=e_, ss=slab_slot, dst=dst_ps,
                     rhs=rhs_fn(e_): e.matmul(
                     dst[:], wslab[:, ss * 2048 + e_ * 128:ss * 2048 + e_ * 128 + 128],
                     rhs, start=(e_ == 0), stop=(e_ == nch_k - 1)),
                     ([(sWs[slab_slot], sWs[slab_slot].n)] + pw(id(dst_ps))
                      + pw(("dn", 0)) + list(extra)) if e_ == 0 else [],
                     inc=sT if e_ == nch_k - 1 else None)

        for j in range(2):
            xin = xn1_tok if j == 0 else xt1_tok
            xout = xt1_tok if j == 0 else ffn1_tok
            k.dma("gpsimd", wgsb[:].rearrange("p (c m) -> p c m", c=NCHUNK),
                  g["L1_Wg"][j].rearrange("(c p) m -> p c m", p=128),
                  waits=[(sT, sT.n)], inc=sW)
            k.dma("gpsimd", bgrow[:], g["L1_bg_row"][j], waits=[(sT, sT.n)], inc=sW)
            wpre = sW.n
            dnm = [sT.n, sT.n]
            for tg in range(T // TAU):
                t0 = tg * TAU
                for c in range(NCHUNK):
                    k.dma("sync", xtT[:, c * TAU:(c + 1) * TAU],
                          xin[t0:t0 + TAU, c * 128:(c + 1) * 128],
                          waits=[(sV, sV.n), (sT, sT.n), (sS, sS.n)], inc=sX,
                          transpose=True)
                xmark = (sX, sX.n)
                for tt in range(4):
                    trel = tt * 128
                    gps = pool[6]
                    for e_ in range(NCHUNK):
                        k.on("tensor", lambda e, e_=e_, trel=trel, gps=gps: e.matmul(
                             gps[:, 0:8], xtT[:, e_ * TAU + trel:e_ * TAU + trel + 128],
                             wgsb[:, e_ * 8:(e_ + 1) * 8],
                             start=(e_ == 0), stop=False),
                             ([(sW, wpre), xmark, (sV, sV.n)] + pw(("dn", 4)))
                             if e_ == 0 else [])
                    k.on("tensor", lambda e, gps=gps: e.matmul(gps[:, 0:8], onesb[:],
                         bgrow[:], start=False, stop=True), [], inc=sT)
                    k.on("vector", lambda e, gps=gps: e.tensor_copy(gate[:], gps[:, 0:8]),
                         [(sT, sT.n)])
                    k.on("vector", lambda e: e.tensor_reduce(gcols[:, 0:1], gate[:],
                         AX.X, ALU.min), [])
                    k.on("vector", lambda e: e.tensor_single_scalar(tmp8[:], gate[:],
                         gcols[:, 0:1], ALU.is_equal), [])
                    k.on("vector", lambda e: e.scalar_tensor_tensor(tmp8[:], tmp8[:],
                         BIG, gate[:], ALU.mult, ALU.add), [])
                    k.on("vector", lambda e: e.tensor_reduce(gcols[:, 1:2], tmp8[:],
                         AX.X, ALU.min), [])
                    k.on("vector", lambda e: e.tensor_single_scalar(m8[:], gate[:],
                         gcols[:, 1:2], ALU.is_gt), [])
                    k.on("vector", lambda e: e.tensor_reduce(gcols[:, 2:3], gate[:],
                         AX.X, ALU.max), [])
                    k.on("vector", lambda e: e.tensor_scalar_mul(gcols[:, 3:4],
                         gcols[:, 2:3], -1.0), [])
                    k.on("scalar", lambda e: e.activation(p8[:], gate[:], AF.Exp,
                         bias=gcols[:, 3:4]), [(sV, sV.n)])
                    k.on("vector", lambda e: e.tensor_mul(p8[:], p8[:], m8[:]),
                         [(sS, sS.n)])
                    k.on("vector", lambda e: e.tensor_reduce(gcols[:, 4:5], p8[:],
                         AX.X, ALU.add), [])
                    k.on("vector", lambda e: e.reciprocal(gcols[:, 5:6],
                         gcols[:, 4:5]), [])
                    k.on("vector", lambda e, tt=tt: e.tensor_single_scalar(
                         wcols[:, tt * 8:(tt + 1) * 8], p8[:], gcols[:, 5:6],
                         ALU.mult), [])
                projm[id(pool[6])] = (sV, sV.n)
                gate_done = (sV, sV.n)
                for n in range(NE):
                    wsl = n % 2
                    if n == 0:
                        for e_ in range(8):
                            k.dma("sync", wdn2[0][:, e_ * 2048:(e_ + 1) * 2048],
                                  wbf["dn"][0][e_ * 128:(e_ + 1) * 128, :],
                                  waits=[(sT, dnm[0])] if e_ == 0 else [],
                                  inc=sWd[0])
                    if n + 1 < NE:
                        for e_ in range(8):
                            k.dma("sync", wdn2[1 - wsl][:, e_ * 2048:(e_ + 1) * 2048],
                                  wbf["dn"][n + 1][e_ * 128:(e_ + 1) * 128, :],
                                  waits=[(sT, dnm[1 - wsl])] if e_ == 0 else [],
                                  inc=sWd[1 - wsl])
                    k.dma("gpsimd", bup[:, (n % 2) * 8:(n % 2) * 8 + 8],
                          g["L1_bupc"][n], waits=[(sS, sS.n)], inc=sW)
                    k.dma("gpsimd", bsw[:, (n % 2) * 8:(n % 2) * 8 + 8],
                          g["L1_bswc"][n], waits=[(sS, sS.n)], inc=sW)
                    bmark = sW.n
                    def up_epi(i, slot, n=n):
                        k.on("scalar", lambda e, i=i, n=n, slot=slot: e.activation(
                             h1[:, i * TAU:(i + 1) * TAU], slot[:], AF.Identity,
                             bias=bup[:, (n % 2) * 8 + i:(n % 2) * 8 + i + 1]),
                             [(sT, sT.n), (sW, bmark), (sV, sV.n)])
                        projm[id(slot)] = (sS, sS.n)
                    ring_proj([wbf["up"][n][:, i * 128:(i + 1) * 128] for i in range(8)],
                              NCHUNK, lambda e_: xtT[:, e_ * TAU:(e_ + 1) * TAU], up_epi)
                    def sw_epi(f, slot, n=n):
                        fsl = f % 2
                        k.on("scalar", lambda e, f=f, n=n, slot=slot, fsl=fsl:
                             e.activation(swsb[:, fsl * TAU:(fsl + 1) * TAU], slot[:],
                             AF.Silu, bias=bsw[:, (n % 2) * 8 + f:(n % 2) * 8 + f + 1]),
                             [(sT, sT.n), (sW, bmark), (sV, sV.n)])
                        projm[id(slot)] = (sS, sS.n)
                        k.on("vector", lambda e, f=f, fsl=fsl: e.tensor_mul(
                             h2[:, f * TAU:(f + 1) * TAU],
                             swsb[:, fsl * TAU:(fsl + 1) * TAU],
                             h1[:, f * TAU:(f + 1) * TAU]), [(sS, sS.n)])
                    ring_proj([wbf["sw"][n][:, f * 128:(f + 1) * 128] for f in range(8)],
                              8, lambda e_: h1[:, e_ * TAU:(e_ + 1) * TAU], sw_epi)
                    h2mark = (sV, sV.n)
                    for tt in range(4):
                        trel = tt * 128
                        bs = (tt % 2) * 4
                        for e_ in range(8):
                            for d4 in range(4):
                                k.on("tensor", lambda e, e_=e_, d4=d4, trel=trel,
                                     bs=bs, wsl=wsl: e.matmul(pool[bs + d4][:],
                                     h2[:, e_ * TAU + trel:e_ * TAU + trel + 128],
                                     wdn2[wsl][:, e_ * 2048 + d4 * 512:
                                               e_ * 2048 + d4 * 512 + 512],
                                     start=(e_ == 0), stop=False),
                                     ([(sWd[wsl], sWd[wsl].n), h2mark, gate_done]
                                      + dn_guards(bs)) if e_ == 0 and d4 == 0 else [])
                        for d4 in range(4):
                            k.on("tensor", lambda e, d4=d4, n=n, bs=bs: e.matmul(
                                 pool[bs + d4][:], onesb[:],
                                 bdnrow[:, n * 2048 + d4 * 512:n * 2048 + (d4 + 1) * 512],
                                 start=False, stop=True), [],
                                 inc=sT if d4 == 3 else None)
                        for d4 in range(4):
                            a = acc[:, tt * 2048 + d4 * 512:tt * 2048 + (d4 + 1) * 512]
                            wc = wcols[:, tt * 8 + n:tt * 8 + n + 1]
                            if n == 0:
                                k.on("vector", lambda e, a=a, wc=wc, d4=d4, bs=bs:
                                     e.tensor_single_scalar(a, pool[bs + d4][:], wc,
                                     ALU.mult), [(sT, sT.n)] if d4 == 0 else [])
                            else:
                                k.on("vector", lambda e, a=a, wc=wc, d4=d4, bs=bs:
                                     e.scalar_tensor_tensor(a, pool[bs + d4][:], wc, a,
                                     ALU.mult, ALU.add), [(sT, sT.n)] if d4 == 0 else [])
                        projm[("dn", bs)] = (sV, sV.n)
                    dnm[wsl] = sT.n
                def su_epi(c, slot):
                    k.on("scalar", lambda e, c=c, slot=slot: e.activation(
                         sh[:, c * TAU:(c + 1) * TAU], slot[:], AF.Identity,
                         bias=bsu[:, c:c + 1]), [(sT, sT.n), (sW, sW.n), (sV, sV.n)])
                    projm[id(slot)] = (sS, sS.n)
                ring_proj([wbf["su"][:, c * 128:(c + 1) * 128] for c in range(NCHUNK)],
                          NCHUNK, lambda e_: xtT[:, e_ * TAU:(e_ + 1) * TAU], su_epi)
                for s2 in range(2):
                    def sa_epi(f, slot, s2=s2):
                        fsl = f % 2
                        k.on("scalar", lambda e, f=f, slot=slot, fsl=fsl: e.activation(
                             swsb[:, fsl * TAU:(fsl + 1) * TAU], slot[:], AF.Silu,
                             bias=bsa[:, f:f + 1]), [(sT, sT.n), (sV, sV.n), (sW, sW.n)])
                        projm[id(slot)] = (sS, sS.n)
                        fi = s2 * 8 + f
                        dstb = h1 if fi < 8 else h2
                        di = fi % 8
                        k.on("vector", lambda e, fi=fi, dstb=dstb, di=di, fsl=fsl:
                             e.tensor_mul(dstb[:, di * TAU:(di + 1) * TAU],
                             swsb[:, fsl * TAU:(fsl + 1) * TAU],
                             sh[:, fi * TAU:(fi + 1) * TAU]), [(sS, sS.n)])
                    ring_proj([wbf["sa"][:, f * 128:(f + 1) * 128] for f in range(8)],
                              8, lambda e_, s2=s2: sh[:, (s2 * 8 + e_) * TAU:
                                                      (s2 * 8 + e_ + 1) * TAU], sa_epi)
                sh2mark = (sV, sV.n)
                for hh in range(2):
                    for e_ in range(8):
                        k.dma("sync", wdn2[hh][:, e_ * 2048:(e_ + 1) * 2048],
                              wbf["sd"][(hh * 8 + e_) * 128:(hh * 8 + e_ + 1) * 128, :],
                              waits=[(sT, dnm[hh])] if e_ == 0 else [], inc=sWd[hh])
                for tt in range(4):
                    trel = tt * 128
                    bs = (tt % 2) * 4
                    for hh in range(2):
                        for e_ in range(8):
                            su = hh * 8 + e_
                            srcb = h1 if su < 8 else h2
                            si = su % 8
                            for d4 in range(4):
                                k.on("tensor", lambda e, si=si, srcb=srcb, e_=e_,
                                     d4=d4, trel=trel, bs=bs, hh=hh:
                                     e.matmul(pool[bs + d4][:],
                                     srcb[:, si * TAU + trel:si * TAU + trel + 128],
                                     wdn2[hh][:, e_ * 2048 + d4 * 512:
                                               e_ * 2048 + d4 * 512 + 512],
                                     start=(hh == 0 and e_ == 0), stop=False),
                                     ([(sWd[0], sWd[0].n), (sWd[1], sWd[1].n),
                                       sh2mark] + dn_guards(bs))
                                     if hh == 0 and e_ == 0 and d4 == 0 else [])
                    for d4 in range(4):
                        k.on("tensor", lambda e, d4=d4, bs=bs: e.matmul(
                             pool[bs + d4][:], onesb[:],
                             bsdrow[:, d4 * 512:(d4 + 1) * 512],
                             start=False, stop=True), [],
                             inc=sT if d4 == 3 else None)
                    for d4 in range(4):
                        a = acc[:, tt * 2048 + d4 * 512:tt * 2048 + (d4 + 1) * 512]
                        k.on("vector", lambda e, a=a, d4=d4, bs=bs:
                             e.tensor_tensor(a, a, pool[bs + d4][:], ALU.add),
                             [(sT, sT.n)] if d4 == 0 else [])
                    projm[("dn", bs)] = (sV, sV.n)
                dnm = [sT.n, sT.n]
                for tt in range(4):
                    trel = tt * 128
                    par = tt % 2
                    k.dma("sync", res[:], xn1_f32[t0 + trel:t0 + trel + 128, :],
                          waits=[(sV, sV.n)], inc=sR)
                    k.on("vector", lambda e, tt=tt, par=par: e.tensor_add(
                         stg[:, par * 2048:(par + 1) * 2048],
                         acc[:, tt * 2048:(tt + 1) * 2048], res[:]),
                         [(sR, sR.n), (sOp[par], sOp[par].n)])
                    k.dma("sync", xout[t0 + trel:t0 + trel + 128, :],
                          stg[:, par * 2048:(par + 1) * 2048],
                          waits=[(sV, sV.n)], inc=sOp[par])
            k.barrier()
            projm.clear()
        k.barrier()


def precast_moe_weights(k, g, nc):
    sPre = k.sem("mo_pre")
    wbf = {"up": [], "sw": [], "dn": []}
    for n in range(6):
        for nm, shp in [("up", (2048, 1024)), ("sw", (1024, 1024)),
                        ("dn", (1024, 2048))]:
            t = dram(nc, f"mo_{nm}{n}_bf", list(shp), BF16)
            src = {"up": "L1_Wup", "sw": "L1_Wsw", "dn": "L1_Wdn"}[nm]
            for r0 in range(0, shp[0], 512):
                k.dma("gpsimd", t[r0:r0 + 512, :], g[src][n][r0:r0 + 512, :],
                      inc=sPre)
            wbf[nm].append(t)
    for nm, src, shp in [("su", "L1_Wsu", (2048, 2048)), ("sa", "L1_Wsa", (1024, 1024)),
                         ("sd", "L1_Wsd", (2048, 2048))]:
        t = dram(nc, f"mo_{nm}_bf", list(shp), BF16)
        for r0 in range(0, shp[0], 512):
            k.dma("gpsimd", t[r0:r0 + 512, :], g[src][r0:r0 + 512, :], inc=sPre)
        wbf[nm] = t
    return wbf


MOE_KEYS = ["L1_Wg", "L1_bg_row", "L1_Wup", "L1_bupc", "L1_Wsw", "L1_bswc",
            "L1_Wdn", "L1_bdn_row", "L1_Wsu", "L1_bsuc", "L1_Wsa", "L1_bsac",
            "L1_Wsd", "L1_bsd_row", "ones_bf"]


def build_l1_moe(w, debug=False):
    nc = bass.Bass("TRN2", target_bir_lowering=False)
    g = declare_weight_inputs(nc, {kk: w[kk] for kk in MOE_KEYS})
    xn1 = dram(nc, "xn1", [T, D], BF16, "ExternalInput")
    xn1f = dram(nc, "xn1f", [T, D], F32, "ExternalInput")
    f1 = dram(nc, "ffn1_tok", [T, D], BF16, "ExternalOutput")
    xt1 = dram(nc, "xt1_tok", [T, D], BF16)
    k = K(nc)
    if debug:
        k._mo_dbg = {
            "wcols": dram(nc, "dbg_wcols", [128, 32], F32, "ExternalOutput"),
            "h1": dram(nc, "dbg_h1", [128, 8 * 512], BF16, "ExternalOutput"),
            "h2": dram(nc, "dbg_h2", [128, 8 * 512], BF16, "ExternalOutput"),
            "acc1": dram(nc, "dbg_acc1", [128, 4 * 2048], F32, "ExternalOutput"),
            "acc2": dram(nc, "dbg_acc2", [128, 4 * 2048], F32, "ExternalOutput"),
            "sh": dram(nc, "dbg_sh", [128, 16 * 512], BF16, "ExternalOutput"),
        }
    wbf = precast_moe_weights(k, g, nc)
    k.barrier()
    emit_l1_moe(k, g, wbf, xn1, xn1f, xt1, f1)
    k.finalize()
    return nc


ALL_WKEYS = (_L0A_KEYS + _L0F_KEYS + _L1A_KEYS + MOE_KEYS)


def build_fused(w, ret_k=False):
    nc = bass.Bass("TRN2", target_bir_lowering=False)
    keys = sorted(set(ALL_WKEYS))
    g = declare_weight_inputs(nc, {kk: w[kk] for kk in keys})
    x0 = dram(nc, "x0", [T, D], F32, "ExternalInput")
    out = dram(nc, "out", [T, D], F32, "ExternalOutput")
    xn0 = dram(nc, "s_xn0", [T, D], BF16)
    a0 = dram(nc, "s_a0fm", [D, T], BF16)
    f0 = dram(nc, "s_f0fm", [D, T], BF16)
    x1 = dram(nc, "s_x1", [T, D], F32)
    xn1 = dram(nc, "s_xn1", [T, D], BF16)
    xn1f = dram(nc, "s_xn1f", [T, D], F32)
    a1 = dram(nc, "s_a1", [T, D], BF16)
    f1 = dram(nc, "s_f1", [T, D], BF16)
    xt1 = dram(nc, "s_xt1", [T, D], BF16)
    kend_dram = dram(nc, "s_kend", [D, T], BF16)
    v_dram = dram(nc, "s_v", [D, T], BF16)
    qk_dram = dram(nc, "s_qk", [2 * D, T], BF16)
    v1_dram = dram(nc, "s_v1", [T, D], BF16)

    k = K(nc)
    sPre = k.sem("precast")
    wbf0 = {}
    for short, nm in [("q", "L0_WqT"), ("k", "L0_WkT"), ("v", "L0_WvT"),
                      ("g", "L0_WgT"), ("o", "L0_WoT")]:
        wbf0[short] = dram(nc, nm + "_bf", [D, D], BF16)
        for r0 in range(0, D, 256):
            k.dma("gpsimd", wbf0[short][r0:r0 + 256, :], g[nm][r0:r0 + 256, :],
                  inc=sPre)
    wbf1 = precast_moe_weights(k, g, nc)
    wbf_f = precast_ffn_weights(k, g, nc)
    k.mark("ln0")
    emit_combine_ln(k, g, x0, [], None, xn0, None)
    k.barrier()
    k.mark("rwkv")
    emit_l0_attn(k, g, xn0, kend_dram, v_dram, a0, wbf0)
    k.mark("ffn0")
    with ExitStack() as st3:
        xnT = st3.enter_context(nc.sbuf_tensor(k._uname("fz_xnT"),
                                               [128, NCHUNK * T], BF16))
        sL = k.sem("fz_ldfm")
        emit_load_fm(k, xnT, xn0, sL)
        k.barrier()
        emit_l0_ffn(k, g, xnT, f0, wbf_f)
    k.mark("comb0")
    emit_combine_ln(k, g, x0, [(a0, "fm_bf"), (f0, "fm_bf")], x1, xn1, xn1f)
    k.mark("mha")
    emit_l1_attn(k, g, xn1, qk_dram, v1_dram, a1)
    k.mark("moe")
    emit_l1_moe(k, g, wbf1, xn1, xn1f, xt1, f1)
    k.mark("comb1")
    emit_combine_ln(k, g, x1, [(a1, "tok_bf"), (f1, "tok_bf")], out, None, None)
    k.mark("end")
    k.finalize()
    if ret_k:
        return nc, k
    return nc


def kernel_unfused(**inputs):
    return _kernel_multi(**inputs)


def kernel(**inputs):
    NC = 8
    cores = list(range(NC))
    w = prep_weights(inputs)
    x_all = np.ascontiguousarray(np.asarray(inputs["L0_attn_hidden_states"],
                                            dtype=np.float32))
    keys = sorted(set(ALL_WKEYS))
    wrep = {kk: w[kk] for kk in keys}
    nc = build_fused(w)
    in_maps = [dict(x0=x_all[c], **wrep) for c in cores]
    res = run_bass_kernel_spmd(nc, in_maps, core_ids=cores)
    return np.stack([np.asarray(res.results[c]["out"], dtype=np.float32)
                     for c in cores])


# version 04 (final)
# speedup vs baseline: 1.4401x; 1.0482x over previous
import math
from contextlib import ExitStack

import numpy as np
import ml_dtypes
import concourse.bass as bass
import concourse.mybir as mybir
from concourse.bass_utils import run_bass_kernel_spmd

F32 = mybir.dt.float32
BF16 = mybir.dt.bfloat16
I8 = mybir.dt.int8
AF = mybir.ActivationFunctionType
ALU = mybir.AluOpType
AX = mybir.AxisListType

T = 2048
D = 2048
NCHUNK = D // 128
NTILE = T // 128
EPS = 1e-5
ENGINES = ("sync", "gpsimd", "vector", "scalar", "tensor")


class Sem:
    def __init__(self, h, name):
        self.h = h
        self.name = name
        self.n = 0


class K:

    def __init__(self, nc):
        self.nc = nc
        self.q = {e: [] for e in ENGINES}
        self.ctx = ExitStack()
        self._sems = []
        self.sV = self.sem("ordV")
        self.sS = self.sem("ordS")
        self.sG = self.sem("ordG")
        self.sT = self.sem("ordT")
        self._ord = {"vector": self.sV, "scalar": self.sS, "gpsimd": self.sG}

    def sem(self, name):
        name = self._uname(name)
        h = self.ctx.enter_context(self.nc.semaphore(name))
        s = Sem(h, name)
        self._sems.append(s)
        return s

    def _uname(self, name):
        self._nid = getattr(self, "_nid", 0) + 1
        return f"{name}_u{self._nid}"

    def sb(self, name, shape, dtype):
        return self.ctx.enter_context(self.nc.sbuf_tensor(self._uname(name), shape,
                                                          dtype))

    def ps(self, name, shape, dtype=F32):
        return self.ctx.enter_context(self.nc.psum_tensor(self._uname(name), shape,
                                                          dtype))

    def _emit(self, eng, fn, waits, incs):
        wl = []
        for w in waits:
            if isinstance(w, Sem):
                wl.append((w.h, w.n))
            else:
                s, v = w
                if v > 0:
                    wl.append((s.h, v))
        best = {}
        for h, v in wl:
            key = id(h)
            if key not in best or best[key][1] < v:
                best[key] = (h, v)
        il = []
        for s, kk in incs:
            s.n += kk
            il.append((s.h, kk))
        self.q[eng].append((list(best.values()), il, fn))

    def on(self, eng, fn, waits=(), inc=None, inc_k=1):
        waits = list(waits)
        incs = []
        if eng in self._ord:
            o = self._ord[eng]
            waits.append((o, o.n))
            incs.append((o, 1))
            assert inc is None or inc is o, "V/S/G ops always inc their order sem"
        elif inc is not None:
            incs.append((inc, inc_k))
        self._emit(eng, fn, waits, incs)

    def dma(self, eng, out, in_, waits=(), inc=None, **kw):
        incs = [(inc, 16)] if inc is not None else []
        self._emit(eng, lambda e: e.dma_start(out=out, in_=in_, **kw),
                   list(waits), incs)

    def mark(self, label):
        if not hasattr(self, "marks"):
            self.marks = []
        self.marks.append((label, {e: len(self.q[e]) for e in ENGINES}))

    def wait_all(self, eng):
        wl = [(s.h, s.n) for s in self._sems if s.n > 0]
        self.q[eng].append((wl, [], None))

    def barrier(self):
        for e in ENGINES:
            self.wait_all(e)

    def finalize(self):
        nc = self.nc
        with nc.Block() as blk:
            def runner(name):
                def go(e):
                    for wl, il, fn in self.q[name]:
                        for h, v in wl:
                            if v > 0:
                                e.wait_ge(h, v)
                        if fn is None:
                            continue
                        inst = fn(e)
                        for h, kk in il:
                            inst = inst.then_inc(h, kk)
                return go
            blk.sync(runner("sync"))
            blk.gpsimd(runner("gpsimd"))
            blk.vector(runner("vector"))
            blk.scalar(runner("scalar"))
            blk.tensor(runner("tensor"))
        self.ctx.close()


def _colmajor(v):
    v = np.asarray(v).reshape(-1)
    return np.ascontiguousarray(v.reshape(-1, 128).T)


def prep_weights(i):
    w = {}
    w["ident_bf"] = np.eye(128, dtype=ml_dtypes.bfloat16)
    w["ident_f32"] = np.eye(128, dtype=np.float32)
    w["ones_bf"] = np.ones((1, 128), dtype=ml_dtypes.bfloat16)
    w["ones_f32"] = np.ones((1, 128), dtype=np.float32)
    w["ones_colbf"] = np.ones((128, 1), dtype=ml_dtypes.bfloat16)
    w["maskU"] = np.triu(np.ones((128, 128))).astype(ml_dtypes.bfloat16)
    for o in range(4):
        kk = np.arange(128)[:, None] + o * 128
        qq = np.arange(512)[None, :]
        w[f"maskD{o}"] = (kk <= qq).astype(ml_dtypes.bfloat16)

    for nm in ["Wq", "Wk", "Wv", "Wg", "Wo"]:
        w[f"L0_{nm}T"] = np.ascontiguousarray(np.asarray(i[f"L0_attn_{nm}"]).T)
    w["L0_maa_w1"] = np.asarray(i["L0_attn_maa_w1"])
    w["L0_maa_w2"] = np.asarray(i["L0_attn_maa_w2"])
    w2p = np.zeros((5, 64, 2048), np.float32)
    mw2 = np.asarray(i["L0_attn_maa_w2"])
    for f in range(4):
        off = 32 * (f % 2)
        w2p[f, off:off + 32, :] = mw2[f]
    w2p[4, 0:32, :] = mw2[4]
    w["L0_w2p"] = w2p
    w["L0_decay_w1"] = np.asarray(i["L0_attn_decay_w1"])
    w["L0_decay_w2"] = np.asarray(i["L0_attn_decay_w2"])
    for nm in ["maa_x", "maa_r", "maa_k", "maa_v", "maa_w", "maa_g", "time_decay"]:
        w[f"L0_{nm}c"] = _colmajor(i[f"L0_attn_{nm}"])
    w["L0_bgc"] = _colmajor(i["L0_attn_bg"])
    w["L0_w_gate"] = np.asarray(i["L0_ffn_w_gate"])
    w["L0_w_up"] = np.asarray(i["L0_ffn_w_up"])
    w["L0_w_down"] = np.asarray(i["L0_ffn_w_down"])
    w["L0_sgc"] = _colmajor(i["L0_ffn_s_gate"])
    w["L0_suc"] = _colmajor(i["L0_ffn_s_up"])
    w["L0_sdc"] = _colmajor(i["L0_ffn_s_down"])
    w["L1_WQ"] = np.asarray(i["L1_attn_W_Q"])
    w["L1_WK"] = np.asarray(i["L1_attn_W_K"])
    w["L1_WV_flat"] = np.ascontiguousarray(
        np.asarray(i["L1_attn_W_V"]).transpose(1, 0, 2).reshape(2048, 2048))
    w["L1_WO"] = np.asarray(i["L1_attn_W_O"])
    w["L1_bQc"] = np.ascontiguousarray(np.asarray(i["L1_attn_b_Q"]).T)
    w["L1_bKc"] = np.ascontiguousarray(np.asarray(i["L1_attn_b_K"]).T)
    w["L1_bV_row"] = np.asarray(i["L1_attn_b_V"]).reshape(1, 2048)
    w["L1_bO_row"] = np.asarray(i["L1_attn_b_O"]).reshape(1, 2048)
    w["L1_Wg"] = np.asarray(i["L1_ffn_Wg"])
    w["L1_bg_row"] = np.ascontiguousarray(np.asarray(i["L1_ffn_bg"]).reshape(2, 1, 8))
    w["L1_Wup"] = np.asarray(i["L1_ffn_Wup"])
    w["L1_bupc"] = np.ascontiguousarray(
        np.asarray(i["L1_ffn_bup"]).reshape(6, 8, 128).transpose(0, 2, 1))
    w["L1_Wsw"] = np.asarray(i["L1_ffn_Wsw"])
    w["L1_bswc"] = np.ascontiguousarray(
        np.asarray(i["L1_ffn_bsw"]).reshape(6, 8, 128).transpose(0, 2, 1))
    w["L1_Wdn"] = np.asarray(i["L1_ffn_Wdn"])
    w["L1_bdn_row"] = np.ascontiguousarray(np.asarray(i["L1_ffn_bdn"]).reshape(6, 1, 2048))
    w["L1_Wsu"] = np.asarray(i["L1_ffn_Wsu"])
    w["L1_bsuc"] = _colmajor(i["L1_ffn_bsu"])
    w["L1_Wsa"] = np.asarray(i["L1_ffn_Wsa"])
    w["L1_bsac"] = _colmajor(i["L1_ffn_bsa"])
    w["L1_Wsd"] = np.asarray(i["L1_ffn_Wsd"])
    w["L1_bsd_row"] = np.asarray(i["L1_ffn_bsd"]).reshape(1, 2048)
    return {k: np.ascontiguousarray(v) for k, v in w.items()}


def declare_weight_inputs(nc, w):
    dt_map = {np.dtype(np.float32): F32, np.dtype(ml_dtypes.bfloat16): BF16,
              np.dtype(np.int8): I8}
    aps = {}
    for kk, v in w.items():
        aps[kk] = nc.dram_tensor(kk, list(v.shape), dt_map[v.dtype],
                                 kind="ExternalInput").ap()
    return aps


def emit_combine_ln(k, g, x_in_f32, adds, x_out_f32, xn_tok_bf, xn_f32, out_is_final=False):
    sX = k.sem("cl_x")
    sV = k.sV
    sS = k.sS
    sO = k.sem("cl_o")
    with ExitStack() as st:
        sb = lambda n, s, d: st.enter_context(k.nc.sbuf_tensor(k._uname(n), s, d))
        xb = sb("cl_xb", [128, 2 * D], F32)
        ab = sb("cl_ab", [128, 2 * D], BF16)
        fb = sb("cl_fb", [128, 2 * D], BF16)
        sq = sb("cl_sq", [128, D], F32)
        x1 = sb("cl_x1", [128, 2 * D], F32)
        xn = sb("cl_xn", [128, D], BF16)
        xnf = sb("cl_xnf", [128, D], F32)
        cols = sb("cl_cols", [128, 8], F32)

        for t in range(NTILE):
            db = t % 2
            xs = xb[:, db * D:(db + 1) * D]
            x1s = x1[:, db * D:(db + 1) * D]
            war = [Sem.__new__(Sem)]
            war = ([(sV, sV.n), (sO, sO.n), (sS, sS.n)] if t >= 2 else [])
            k.dma("sync", xs, x_in_f32[t * 128:(t + 1) * 128, :], waits=war, inc=sX)
            srcs = []
            for idx, (ap, kind) in enumerate(adds):
                buf = (ab if idx == 0 else fb)[:, db * D:(db + 1) * D]
                if kind == "tok_bf":
                    k.dma("sync", buf, ap[t * 128:(t + 1) * 128, :], waits=war, inc=sX)
                else:
                    k.dma("sync", buf, ap[:, t * 128:(t + 1) * 128], waits=war, inc=sX,
                          transpose=True)
                srcs.append(buf)
            dmaw = [(sX, sX.n), (sO, sO.n), (sS, sS.n)]
            if not srcs:
                k.on("vector", lambda e, a=xs, o=x1s: e.tensor_copy(o, a), dmaw, sV)
            else:
                k.on("vector", lambda e, a=xs, b=srcs[0], o=x1s:
                     e.tensor_tensor(o, a, b, ALU.add), dmaw, sV)
                for b2 in srcs[1:]:
                    k.on("vector", lambda e, a=x1s, b=b2, o=x1s:
                         e.tensor_tensor(o, a, b, ALU.add), [], sV)
            if x_out_f32 is not None:
                k.dma("sync", x_out_f32[t * 128:(t + 1) * 128, :], x1s,
                      waits=[(sV, sV.n)], inc=sO)
            if xn_tok_bf is None and xn_f32 is None:
                continue
            k.on("scalar", lambda e, a=x1s: e.activation(sq[:], a, AF.Square,
                 accum_out=cols[:, 1:2]), [(sV, sV.n)], sS)
            k.on("vector", lambda e, a=x1s: e.tensor_reduce(cols[:, 0:1], a, AX.X,
                 ALU.add), [], sV)
            k.on("vector", lambda e: e.tensor_scalar_mul(cols[:, 2:3], cols[:, 0:1],
                 1.0 / D), [], sV)
            k.on("vector", lambda e: e.tensor_tensor(cols[:, 3:4], cols[:, 2:3],
                 cols[:, 2:3], ALU.mult), [], sV)
            k.on("vector", lambda e: e.tensor_scalar_mul(cols[:, 4:5], cols[:, 1:2],
                 1.0 / D), [(sS, sS.n)], sV)
            k.on("vector", lambda e: e.tensor_tensor(cols[:, 4:5], cols[:, 4:5],
                 cols[:, 3:4], ALU.subtract), [], sV)
            k.on("vector", lambda e: e.tensor_scalar_add(cols[:, 4:5], cols[:, 4:5],
                 float(EPS)), [], sV)
            k.on("scalar", lambda e: e.activation(cols[:, 5:6], cols[:, 4:5], AF.Sqrt),
                 [(sV, sV.n)], sS)
            k.on("vector", lambda e: e.reciprocal(cols[:, 6:7], cols[:, 5:6]),
                 [(sS, sS.n)], sV)
            if xn_tok_bf is not None:
                k.on("vector", lambda e, a=x1s: e.tensor_scalar(xn[:], a, cols[:, 2:3],
                     cols[:, 6:7], ALU.subtract, ALU.mult), [], sV)
                k.dma("sync", xn_tok_bf[t * 128:(t + 1) * 128, :], xn[:],
                      waits=[(sV, sV.n)], inc=sO)
            if xn_f32 is not None:
                k.on("vector", lambda e, a=x1s: e.tensor_scalar(xnf[:], a, cols[:, 2:3],
                     cols[:, 6:7], ALU.subtract, ALU.mult), [], sV)
                k.dma("sync", xn_f32[t * 128:(t + 1) * 128, :], xnf[:],
                      waits=[(sV, sV.n)], inc=sO)
        k.barrier()


def emit_load_fm(k, xnT, xn_tok_bf, sem, waits=()):
    for c in range(NCHUNK):
        k.dma("sync", xnT[:, c * T:(c + 1) * T],
              xn_tok_bf[:, c * 128:(c + 1) * 128], waits=waits, inc=sem,
              transpose=True)


def emit_l0_ffn(k, g, xnT, f0_fm, wbf=None):
    I = 8192
    NI = I // 128
    TG = 512
    sW = k.sem("ff_w")
    sWp = [k.sem(f"ff_wp{i}") for i in range(4)]
    sWd = [k.sem(f"ff_wd{i}") for i in range(4)]
    sOp = [k.sem("ff_op0"), k.sem("ff_op1")]
    sT = k.sT
    sV = k.sV
    sS = k.sS
    with ExitStack() as st:
        sb = lambda n, s, d: st.enter_context(k.nc.sbuf_tensor(k._uname(n), s, d))
        ps = lambda n, s: st.enter_context(k.nc.psum_tensor(k._uname(n), s, F32))
        wg = sb("ff_wg", [128, 4 * NCHUNK * 128], BF16)
        wu = sb("ff_wu", [128, 4 * NCHUNK * 128], BF16)
        wd = sb("ff_wd", [128, 4 * D], BF16)
        fuse = sb("ff_fuse", [128, NI * TG], BF16)
        gsb = sb("ff_gsb", [128, 2 * TG], BF16)
        outb = sb("ff_outb", [128, 8 * TG], BF16)
        scg = sb("ff_scg", [128, 64], F32)
        scu = sb("ff_scu", [128, 64], F32)
        scd = sb("ff_scd", [128, 16], F32)
        pool8 = [ps(f"ff_pb{j}", [128, TG]) for j in range(8)]
        psg = [pool8[0], pool8[1]]
        psu = [pool8[2], pool8[3]]

        k.dma("gpsimd", scg[:], g["L0_sgc"], inc=sW)
        k.dma("gpsimd", scu[:], g["L0_suc"], inc=sW)
        k.dma("gpsimd", scd[:], g["L0_sdc"], inc=sW)
        gum = {}
        gm = {}
        wdm = {}

        def gw(key):
            return [gm.get(key, (sT, 0))]

        def load_gu(i, sl):
            for nm, dst in ((wbf["gate"], wg), (wbf["up"], wu)):
                src = nm[:, i * 128:(i + 1) * 128].rearrange("(c p) i -> p c i", p=128)
                d3 = dst[:, sl * 2048:(sl + 1) * 2048].rearrange(
                    "p (c i) -> p c i", c=NCHUNK)
                k.dma("sync", d3, src, waits=[(sT, gum.get(sl, 0))], inc=sWp[sl])

        def load_wd(i, sl):
            k.dma("sync", wd[:, sl * D:(sl + 1) * D],
                  wbf["down"][i * 128:(i + 1) * 128, :],
                  waits=[(sT, wdm.get(sl, 0))], inc=sWd[sl])

        for tg in range(T // TG):
            t0 = tg * TG
            for pre in range(3):
                load_gu(pre, pre)
            for i in range(NI):
                sl = i % 4
                pp = i % 2
                if i + 3 < NI:
                    load_gu(i + 3, (i + 3) % 4)
                for e in range(NCHUNK):
                    k.on("tensor", lambda e_, sl=sl, pp=pp, ec=e, t0=t0: e_.matmul(
                        psg[pp][:], wg[:, sl * 2048 + ec * 128:sl * 2048 + ec * 128 + 128],
                        xnT[:, ec * T + t0:ec * T + t0 + TG],
                        start=(ec == 0), stop=(ec == NCHUNK - 1)),
                        ([(sWp[sl], sWp[sl].n), (sW, sW.n)] + gw(("b", pp)))
                        if e == 0 else [],
                        inc=sT if e == NCHUNK - 1 else None)
                for e in range(NCHUNK):
                    k.on("tensor", lambda e_, sl=sl, pp=pp, ec=e, t0=t0: e_.matmul(
                        psu[pp][:], wu[:, sl * 2048 + ec * 128:sl * 2048 + ec * 128 + 128],
                        xnT[:, ec * T + t0:ec * T + t0 + TG],
                        start=(ec == 0), stop=(ec == NCHUNK - 1)),
                        gw(("b", 2 + pp)) if e == 0 else [],
                        inc=sT if e == NCHUNK - 1 else None)
                gum[sl] = sT.n
                k.on("scalar", lambda e_, i=i, pp=pp: e_.activation(
                     gsb[:, pp * TG:(pp + 1) * TG], psg[pp][:], AF.Silu,
                     scale=scg[:, i:i + 1]), [(sT, sT.n - 1), (sV, sV.n)])
                gm[("b", pp)] = (sS, sS.n)
                k.on("vector", lambda e_, i=i, pp=pp: e_.scalar_tensor_tensor(
                     fuse[:, i * TG:(i + 1) * TG], psu[pp][:], scu[:, i:i + 1],
                     gsb[:, pp * TG:(pp + 1) * TG], ALU.mult, ALU.mult),
                     [(sT, sT.n), (sS, sS.n)])
                gm[("b", 2 + pp)] = (sV, sV.n)
            fuse_mark = sV.n
            for pre in range(3):
                load_wd(pre, pre)
            for dh in range(2):
                for i in range(NI):
                    sl4 = i % 4
                    if i + 3 < NI:
                        load_wd(i + 3, (i + 3) % 4)
                    wmark = (sWd[sl4], sWd[sl4].n)
                    for j in range(8):
                        dc = dh * 8 + j
                        k.on("tensor", lambda e_, sl4=sl4, dc=dc, i=i, j=j, t0=t0:
                             e_.matmul(pool8[j][:],
                             wd[:, sl4 * D + dc * 128:sl4 * D + dc * 128 + 128],
                             fuse[:, i * TG:(i + 1) * TG],
                             start=(i == 0), stop=(i == NI - 1)),
                             ([wmark, (sV, fuse_mark)]
                              + [w for jj in range(8) for w in gw(("b", jj))])
                             if j == 0 and i == 0 else ([wmark] if j == 0 else []),
                             inc=sT if j == 7 else None)
                    wdm[sl4] = sT.n
                if dh + 1 < 2:
                    for pre in range(3):
                        load_wd(pre, pre)
                par = dh % 2
                for j in range(8):
                    dc = dh * 8 + j
                    slot = outb[:, j * TG:(j + 1) * TG]
                    k.on("scalar", lambda e_, dc=dc, j=j, slot=slot: e_.activation(
                         slot, pool8[j][:], AF.Copy, scale=scd[:, dc:dc + 1]),
                         [(sT, sT.n), (sOp[0], sOp[0].n), (sOp[1], sOp[1].n),
                          (sW, sW.n)])
                    gm[("b", j)] = (sS, sS.n)
                    k.dma("sync", f0_fm[dc * 128:(dc + 1) * 128, t0:t0 + TG],
                          slot, waits=[(sS, sS.n)], inc=sOp[par])
        k.barrier()


def precast_ffn_weights(k, g, nc):
    sPre = k.sem("ff_pre")
    wbf = {}
    for nm, src, shp in [("gate", "L0_w_gate", (2048, 8192)),
                         ("up", "L0_w_up", (2048, 8192)),
                         ("down", "L0_w_down", (8192, 2048))]:
        t = dram(nc, f"ff_{nm}_bf", list(shp), BF16)
        for r0 in range(0, shp[0], 256):
            k.dma("gpsimd", t[r0:r0 + 256, :], g[src][r0:r0 + 256, :], inc=sPre)
        wbf[nm] = t
    return wbf


def dram(nc, name, shape, dtype, kind="Internal"):
    return nc.dram_tensor(name, list(shape), dtype, kind=kind).ap()


def build_ln0():
    nc = bass.Bass("TRN2", target_bir_lowering=False)
    x0 = dram(nc, "x0", [T, D], F32, "ExternalInput")
    xn0 = dram(nc, "xn0", [T, D], BF16, "ExternalOutput")
    k = K(nc)
    emit_combine_ln(k, None, x0, [], None, xn0, None)
    k.finalize()
    return nc


def build_l0_ffn(w, debug=False):
    nc = bass.Bass("TRN2", target_bir_lowering=False)
    g = declare_weight_inputs(nc, {kk: w[kk] for kk in
        ["L0_w_gate", "L0_w_up", "L0_w_down", "L0_sgc", "L0_suc", "L0_sdc"]})
    xn0 = dram(nc, "xn0", [T, D], BF16, "ExternalInput")
    f0 = dram(nc, "f0_fm", [D, T], BF16, "ExternalOutput")
    k = K(nc)
    wbf_f = precast_ffn_weights(k, g, nc)
    if debug:
        k._ffn_dbg = dram(nc, "fuse_dbg", [8192, 512], BF16, "ExternalOutput")
        k._ffn_dbg2 = dram(nc, "psg_dbg", [128, 512], F32, "ExternalOutput")
        k._ffn_dbg3 = dram(nc, "slab_dbg", [128, 2048], BF16, "ExternalOutput")
        k._ffn_dbg4 = dram(nc, "xnT_dbg", [128, 2048], BF16, "ExternalOutput")
        k._ffn_dbg2_sb = k.sb("dbg2sb", [128, 512], F32)
    sL = k.sem("ld_fm")
    xnT = k.sb("xnT", [128, NCHUNK * T], BF16)
    emit_load_fm(k, xnT, xn0, sL)
    k.barrier()
    emit_l0_ffn(k, g, xnT, f0, wbf_f)
    k.finalize()
    return nc


def build_combine0(w):
    nc = bass.Bass("TRN2", target_bir_lowering=False)
    x0 = dram(nc, "x0", [T, D], F32, "ExternalInput")
    a0 = dram(nc, "attn0_fm", [D, T], BF16, "ExternalInput")
    f0 = dram(nc, "f0_fm", [D, T], BF16, "ExternalInput")
    x1 = dram(nc, "x1", [T, D], F32, "ExternalOutput")
    xn1 = dram(nc, "xn1", [T, D], BF16, "ExternalOutput")
    xn1f = dram(nc, "xn1f", [T, D], F32, "ExternalOutput")
    k = K(nc)
    emit_combine_ln(k, None, x0, [(a0, "fm_bf"), (f0, "fm_bf")], x1, xn1, xn1f)
    k.finalize()
    return nc


def build_combine1():
    nc = bass.Bass("TRN2", target_bir_lowering=False)
    x1 = dram(nc, "x1", [T, D], F32, "ExternalInput")
    a1 = dram(nc, "attn1_tok", [T, D], BF16, "ExternalInput")
    f1 = dram(nc, "ffn1_tok", [T, D], BF16, "ExternalInput")
    out = dram(nc, "out", [T, D], F32, "ExternalOutput")
    k = K(nc)
    emit_combine_ln(k, None, x1, [(a1, "tok_bf"), (f1, "tok_bf")], out, None, None)
    k.finalize()
    return nc


_L0A_KEYS = ["L0_maa_w1", "L0_w2p", "L0_decay_w1", "L0_decay_w2", "maskU",
             "L0_maa_xc", "L0_maa_rc", "L0_maa_kc", "L0_maa_vc", "L0_maa_wc",
             "L0_maa_gc", "L0_time_decayc", "L0_bgc",
             "L0_WqT", "L0_WkT", "L0_WvT", "L0_WgT", "L0_WoT"]
_L0F_KEYS = ["L0_w_gate", "L0_w_up", "L0_w_down", "L0_sgc", "L0_suc", "L0_sdc"]
_L1A_KEYS = ["L1_WQ", "L1_WK", "L1_WV_flat", "L1_WO", "L1_bQc", "L1_bKc",
             "L1_bV_row", "L1_bO_row", "ones_bf", "ones_f32", "ones_colbf",
             "maskD0", "maskD1", "maskD2", "maskD3"]


def _kernel_multi(**inputs):
    NC = 8
    cores = list(range(NC))
    w = prep_weights(inputs)
    x_all = np.ascontiguousarray(np.asarray(inputs["L0_attn_hidden_states"],
                                            dtype=np.float32))

    def launch(nc, per_core_fn):
        res = run_bass_kernel_spmd(nc, [per_core_fn(c) for c in cores],
                                   core_ids=cores)
        return res.results

    r = launch(build_ln0(), lambda c: dict(x0=x_all[c]))
    xn0 = [r[c]["xn0"] for c in cores]
    wa = {kk: w[kk] for kk in _L0A_KEYS}
    r = launch(build_l0_attn(w), lambda c: dict(xn0=xn0[c], **wa))
    a0 = [r[c]["attn0_fm"] for c in cores]
    wf = {kk: w[kk] for kk in _L0F_KEYS}
    r = launch(build_l0_ffn(w), lambda c: dict(xn0=xn0[c], **wf))
    f0 = [r[c]["f0_fm"] for c in cores]
    r = launch(build_combine0(w), lambda c: dict(x0=x_all[c], attn0_fm=a0[c],
                                                 f0_fm=f0[c]))
    x1 = [r[c]["x1"] for c in cores]
    xn1 = [r[c]["xn1"] for c in cores]
    xn1f = [r[c]["xn1f"] for c in cores]
    wm = {kk: w[kk] for kk in _L1A_KEYS}
    r = launch(build_l1_attn(w), lambda c: dict(xn1=xn1[c], **wm))
    a1 = [r[c]["attn1_tok"] for c in cores]
    wo = {kk: w[kk] for kk in MOE_KEYS}
    r = launch(build_l1_moe(w), lambda c: dict(xn1=xn1[c], xn1f=xn1f[c], **wo))
    f1 = [r[c]["ffn1_tok"] for c in cores]
    r = launch(build_combine1(), lambda c: dict(x1=x1[c], attn1_tok=a1[c],
                                                ffn1_tok=f1[c]))
    out = np.stack([np.asarray(r[c]["out"], dtype=np.float32) for c in cores])
    return out


def emit_l0_attn(k, g, xn_tok, kend_dram, v_dram, attn0_fm, wbf, phases=99, ntau=None):
    TAU = 512
    NTAU = T // TAU
    sX = k.sem("ra_x")
    sW = k.sem("ra_w")
    sWs = [k.sem("ra_ws0"), k.sem("ra_ws1")]
    sKVp = [k.sem("ra_kv0"), k.sem("ra_kv1")]
    sAOp = [k.sem("ra_ao0"), k.sem("ra_ao1")]
    sT, sV, sS = k.sT, k.sV, k.sS
    with ExitStack() as st:
        sb = lambda n, s, d: st.enter_context(k.nc.sbuf_tensor(k._uname(n), s, d))
        ps = lambda n, s: st.enter_context(k.nc.psum_tensor(k._uname(n), s, F32))
        xnT = sb("ra_xnT", [128, NCHUNK * TAU], BF16)
        xnprev = sb("ra_xnprev", [128, NCHUNK], BF16)
        S = sb("ra_S", [128, NCHUNK * 64], F32)
        Sbf = sb("ra_Sbf", [128, NCHUNK * 64], BF16)
        expL = sb("ra_expL", [128, NCHUNK * TAU], BF16)
        qd = sb("ra_qd", [128, NCHUNK * TAU], BF16)
        kd = sb("ra_kd", [128, NCHUNK * TAU], BF16)
        gF = sb("ra_gF", [128, NCHUNK * TAU], BF16)
        xf = sb("ra_xf", [128, NCHUNK * TAU], BF16)
        work = sb("ra_work", [128, 8192], BF16)
        yg = sb("ra_yg", [128, NCHUNK * TAU], BF16)
        t501 = sb("ra_t501", [64, TAU], BF16)
        t523 = sb("ra_t523", [64, TAU], BF16)
        t54 = sb("ra_t54", [32, TAU], BF16)
        d1 = sb("ra_d1", [64, TAU], BF16)
        w1sb = sb("ra_w1sb", [128, NCHUNK * 160], BF16)
        w2sb = sb("ra_w2sb", [64, 5 * 2048], BF16)
        dw1sb = sb("ra_dw1sb", [128, NCHUNK * 64], BF16)
        dw2sb = sb("ra_dw2sb", [64, 2048], BF16)
        cvec = sb("ra_cvec", [128, 8 * NCHUNK], F32)
        wslab = sb("ra_wslab", [128, 2 * 2048], BF16)
        e1b = sb("ra_e1b", [128, TAU], F32)
        cumE = sb("ra_cumE", [128, TAU], F32)
        tmpR = sb("ra_tmpR", [128, TAU], F32)
        elogw = sb("ra_elogw", [128, TAU], F32)
        onem = sb("ra_onem", [128, TAU], BF16)
        expmL = sb("ra_expmL", [128, TAU], BF16)
        expR = sb("ra_expR", [128, TAU], BF16)
        k2b = sb("ra_k2b", [128, TAU], BF16)
        stg = sb("ra_stg", [128, 2 * TAU], BF16)
        mmf = sb("ra_mmf", [128, 2 * TAU], BF16)
        expLcs = sb("ra_expLcs", [128, 64], F32)
        Asb = sb("ra_Asb", [128, 256], BF16)
        masku = sb("ra_masku", [128, 128], BF16)
        outb = sb("ra_outb", [128, 2 * TAU], BF16)
        zeros = sb("ra_zeros", [128, 128], F32)
        pool = [ps(f"ra_p{j}", [128, TAU]) for j in range(4)]
        psAe = ps("ra_psAe", [128, 512])
        psAo = ps("ra_psAo", [128, 512])
        psY = ps("ra_psY", [128, 128])
        psDS = ps("ra_psDS", [128, 64])

        k._ra_gm = {}
        MAA = {"r": 1, "k": 2, "v": 3, "w": 4, "g": 5}
        T5F = {"r": 0, "k": 1, "v": 2, "w": 3, "g": 4}
        k.dma("gpsimd", w1sb[:].rearrange("p (c m) -> p c m", c=NCHUNK),
              g["L0_maa_w1"].rearrange("(c p) m -> p c m", p=128), inc=sW)
        for f in range(5):
            k.dma("gpsimd", w2sb[:, f * 2048:(f + 1) * 2048], g["L0_w2p"][f], inc=sW)
        k.dma("gpsimd", dw1sb[:].rearrange("p (c m) -> p c m", c=NCHUNK),
              g["L0_decay_w1"].rearrange("(c p) m -> p c m", p=128), inc=sW)
        k.dma("gpsimd", dw2sb[:], g["L0_decay_w2"], inc=sW)
        for j, nm in enumerate(["maa_x", "maa_r", "maa_k", "maa_v", "maa_w", "maa_g",
                                "time_decay"]):
            k.dma("gpsimd", cvec[:, j * NCHUNK:(j + 1) * NCHUNK], g[f"L0_{nm}c"], inc=sW)
        k.dma("gpsimd", cvec[:, 7 * NCHUNK:8 * NCHUNK], g["L0_bgc"], inc=sW)
        k.dma("gpsimd", masku[:], g["maskU"], inc=sW)
        k.on("vector", lambda e: e.memset(zeros[:], 0.0))
        k.on("vector", lambda e: e.memset(S[:], 0.0))
        k.on("vector", lambda e: e.memset(Sbf[:], 0.0))
        k.on("vector", lambda e: e.memset(xnprev[:], 0.0))
        wpre = sW.n

        def slabload(dst_slot, srcap, waits):
            d3 = wslab[:, dst_slot * 2048:(dst_slot + 1) * 2048].rearrange(
                "p (c i) -> p c i", c=NCHUNK)
            s3 = srcap.rearrange("(c p) i -> p c i", p=128)
            k.dma("gpsimd", d3, s3, waits=waits, inc=sWs[dst_slot])

        for tg in range(ntau if ntau is not None else NTAU):
            t0 = tg * TAU
            if tg > 0:
                k.on("vector", lambda e: e.tensor_copy(
                    xnprev[:], xnT[:, TAU - 1::TAU]))
            xw_mark = (sV, sV.n)
            for c in range(NCHUNK):
                k.dma("sync", xnT[:, c * TAU:(c + 1) * TAU],
                      xn_tok[t0:t0 + TAU, c * 128:(c + 1) * 128],
                      waits=[xw_mark, (sT, sT.n)], inc=sX, transpose=True)
            xn_mark = (sX, sX.n)
            dx = work[:, 0:NCHUNK * TAU]
            for c in range(NCHUNK):
                k.on("vector", lambda e, c=c: e.tensor_sub(
                    dx[:, c * TAU + 1:(c + 1) * TAU],
                    xnT[:, c * TAU:(c + 1) * TAU - 1],
                    xnT[:, c * TAU + 1:(c + 1) * TAU]),
                    [xn_mark, (sT, sT.n)])
                k.on("vector", lambda e, c=c: e.tensor_sub(
                    dx[:, c * TAU:c * TAU + 1], xnprev[:, c:c + 1],
                    xnT[:, c * TAU:c * TAU + 1]), [])

            def mix(f):
                if f == "x":
                    for c in range(NCHUNK):
                        k.on("vector", lambda e, c=c: e.scalar_tensor_tensor(
                            xf[:, c * TAU:(c + 1) * TAU], dx[:, c * TAU:(c + 1) * TAU],
                            cvec[:, 0 * NCHUNK + c:0 * NCHUNK + c + 1],
                            xnT[:, c * TAU:(c + 1) * TAU], ALU.mult, ALU.add),
                            [(sW, wpre)])
                    k._ra_xfmark = (sV, sV.n)
                    return
                fi = T5F[f]
                rhs = {0: t501, 1: t501, 2: t523, 3: t523, 4: t54}[fi]
                kdim = 32 if fi == 4 else 64
                gm = k._ra_gm
                for c in range(NCHUNK):
                    par = c % 2
                    slot = pool[2 + par]
                    k.on("tensor", lambda e, c=c, fi=fi, rhs=rhs, kdim=kdim, slot=slot:
                         e.matmul(slot[:], w2sb[0:kdim, fi * 2048 + c * 128:
                                                fi * 2048 + c * 128 + 128],
                                  rhs[0:kdim, :], start=True, stop=True),
                         [gm.get(("mx", par), (sT, 0)),
                          gm.get(("pj", 2 + par), (sT, 0)),
                          gm.get(("t5", 0), (sV, 0))], inc=sT)
                    k.on("scalar", lambda e, c=c, f=f, slot=slot, par=par: e.activation(
                         mmf[:, par * TAU:(par + 1) * TAU], slot[:], AF.Identity,
                         bias=cvec[:, MAA[f] * NCHUNK + c:MAA[f] * NCHUNK + c + 1]),
                         [(sT, sT.n), (sW, wpre), gm.get(("mf", par), (sV, 0))])
                    gm[("mx", par)] = (sS, sS.n)
                    k.on("vector", lambda e, c=c, par=par: e.tensor_mul(
                         xf[:, c * TAU:(c + 1) * TAU], dx[:, c * TAU:(c + 1) * TAU],
                         mmf[:, par * TAU:(par + 1) * TAU]), [(sS, sS.n)])
                    k.on("vector", lambda e, c=c: e.tensor_add(
                         xf[:, c * TAU:(c + 1) * TAU], xf[:, c * TAU:(c + 1) * TAU],
                         xnT[:, c * TAU:(c + 1) * TAU]), [])
                    gm[("mf", par)] = (sV, sV.n)
                k._ra_xfmark = (sV, sV.n)

            def proj16(c, slot, slab_slot):
                gm = k._ra_gm
                for e_ in range(NCHUNK):
                    k.on("tensor", lambda e, c=c, e_=e_, slot=slot, ss=slab_slot:
                         e.matmul(pool[slot][:],
                                  wslab[:, ss * 2048 + e_ * 128:ss * 2048 + e_ * 128 + 128],
                                  xf[:, e_ * TAU:(e_ + 1) * TAU],
                                  start=(e_ == 0), stop=(e_ == NCHUNK - 1)),
                         [(sWs[slab_slot], sWs[slab_slot].n), k._ra_xfmark,
                          gm.get(("pj", slot), (sT, 0)),
                          gm.get(("mx", slot - 2), (sT, 0))]
                         if e_ == 0 else [],
                         inc=sT if e_ == NCHUNK - 1 else None)

            mix("x")
            for e_ in range(NCHUNK):
                k.on("tensor", lambda e, e_=e_: e.matmul(
                    pool[0][:], w1sb[:, e_ * 160:e_ * 160 + 128],
                    xf[:, e_ * TAU:(e_ + 1) * TAU],
                    start=(e_ == 0), stop=(e_ == NCHUNK - 1)),
                    [(sW, wpre), (sV, sV.n), (sS, sS.n)] if e_ == 0 else [],
                    inc=sT if e_ == NCHUNK - 1 else None)
            for e_ in range(NCHUNK):
                k.on("tensor", lambda e, e_=e_: e.matmul(
                    pool[1][0:32, :], w1sb[:, e_ * 160 + 128:e_ * 160 + 160],
                    xf[:, e_ * TAU:(e_ + 1) * TAU],
                    start=(e_ == 0), stop=(e_ == NCHUNK - 1)),
                    inc=sT if e_ == NCHUNK - 1 else None)
            k.on("scalar", lambda e: e.activation(t501[:], pool[0][0:64, :], AF.Tanh),
                 [(sT, sT.n)])
            k.on("scalar", lambda e: e.activation(t523[:], pool[0][64:128, :], AF.Tanh), [])
            k.on("scalar", lambda e: e.activation(t54[:], pool[1][0:32, :], AF.Tanh), [])
            k._ra_gm[("t5", 0)] = (sS, sS.n)
            if phases < 2:
                continue
            mix("w")
            for e_ in range(NCHUNK):
                k.on("tensor", lambda e, e_=e_: e.matmul(
                    pool[0][0:64, :], dw1sb[:, e_ * 64:(e_ + 1) * 64],
                    xf[:, e_ * TAU:(e_ + 1) * TAU],
                    start=(e_ == 0), stop=(e_ == NCHUNK - 1)),
                    [(sS, sS.n), (sV, sV.n)] if e_ == 0 else [],
                    inc=sT if e_ == NCHUNK - 1 else None)
            k.on("scalar", lambda e: e.activation(d1[:], pool[0][0:64, :], AF.Tanh),
                 [(sT, sT.n)])
            if phases < 3:
                continue
            mix("k")
            slabload(0, wbf["k"][:, 0:128], [(sT, sT.n)])
            for c in range(NCHUNK):
                if c + 1 < NCHUNK:
                    slabload(1 - c % 2, wbf["k"][:, (c + 1) * 128:(c + 2) * 128],
                             [(sT, sT.n)])
                k.on("tensor", lambda e, c=c: e.matmul(
                    pool[0][:], dw2sb[:, c * 128:(c + 1) * 128], d1[:],
                    start=True, stop=True),
                    [(sS, sS.n), (sV, sV.n)], inc=sT)
                k.on("scalar", lambda e, c=c: e.activation(
                    e1b[:], pool[0][:], AF.Exp,
                    bias=cvec[:, 6 * NCHUNK + c:6 * NCHUNK + c + 1]),
                    [(sT, sT.n), (sW, wpre)])
                for q in range(4):
                    k.on("vector", lambda e, q=q: e.tensor_tensor_scan(
                        cumE[:, q * 128:(q + 1) * 128], e1b[:, q * 128:(q + 1) * 128],
                        zeros[:], 0.0, ALU.add, ALU.add), [(sS, sS.n)])
                k.on("scalar", lambda e, c=c: e.activation(
                    expLcs[:, c * 4:(c + 1) * 4],
                    cumE[:, 127:TAU:128], AF.Exp, scale=-1.0), [(sV, sV.n)])
                k.on("scalar", lambda e, c=c: e.activation(
                    expL[:, c * TAU:(c + 1) * TAU], cumE[:], AF.Exp, scale=-1.0), [])
                k.on("scalar", lambda e: e.activation(expmL[:], cumE[:], AF.Exp), [])
                for q in range(4):
                    k.on("vector", lambda e, q=q: e.tensor_single_scalar(
                        tmpR[:, q * 128:(q + 1) * 128], cumE[:, q * 128:(q + 1) * 128],
                        cumE[:, q * 128 + 127:q * 128 + 128], ALU.subtract), [(sS, sS.n)])
                k.on("scalar", lambda e: e.activation(expR[:], tmpR[:], AF.Exp),
                     [(sV, sV.n)])
                k.on("scalar", lambda e: e.activation(elogw[:], e1b[:], AF.Exp,
                     scale=-1.0), [])
                k.on("vector", lambda e: e.tensor_scalar(onem[:], elogw[:], -1.0, 1.0,
                     ALU.mult, ALU.add), [(sS, sS.n)])
                proj16(c, 2 + c % 2, c % 2)
                k.on("vector", lambda e, c=c: e.tensor_mul(k2b[:], pool[2 + c % 2][:],
                     onem[:]), [(sT, sT.n)])
                k._ra_gm[("pj", 2 + c % 2)] = (sV, sV.n)
                k.on("vector", lambda e, c=c: e.tensor_mul(kd[:, c * TAU:(c + 1) * TAU],
                     k2b[:], expmL[:]), [])
                k.on("vector", lambda e, c=c: e.tensor_mul(stg[:, (c % 2) * TAU:
                     (c % 2 + 1) * TAU], k2b[:], expR[:]),
                     [(sKVp[c % 2], sKVp[c % 2].n)])
                k.dma("sync", kend_dram[c * 128:(c + 1) * 128, t0:t0 + TAU],
                      stg[:, (c % 2) * TAU:(c % 2 + 1) * TAU],
                      waits=[(sV, sV.n)], inc=sKVp[c % 2])
            if phases < 4:
                continue
            mix("r")
            slabload(0, wbf["q"][:, 0:128], [(sT, sT.n)])
            for c in range(NCHUNK):
                if c + 1 < NCHUNK:
                    slabload(1 - c % 2, wbf["q"][:, (c + 1) * 128:(c + 2) * 128],
                             [(sT, sT.n)])
                proj16(c, 2 + c % 2, c % 2)
                k.on("vector", lambda e, c=c: e.tensor_mul(qd[:, c * TAU:(c + 1) * TAU],
                     pool[2 + c % 2][:], expL[:, c * TAU:(c + 1) * TAU]), [(sT, sT.n)])
                k._ra_gm[("pj", 2 + c % 2)] = (sV, sV.n)
            if phases < 5:
                continue
            mix("v")
            slabload(0, wbf["v"][:, 0:128], [(sT, sT.n)])
            for c in range(NCHUNK):
                if c + 1 < NCHUNK:
                    slabload(1 - c % 2, wbf["v"][:, (c + 1) * 128:(c + 2) * 128],
                             [(sT, sT.n)])
                proj16(c, 2 + c % 2, c % 2)
                k.on("vector", lambda e, c=c: e.tensor_copy(
                     stg[:, (c % 2) * TAU:(c % 2 + 1) * TAU], pool[2 + c % 2][:]),
                     [(sT, sT.n), (sKVp[c % 2], sKVp[c % 2].n)])
                k._ra_gm[("pj", 2 + c % 2)] = (sV, sV.n)
                k.dma("sync", v_dram[c * 128:(c + 1) * 128, t0:t0 + TAU],
                      stg[:, (c % 2) * TAU:(c % 2 + 1) * TAU],
                      waits=[(sV, sV.n)], inc=sKVp[c % 2])
            if phases < 6:
                continue
            mix("g")
            slabload(0, wbf["g"][:, 0:128], [(sT, sT.n)])
            for c in range(NCHUNK):
                if c + 1 < NCHUNK:
                    slabload(1 - c % 2, wbf["g"][:, (c + 1) * 128:(c + 2) * 128],
                             [(sT, sT.n)])
                proj16(c, 2 + c % 2, c % 2)
                k.on("scalar", lambda e, c=c: e.activation(
                     gF[:, c * TAU:(c + 1) * TAU], pool[2 + c % 2][:], AF.Silu,
                     bias=cvec[:, 7 * NCHUNK + c:7 * NCHUNK + c + 1]),
                     [(sT, sT.n), (sW, wpre)])
                k._ra_gm[("pj", 2 + c % 2)] = (sS, sS.n)
            if phases < 7:
                continue
            kv_mark = [(sKVp[0], sKVp[0].n), (sKVp[1], sKVp[1].n)]
            for cc in range(4):
                ket = work[:, (cc % 2) * 2048:(cc % 2) * 2048 + 2048]
                vt = work[:, 4096 + (cc % 2) * 2048:4096 + (cc % 2) * 2048 + 2048]
                k.dma("sync", ket, kend_dram[:, t0 + cc * 128:t0 + (cc + 1) * 128],
                      waits=kv_mark + [(sT, sT.n)], inc=sX, transpose=True)
                k.dma("sync", vt, v_dram[:, t0 + cc * 128:t0 + (cc + 1) * 128],
                      waits=kv_mark + [(sT, sT.n)], inc=sX, transpose=True)
                kvx = (sX, sX.n)
                for hp in range(NCHUNK):
                    cs = slice(hp * TAU + cc * 128, hp * TAU + cc * 128 + 128)
                    k.on("tensor", lambda e, cs=cs: e.matmul(
                        psAe[:, 0:128], kd[0:64, cs], qd[0:64, cs],
                        start=True, stop=True), [(sV, sV.n), kvx])
                    k.on("tensor", lambda e, cs=cs: e.matmul(
                        psAo[:, 0:128], kd[64:128, cs], qd[64:128, cs],
                        start=True, stop=True), [], inc=sT)
                    k.on("vector", lambda e: e.tensor_mul(Asb[:, 0:128], psAe[:, 0:128],
                         masku[:]), [(sT, sT.n)])
                    k.on("vector", lambda e: e.tensor_mul(Asb[:, 128:256],
                         psAo[:, 0:128], masku[:]), [])
                    k.on("tensor", lambda e, hp=hp, vt=vt: e.matmul(
                        psY[0:64, :], vt[:, hp * 128:hp * 128 + 64], Asb[:, 0:128],
                        start=True, stop=False), [(sV, sV.n)])
                    k.on("tensor", lambda e, hp=hp, cs=cs: e.matmul(
                        psY[0:64, :], Sbf[0:64, hp * 64:(hp + 1) * 64], qd[0:64, cs],
                        start=False, stop=True), [])
                    k.on("tensor", lambda e, hp=hp, vt=vt: e.matmul(
                        psY[64:128, :], vt[:, hp * 128 + 64:hp * 128 + 128],
                        Asb[:, 128:256], start=True, stop=False,
                        tile_position=(0, 64)), [])
                    k.on("tensor", lambda e, hp=hp, cs=cs: e.matmul(
                        psY[64:128, :], Sbf[64:128, hp * 64:(hp + 1) * 64],
                        qd[64:128, cs], start=False, stop=True,
                        tile_position=(64, 64)), [], inc=sT)
                    k.on("tensor", lambda e, hp=hp, ket=ket, vt=vt: e.matmul(
                        psDS[0:64, :], ket[:, hp * 128:hp * 128 + 64],
                        vt[:, hp * 128:hp * 128 + 64], start=True, stop=True), [])
                    k.on("tensor", lambda e, hp=hp, ket=ket, vt=vt: e.matmul(
                        psDS[64:128, :], ket[:, hp * 128 + 64:hp * 128 + 128],
                        vt[:, hp * 128 + 64:hp * 128 + 128], start=True, stop=True,
                        tile_position=(0, 64)), [], inc=sT)
                    k.on("vector", lambda e, hp=hp, cc=cc: e.tensor_mul(
                        yg[:, hp * TAU + cc * 128:hp * TAU + cc * 128 + 128], psY[:],
                        gF[:, hp * TAU + cc * 128:hp * TAU + cc * 128 + 128]),
                        [(sT, sT.n - 1), (sS, sS.n)])
                    k.on("vector", lambda e, hp=hp, cc=cc: e.scalar_tensor_tensor(
                        S[:, hp * 64:(hp + 1) * 64], S[:, hp * 64:(hp + 1) * 64],
                        expLcs[:, hp * 4 + cc:hp * 4 + cc + 1], psDS[:],
                        ALU.mult, ALU.add), [(sT, sT.n)])
                    k.on("vector", lambda e, hp=hp: e.tensor_copy(
                        Sbf[:, hp * 64:(hp + 1) * 64], S[:, hp * 64:(hp + 1) * 64]), [])
            if phases < 8:
                continue
            slabload(0, wbf["o"][:, 0:128], [(sT, sT.n)])
            for c in range(NCHUNK):
                if c + 1 < NCHUNK:
                    slabload(1 - c % 2, wbf["o"][:, (c + 1) * 128:(c + 2) * 128],
                             [(sT, sT.n)])
                slot = 2 + c % 2
                for e_ in range(NCHUNK):
                    k.on("tensor", lambda e, c=c, e_=e_, slot=slot: e.matmul(
                         pool[slot][:],
                         wslab[:, (c % 2) * 2048 + e_ * 128:(c % 2) * 2048 + e_ * 128 + 128],
                         yg[:, e_ * TAU:(e_ + 1) * TAU],
                         start=(e_ == 0), stop=(e_ == NCHUNK - 1)),
                         [(sWs[c % 2], sWs[c % 2].n), (sV, sV.n)] if e_ == 0 else [],
                         inc=sT if e_ == NCHUNK - 1 else None)
                k.on("vector", lambda e, c=c, slot=slot: e.tensor_copy(
                     outb[:, (c % 2) * TAU:(c % 2 + 1) * TAU], pool[slot][:]),
                     [(sT, sT.n), (sAOp[c % 2], sAOp[c % 2].n)])
                k.dma("sync", attn0_fm[c * 128:(c + 1) * 128, t0:t0 + TAU],
                      outb[:, (c % 2) * TAU:(c % 2 + 1) * TAU],
                      waits=[(sV, sV.n)], inc=sAOp[c % 2])
        k.barrier()


def build_l0_attn(w, phases=99, ntau=None):
    nc = bass.Bass("TRN2", target_bir_lowering=False)
    keys = ["L0_maa_w1", "L0_w2p", "L0_decay_w1", "L0_decay_w2", "maskU",
            "L0_maa_xc", "L0_maa_rc", "L0_maa_kc", "L0_maa_vc", "L0_maa_wc",
            "L0_maa_gc", "L0_time_decayc", "L0_bgc",
            "L0_WqT", "L0_WkT", "L0_WvT", "L0_WgT", "L0_WoT"]
    g = declare_weight_inputs(nc, {kk: w[kk] for kk in keys})
    xn0 = dram(nc, "xn0", [T, D], BF16, "ExternalInput")
    a0 = dram(nc, "attn0_fm", [D, T], BF16, "ExternalOutput")
    kend_dram = dram(nc, "kend_dram", [D, T], BF16)
    v_dram = dram(nc, "v_dram", [D, T], BF16)
    wbf = {}
    k = K(nc)
    sPre = k.sem("precast")
    for short, nm in [("q", "L0_WqT"), ("k", "L0_WkT"), ("v", "L0_WvT"),
                      ("g", "L0_WgT"), ("o", "L0_WoT")]:
        wbf[short] = dram(nc, nm + "_bf", [D, D], BF16)
        for r0 in range(0, D, 256):
            k.dma("gpsimd", wbf[short][r0:r0 + 256, :], g[nm][r0:r0 + 256, :], inc=sPre)
    k.barrier()
    emit_l0_attn(k, g, xn0, kend_dram, v_dram, a0, wbf, phases, ntau)
    k.finalize()
    return nc


def emit_l1_attn(k, g, xn_tok, qk_dram, v1_dram, attn1_tok):
    TAU = 512
    SC = 1.0 / math.sqrt(128.0)
    sX = k.sem("ma_x")
    sW = k.sem("ma_w")
    sWh = [k.sem("ma_wh0"), k.sem("ma_wh1")]
    sQp = [k.sem("ma_qp0"), k.sem("ma_qp1")]
    sHL = k.sem("ma_hl")
    sAO = [k.sem("ma_ao0"), k.sem("ma_ao1")]
    sT, sV, sS = k.sT, k.sV, k.sS
    with ExitStack() as st:
        sb = lambda n, s, d: st.enter_context(k.nc.sbuf_tensor(k._uname(n), s, d))
        ps = lambda n, s: st.enter_context(k.nc.psum_tensor(k._uname(n), s, F32))
        xnT = sb("ma_xnT", [128, NCHUNK * TAU], BF16)
        wqk = sb("ma_wqk", [128, 2 * 2048], BF16)
        wbig = sb("ma_wbig", [128, NCHUNK * 2048], BF16)
        bq = sb("ma_bq", [128, 16], F32)
        bk = sb("ma_bk", [128, 16], F32)
        bvrow = sb("ma_bvrow", [1, 2048], BF16)
        borow = sb("ma_borow", [1, 2048], BF16)
        onesb = sb("ma_onesb", [1, 128], BF16)
        onesf = sb("ma_onesf", [1, 128], F32)
        onescol = sb("ma_onescol", [128, 1], BF16)
        stq = sb("ma_stq", [128, 2 * TAU], BF16)
        vstg = sb("ma_vstg", [128, 2 * 2048], BF16)
        masks = sb("ma_masks", [128, 4 * TAU], BF16)
        qh = sb("ma_qh", [128, 2048], BF16)
        kh = sb("ma_kh", [128, 2048], BF16)
        vh = sb("ma_vh", [128, 2048], BF16)
        sqb = sb("ma_sqb", [128, 2048], BF16)
        ptile = sb("ma_ptile", [128, 2 * TAU], BF16)
        zT = sb("ma_zT", [128, 16 * 2048], BF16)
        ILsb = sb("ma_ILsb", [128, TAU], F32)
        nsc = sb("ma_nsc", [1, 16], F32)
        mhc = sb("ma_mhc", [128, 1], F32)
        aostg = sb("ma_aostg", [128, 2 * 2048], BF16)
        pool = [ps(f"ma_p{j}", [128, TAU]) for j in range(4)]
        psZ = ps("ma_psZ", [128, TAU])
        psL = ps("ma_psL", [128, TAU])
        psB = ps("ma_psB", [128, TAU])

        k._ma_gm = {}
        k.dma("gpsimd", bq[:], g["L1_bQc"], inc=sW)
        k.dma("gpsimd", bk[:], g["L1_bKc"], inc=sW)
        k.dma("gpsimd", bvrow[:], g["L1_bV_row"], inc=sW)
        k.dma("gpsimd", borow[:], g["L1_bO_row"], inc=sW)
        k.dma("gpsimd", onesb[:], g["ones_bf"], inc=sW)
        k.dma("gpsimd", onesf[:], g["ones_f32"], inc=sW)
        k.dma("gpsimd", onescol[:], g["ones_colbf"], inc=sW)
        for o in range(4):
            k.dma("gpsimd", masks[:, o * TAU:(o + 1) * TAU], g[f"maskD{o}"], inc=sW)
        for e_ in range(NCHUNK):
            k.dma("gpsimd", wbig[:, e_ * 2048:(e_ + 1) * 2048],
                  g["L1_WV_flat"][e_ * 128:(e_ + 1) * 128, :], inc=sW)
        wpre = sW.n

        for tg in range(T // TAU):
            t0 = tg * TAU
            for c in range(NCHUNK):
                k.dma("sync", xnT[:, c * TAU:(c + 1) * TAU],
                      xn_tok[t0:t0 + TAU, c * 128:(c + 1) * 128],
                      waits=[(sV, sV.n), (sT, sT.n), (sS, sS.n)], inc=sX,
                      transpose=True)
            xmark = (sX, sX.n)
            for (wname, brow, off) in [("L1_WQ", bq, 0), ("L1_WK", bk, D)]:
                k.dma("gpsimd", wqk[:, 0:2048].rearrange("p (c m) -> p c m", c=NCHUNK),
                      g[wname][0].rearrange("(c p) m -> p c m", p=128),
                      waits=[(sT, sT.n)], inc=sWh[0])
                for h in range(16):
                    par = h % 2
                    if h + 1 < 16:
                        k.dma("gpsimd",
                              wqk[:, (1 - par) * 2048:(2 - par) * 2048].rearrange(
                                  "p (c m) -> p c m", c=NCHUNK),
                              g[wname][h + 1].rearrange("(c p) m -> p c m", p=128),
                              waits=[(sT, sT.n)], inc=sWh[1 - par])
                    slot = pool[2 + par]
                    for e_ in range(NCHUNK):
                        k.on("tensor", lambda e, e_=e_, par=par, slot=slot: e.matmul(
                             slot[:],
                             wqk[:, par * 2048 + e_ * 128:par * 2048 + e_ * 128 + 128],
                             xnT[:, e_ * TAU:(e_ + 1) * TAU],
                             start=(e_ == 0), stop=(e_ == NCHUNK - 1)),
                             [(sWh[par], sWh[par].n), xmark,
                              k._ma_gm.get(("qk", par), (sT, 0))]
                             if e_ == 0 else [], inc=sT if e_ == NCHUNK - 1 else None)
                    k.on("scalar", lambda e, h=h, par=par, brow=brow, slot=slot:
                         e.activation(stq[:, par * TAU:(par + 1) * TAU], slot[:],
                                      AF.Identity, bias=brow[:, h:h + 1]),
                         [(sT, sT.n), (sQp[par], sQp[par].n), (sW, wpre)])
                    k._ma_gm[("qk", par)] = (sS, sS.n)
                    k.dma("sync", qk_dram[off + h * 128:off + (h + 1) * 128,
                          t0:t0 + TAU], stq[:, par * TAU:(par + 1) * TAU],
                          waits=[(sS, sS.n)], inc=sQp[par])
            for tt in range(4):
                trel = tt * 128
                par = tt % 2
                for e_ in range(NCHUNK):
                    for d4 in range(4):
                        k.on("tensor", lambda e, e_=e_, d4=d4, trel=trel: e.matmul(
                             pool[d4][:],
                             xnT[:, e_ * TAU + trel:e_ * TAU + trel + 128],
                             wbig[:, e_ * 2048 + d4 * 512:e_ * 2048 + d4 * 512 + 512],
                             start=(e_ == 0), stop=False),
                             [(sW, wpre), k._ma_gm.get(("v", 0), (sT, 0)),
                              k._ma_gm.get(("qk", 0), (sT, 0)),
                              k._ma_gm.get(("qk", 1), (sT, 0))]
                             if e_ == 0 and d4 == 0 else [])
                for d4 in range(4):
                    k.on("tensor", lambda e, d4=d4: e.matmul(
                         pool[d4][:], onesb[:], bvrow[:, d4 * 512:(d4 + 1) * 512],
                         start=False, stop=True), [],
                         inc=sT if d4 == 3 else None)
                for d4 in range(4):
                    k.on("vector", lambda e, d4=d4, par=par: e.tensor_copy(
                         vstg[:, par * 2048 + d4 * 512:par * 2048 + (d4 + 1) * 512],
                         pool[d4][:]),
                         [(sT, sT.n), (sQp[par], sQp[par].n)] if d4 == 0 else [])
                k._ma_gm[("v", 0)] = (sV, sV.n)
                k.dma("sync", v1_dram[t0 + trel:t0 + trel + 128, :],
                      vstg[:, par * 2048:(par + 1) * 2048],
                      waits=[(sV, sV.n)], inc=sQp[par])
        k.barrier()
        for h in range(16):
            k.dma("gpsimd", wbig[:, h * 2048:(h + 1) * 2048], g["L1_WO"][h], inc=sW)

        for h in range(16):
            k.dma("sync", qh[:], qk_dram[h * 128:(h + 1) * 128, :],
                  waits=[(sT, sT.n), (sS, sS.n)], inc=sHL)
            k.dma("sync", kh[:], qk_dram[D + h * 128:D + (h + 1) * 128, :],
                  waits=[(sT, sT.n)], inc=sHL)
            k.dma("sync", vh[:].rearrange("p (tt hd) -> p tt hd", tt=NTILE),
                  v1_dram[:, h * 128:(h + 1) * 128].rearrange(
                      "(tt p) hd -> p tt hd", p=128),
                  waits=[(sT, sT.n)], inc=sHL)
            hmark = (sHL, sHL.n)
            for j, src in [(0, qh), (1, kh)]:
                k.on("scalar", lambda e, src=src: e.activation(sqb[:], src[:],
                     AF.Square), [hmark, (sV, sV.n)])
                for q4 in range(4):
                    k.on("tensor", lambda e, q4=q4: e.matmul(
                         psB[0:1, :], onescol[:], sqb[:, q4 * 512:(q4 + 1) * 512],
                         start=True, stop=True), [(sS, sS.n), (sV, sV.n)],
                         inc=sT)
                    k.on("vector", lambda e, j=j, q4=q4: e.tensor_reduce(
                         nsc[0:1, j * 4 + q4:j * 4 + q4 + 1], psB[0:1, :], AX.X,
                         ALU.max), [(sT, sT.n)])
            k.on("vector", lambda e: e.tensor_reduce(nsc[0:1, 8:9], nsc[0:1, 0:4],
                 AX.X, ALU.max), [])
            k.on("vector", lambda e: e.tensor_reduce(nsc[0:1, 9:10], nsc[0:1, 4:8],
                 AX.X, ALU.max), [])
            k.on("vector", lambda e: e.tensor_tensor(nsc[0:1, 10:11], nsc[0:1, 8:9],
                 nsc[0:1, 9:10], ALU.mult), [])
            k.on("scalar", lambda e: e.activation(nsc[0:1, 11:12], nsc[0:1, 10:11],
                 AF.Sqrt), [(sV, sV.n)])
            k.on("vector", lambda e: e.tensor_scalar_mul(nsc[0:1, 12:13],
                 nsc[0:1, 11:12], -SC), [(sS, sS.n)])
            k.on("tensor", lambda e: e.matmul(psB[:, 0:1], onesf[:],
                 nsc[0:1, 12:13], start=True, stop=True), [(sV, sV.n)], inc=sT)
            k.on("vector", lambda e: e.tensor_copy(mhc[:], psB[:, 0:1]),
                 [(sT, sT.n)])
            k._ma_gm[("mh",)] = (sV, sV.n)
            for qg in range(4):
                nkt = 4 * qg + 4
                pm = {}

                def emit_zl(kt, nkt=nkt, qg=qg):
                    par = kt % 2
                    zw = list(pm[kt])
                    if kt == 0:
                        zw.append(k._ma_gm.get(("zn",), (sT, 0)))
                    k.on("tensor", lambda e, kt=kt, par=par, nkt=nkt: e.matmul(
                         psZ[:], vh[:, kt * 128:(kt + 1) * 128],
                         ptile[:, par * TAU:(par + 1) * TAU],
                         start=(kt == 0), stop=(kt == nkt - 1)), zw)
                    k.on("tensor", lambda e, kt=kt, par=par, nkt=nkt: e.matmul(
                         psL[0:1, :], onescol[:],
                         ptile[:, par * TAU:(par + 1) * TAU],
                         start=(kt == 0), stop=(kt == nkt - 1)), [],
                         inc=sT if kt == nkt - 1 else None)

                for kt in range(nkt):
                    par = kt % 2
                    k.on("tensor", lambda e, kt=kt, qg=qg, par=par: e.matmul(
                         pool[par][:], kh[:, kt * 128:(kt + 1) * 128],
                         qh[:, qg * 512:(qg + 1) * 512], start=True, stop=True),
                         [k._ma_gm.get(("sc", par), (sT, 0)),
                          k._ma_gm.get(("v", 0), (sT, 0))], inc=sT)
                    k.on("scalar", lambda e, par=par: e.activation(
                         ptile[:, par * TAU:(par + 1) * TAU], pool[par][:], AF.Exp,
                         bias=mhc[:], scale=SC),
                         [(sT, sT.n), k._ma_gm[("mh",)]])
                    k._ma_gm[("sc", par)] = (sS, sS.n)
                    o = kt - 4 * qg
                    if o >= 0:
                        k.on("vector", lambda e, par=par, o=o: e.tensor_mul(
                             ptile[:, par * TAU:(par + 1) * TAU],
                             ptile[:, par * TAU:(par + 1) * TAU],
                             masks[:, o * TAU:(o + 1) * TAU]),
                             [(sS, sS.n), (sW, wpre)])
                        pm[kt] = [(sV, sV.n)]
                    else:
                        pm[kt] = [(sS, sS.n)]
                    if kt >= 1:
                        emit_zl(kt - 1)
                emit_zl(nkt - 1)
                k.on("vector", lambda e: e.reciprocal(ILsb[0:1, :], psL[0:1, :]),
                     [(sT, sT.n)])
                k.on("tensor", lambda e: e.matmul(psB[:], onesf[:], ILsb[0:1, :],
                     start=True, stop=True), [(sV, sV.n)], inc=sT)
                k.on("vector", lambda e: e.tensor_copy(ILsb[:], psB[:]),
                     [(sT, sT.n)])
                k.on("vector", lambda e, h=h, qg=qg: e.tensor_mul(
                     zT[:, h * 2048 + qg * 512:h * 2048 + (qg + 1) * 512],
                     psZ[:], ILsb[:]), [])
                k._ma_gm[("zn",)] = (sV, sV.n)
        k.barrier()
        for tt in range(NTILE):
            trel = tt * 128
            par = tt % 2
            for h in range(16):
                for d4 in range(4):
                    k.on("tensor", lambda e, h=h, d4=d4, trel=trel: e.matmul(
                         pool[d4][:], zT[:, h * 2048 + trel:h * 2048 + trel + 128],
                         wbig[:, h * 2048 + d4 * 512:h * 2048 + d4 * 512 + 512],
                         start=(h == 0), stop=False),
                         [(sW, sW.n), (sV, sV.n)] if h == 0 and d4 == 0 else [])
            for d4 in range(4):
                k.on("tensor", lambda e, d4=d4: e.matmul(
                     pool[d4][:], onesb[:], borow[:, d4 * 512:(d4 + 1) * 512],
                     start=False, stop=True), [], inc=sT if d4 == 3 else None)
            for d4 in range(4):
                k.on("vector", lambda e, d4=d4, par=par: e.tensor_copy(
                     aostg[:, par * 2048 + d4 * 512:par * 2048 + (d4 + 1) * 512],
                     pool[d4][:]),
                     [(sT, sT.n), (sAO[par], sAO[par].n)] if d4 == 0 else [])
            k.dma("sync", attn1_tok[trel:trel + 128, :],
                  aostg[:, par * 2048:(par + 1) * 2048],
                  waits=[(sV, sV.n)], inc=sAO[par])
        k.barrier()


def build_l1_attn(w):
    nc = bass.Bass("TRN2", target_bir_lowering=False)
    keys = ["L1_WQ", "L1_WK", "L1_WV_flat", "L1_WO", "L1_bQc", "L1_bKc",
            "L1_bV_row", "L1_bO_row", "ones_bf", "ones_f32", "ones_colbf",
            "maskD0", "maskD1", "maskD2", "maskD3"]
    g = declare_weight_inputs(nc, {kk: w[kk] for kk in keys})
    xn1 = dram(nc, "xn1", [T, D], BF16, "ExternalInput")
    a1 = dram(nc, "attn1_tok", [T, D], BF16, "ExternalOutput")
    qk_dram = dram(nc, "qk_dram", [2 * D, T], BF16)
    v1_dram = dram(nc, "v1_dram", [T, D], BF16)
    k = K(nc)
    emit_l1_attn(k, g, xn1, qk_dram, v1_dram, a1)
    k.finalize()
    return nc


def emit_l1_moe(k, g, wbf, xn1_tok, xn1_f32, xt1_tok, ffn1_tok):
    TAU = 512
    NE = 6
    BIG = 1e9
    sX = k.sem("mo_x")
    sW = k.sem("mo_w")
    sWs = [k.sem(f"mo_ws{i}") for i in range(4)]
    sWd = [k.sem("mo_wd0"), k.sem("mo_wd1")]
    sR = k.sem("mo_r")
    sOp = [k.sem("mo_op0"), k.sem("mo_op1")]
    sT, sV, sS = k.sT, k.sV, k.sS
    with ExitStack() as st:
        sb = lambda n, s, d: st.enter_context(k.nc.sbuf_tensor(k._uname(n), s, d))
        ps = lambda n, s: st.enter_context(k.nc.psum_tensor(k._uname(n), s, F32))
        xtT = sb("mo_xtT", [128, NCHUNK * TAU], BF16)
        h1 = sb("mo_h1", [128, 8 * TAU], BF16)
        h2 = sb("mo_h2", [128, 8 * TAU], BF16)
        sh = sb("mo_sh", [128, NCHUNK * TAU], BF16)
        acc = sb("mo_acc", [128, 4 * 2048], F32)
        wdn2 = [sb("mo_wdnA", [128, 8 * 2048], BF16),
                sb("mo_wdnB", [128, 8 * 2048], BF16)]
        wslab = sb("mo_wslab", [128, 4 * 2048], BF16)
        wgsb = sb("mo_wgsb", [128, NCHUNK * 8], BF16)
        swsb = sb("mo_swsb", [128, 2 * TAU], BF16)
        gate = sb("mo_gate", [128, 8], F32)
        tmp8 = sb("mo_tmp8", [128, 8], F32)
        m8 = sb("mo_m8", [128, 8], F32)
        p8 = sb("mo_p8", [128, 8], F32)
        gcols = sb("mo_gcols", [128, 8], F32)
        wcols = sb("mo_wcols", [128, 4 * 8], F32)
        bup = sb("mo_bup", [128, 2 * 8], F32)
        bsw = sb("mo_bsw", [128, 2 * 8], F32)
        bsu = sb("mo_bsu", [128, 16], F32)
        bsa = sb("mo_bsa", [128, 8], F32)
        bgrow = sb("mo_bgrow", [1, 8], BF16)
        bdnrow = sb("mo_bdnrow", [1, 6 * 2048], BF16)
        bsdrow = sb("mo_bsdrow", [1, 2048], BF16)
        onesb = sb("mo_onesb", [1, 128], BF16)
        res = sb("mo_res", [128, 2048], F32)
        stg = sb("mo_stg", [128, 2 * 2048], BF16)
        pool = [ps(f"mo_p{j}", [128, TAU]) for j in range(8)]

        k.dma("gpsimd", onesb[:], g["ones_bf"], inc=sW)
        k.dma("gpsimd", bsu[:], g["L1_bsuc"], inc=sW)
        k.dma("gpsimd", bsa[:], g["L1_bsac"], inc=sW)
        k.dma("gpsimd", bsdrow[:], g["L1_bsd_row"], inc=sW)
        for n in range(NE):
            k.dma("gpsimd", bdnrow[:, n * 2048:(n + 1) * 2048],
                  g["L1_bdn_row"][n], inc=sW)

        slabm = [0, 0]
        projm = {}

        def pw(slot):
            return [projm.get(slot, (sT, 0))]

        def dn_guards(bs):
            gw = pw(("dn", bs))
            if bs == 0:
                gw += pw(id(pool[2])) + pw(id(pool[3]))
            else:
                gw += pw(id(pool[6]))
            return gw

        def colslab(dst_slot, srcap, nch, tmark):
            d3 = wslab[:, dst_slot * 2048:dst_slot * 2048 + nch * 128].rearrange(
                "p (c i) -> p c i", c=nch)
            s3 = srcap.rearrange("(c p) i -> p c i", p=128)
            k.dma("sync", d3, s3, waits=[(sT, tmark)], inc=sWs[dst_slot])

        ring_tmarks = {}

        def ring_proj(srcs, nch, rhs_fn, epi, extra=()):
            ng = len(srcs)
            tmarks = ring_tmarks
            for pre in range(min(4 - 1, ng)):
                colslab(pre % 4, srcs[pre], nch, tmarks.get(pre % 4, 0))
            for i in range(ng):
                isl = i % 4
                if i + 3 < ng:
                    colslab((i + 3) % 4, srcs[i + 3], nch,
                            tmarks.get((i + 3) % 4, 0))
                slot = pool[2 + i % 2]
                proj_slab(nch, isl, slot, rhs_fn, extra=extra)
                tmarks[isl] = sT.n
                epi(i, slot)

        def proj_slab(nch_k, slab_slot, dst_ps, rhs_fn, extra=()):
            for e_ in range(nch_k):
                k.on("tensor", lambda e, e_=e_, ss=slab_slot, dst=dst_ps,
                     rhs=rhs_fn(e_): e.matmul(
                     dst[:], wslab[:, ss * 2048 + e_ * 128:ss * 2048 + e_ * 128 + 128],
                     rhs, start=(e_ == 0), stop=(e_ == nch_k - 1)),
                     ([(sWs[slab_slot], sWs[slab_slot].n)] + pw(id(dst_ps))
                      + pw(("dn", 0)) + list(extra)) if e_ == 0 else [],
                     inc=sT if e_ == nch_k - 1 else None)

        for j in range(2):
            xin = xn1_tok if j == 0 else xt1_tok
            xout = xt1_tok if j == 0 else ffn1_tok
            k.dma("gpsimd", wgsb[:].rearrange("p (c m) -> p c m", c=NCHUNK),
                  g["L1_Wg"][j].rearrange("(c p) m -> p c m", p=128),
                  waits=[(sT, sT.n)], inc=sW)
            k.dma("gpsimd", bgrow[:], g["L1_bg_row"][j], waits=[(sT, sT.n)], inc=sW)
            wpre = sW.n
            dnm = [sT.n, sT.n]
            for tg in range(T // TAU):
                t0 = tg * TAU
                for c in range(NCHUNK):
                    k.dma("sync", xtT[:, c * TAU:(c + 1) * TAU],
                          xin[t0:t0 + TAU, c * 128:(c + 1) * 128],
                          waits=[(sV, sV.n), (sT, sT.n), (sS, sS.n)], inc=sX,
                          transpose=True)
                xmark = (sX, sX.n)
                for tt in range(4):
                    trel = tt * 128
                    gps = pool[6]
                    for e_ in range(NCHUNK):
                        k.on("tensor", lambda e, e_=e_, trel=trel, gps=gps: e.matmul(
                             gps[:, 0:8], xtT[:, e_ * TAU + trel:e_ * TAU + trel + 128],
                             wgsb[:, e_ * 8:(e_ + 1) * 8],
                             start=(e_ == 0), stop=False),
                             ([(sW, wpre), xmark, (sV, sV.n)] + pw(("dn", 4)))
                             if e_ == 0 else [])
                    k.on("tensor", lambda e, gps=gps: e.matmul(gps[:, 0:8], onesb[:],
                         bgrow[:], start=False, stop=True), [], inc=sT)
                    k.on("vector", lambda e, gps=gps: e.tensor_copy(gate[:], gps[:, 0:8]),
                         [(sT, sT.n)])
                    k.on("vector", lambda e: e.tensor_reduce(gcols[:, 0:1], gate[:],
                         AX.X, ALU.min), [])
                    k.on("vector", lambda e: e.tensor_single_scalar(tmp8[:], gate[:],
                         gcols[:, 0:1], ALU.is_equal), [])
                    k.on("vector", lambda e: e.scalar_tensor_tensor(tmp8[:], tmp8[:],
                         BIG, gate[:], ALU.mult, ALU.add), [])
                    k.on("vector", lambda e: e.tensor_reduce(gcols[:, 1:2], tmp8[:],
                         AX.X, ALU.min), [])
                    k.on("vector", lambda e: e.tensor_single_scalar(m8[:], gate[:],
                         gcols[:, 1:2], ALU.is_gt), [])
                    k.on("vector", lambda e: e.tensor_reduce(gcols[:, 2:3], gate[:],
                         AX.X, ALU.max), [])
                    k.on("vector", lambda e: e.tensor_scalar_mul(gcols[:, 3:4],
                         gcols[:, 2:3], -1.0), [])
                    k.on("scalar", lambda e: e.activation(p8[:], gate[:], AF.Exp,
                         bias=gcols[:, 3:4]), [(sV, sV.n)])
                    k.on("vector", lambda e: e.tensor_mul(p8[:], p8[:], m8[:]),
                         [(sS, sS.n)])
                    k.on("vector", lambda e: e.tensor_reduce(gcols[:, 4:5], p8[:],
                         AX.X, ALU.add), [])
                    k.on("vector", lambda e: e.reciprocal(gcols[:, 5:6],
                         gcols[:, 4:5]), [])
                    k.on("vector", lambda e, tt=tt: e.tensor_single_scalar(
                         wcols[:, tt * 8:(tt + 1) * 8], p8[:], gcols[:, 5:6],
                         ALU.mult), [])
                projm[id(pool[6])] = (sV, sV.n)
                gate_done = (sV, sV.n)
                for n in range(NE):
                    wsl = n % 2
                    if n == 0:
                        for e_ in range(8):
                            k.dma("sync", wdn2[0][:, e_ * 2048:(e_ + 1) * 2048],
                                  wbf["dn"][0][e_ * 128:(e_ + 1) * 128, :],
                                  waits=[(sT, dnm[0])] if e_ == 0 else [],
                                  inc=sWd[0])
                    if n + 1 < NE:
                        for e_ in range(8):
                            k.dma("sync", wdn2[1 - wsl][:, e_ * 2048:(e_ + 1) * 2048],
                                  wbf["dn"][n + 1][e_ * 128:(e_ + 1) * 128, :],
                                  waits=[(sT, dnm[1 - wsl])] if e_ == 0 else [],
                                  inc=sWd[1 - wsl])
                    k.dma("gpsimd", bup[:, (n % 2) * 8:(n % 2) * 8 + 8],
                          g["L1_bupc"][n], waits=[(sS, sS.n)], inc=sW)
                    k.dma("gpsimd", bsw[:, (n % 2) * 8:(n % 2) * 8 + 8],
                          g["L1_bswc"][n], waits=[(sS, sS.n)], inc=sW)
                    bmark = sW.n
                    def up_epi(i, slot, n=n):
                        k.on("scalar", lambda e, i=i, n=n, slot=slot: e.activation(
                             h1[:, i * TAU:(i + 1) * TAU], slot[:], AF.Identity,
                             bias=bup[:, (n % 2) * 8 + i:(n % 2) * 8 + i + 1]),
                             [(sT, sT.n), (sW, bmark), (sV, sV.n)])
                        projm[id(slot)] = (sS, sS.n)
                    ring_proj([wbf["up"][n][:, i * 128:(i + 1) * 128] for i in range(8)],
                              NCHUNK, lambda e_: xtT[:, e_ * TAU:(e_ + 1) * TAU], up_epi)
                    def sw_epi(f, slot, n=n):
                        fsl = f % 2
                        k.on("scalar", lambda e, f=f, n=n, slot=slot, fsl=fsl:
                             e.activation(swsb[:, fsl * TAU:(fsl + 1) * TAU], slot[:],
                             AF.Silu, bias=bsw[:, (n % 2) * 8 + f:(n % 2) * 8 + f + 1]),
                             [(sT, sT.n), (sW, bmark), (sV, sV.n)])
                        projm[id(slot)] = (sS, sS.n)
                        k.on("vector", lambda e, f=f, fsl=fsl: e.tensor_mul(
                             h2[:, f * TAU:(f + 1) * TAU],
                             swsb[:, fsl * TAU:(fsl + 1) * TAU],
                             h1[:, f * TAU:(f + 1) * TAU]), [(sS, sS.n)])
                    ring_proj([wbf["sw"][n][:, f * 128:(f + 1) * 128] for f in range(8)],
                              8, lambda e_: h1[:, e_ * TAU:(e_ + 1) * TAU], sw_epi,
                              extra=[(sS, sS.n)])
                    h2mark = (sV, sV.n)
                    for tt in range(4):
                        trel = tt * 128
                        bs = (tt % 2) * 4
                        for e_ in range(8):
                            for d4 in range(4):
                                k.on("tensor", lambda e, e_=e_, d4=d4, trel=trel,
                                     bs=bs, wsl=wsl: e.matmul(pool[bs + d4][:],
                                     h2[:, e_ * TAU + trel:e_ * TAU + trel + 128],
                                     wdn2[wsl][:, e_ * 2048 + d4 * 512:
                                               e_ * 2048 + d4 * 512 + 512],
                                     start=(e_ == 0), stop=False),
                                     ([(sWd[wsl], sWd[wsl].n), h2mark, gate_done]
                                      + dn_guards(bs)) if e_ == 0 and d4 == 0 else [])
                        for d4 in range(4):
                            k.on("tensor", lambda e, d4=d4, n=n, bs=bs: e.matmul(
                                 pool[bs + d4][:], onesb[:],
                                 bdnrow[:, n * 2048 + d4 * 512:n * 2048 + (d4 + 1) * 512],
                                 start=False, stop=True), [],
                                 inc=sT if d4 == 3 else None)
                        for d4 in range(4):
                            a = acc[:, tt * 2048 + d4 * 512:tt * 2048 + (d4 + 1) * 512]
                            wc = wcols[:, tt * 8 + n:tt * 8 + n + 1]
                            if n == 0:
                                k.on("vector", lambda e, a=a, wc=wc, d4=d4, bs=bs:
                                     e.tensor_single_scalar(a, pool[bs + d4][:], wc,
                                     ALU.mult), [(sT, sT.n)] if d4 == 0 else [])
                            else:
                                k.on("vector", lambda e, a=a, wc=wc, d4=d4, bs=bs:
                                     e.scalar_tensor_tensor(a, pool[bs + d4][:], wc, a,
                                     ALU.mult, ALU.add), [(sT, sT.n)] if d4 == 0 else [])
                        projm[("dn", bs)] = (sV, sV.n)
                    dnm[wsl] = sT.n
                def su_epi(c, slot):
                    k.on("scalar", lambda e, c=c, slot=slot: e.activation(
                         sh[:, c * TAU:(c + 1) * TAU], slot[:], AF.Identity,
                         bias=bsu[:, c:c + 1]), [(sT, sT.n), (sW, sW.n), (sV, sV.n)])
                    projm[id(slot)] = (sS, sS.n)
                ring_proj([wbf["su"][:, c * 128:(c + 1) * 128] for c in range(NCHUNK)],
                          NCHUNK, lambda e_: xtT[:, e_ * TAU:(e_ + 1) * TAU], su_epi)
                for s2 in range(2):
                    def sa_epi(f, slot, s2=s2):
                        fsl = f % 2
                        k.on("scalar", lambda e, f=f, slot=slot, fsl=fsl: e.activation(
                             swsb[:, fsl * TAU:(fsl + 1) * TAU], slot[:], AF.Silu,
                             bias=bsa[:, f:f + 1]), [(sT, sT.n), (sV, sV.n), (sW, sW.n)])
                        projm[id(slot)] = (sS, sS.n)
                        fi = s2 * 8 + f
                        dstb = h1 if fi < 8 else h2
                        di = fi % 8
                        k.on("vector", lambda e, fi=fi, dstb=dstb, di=di, fsl=fsl:
                             e.tensor_mul(dstb[:, di * TAU:(di + 1) * TAU],
                             swsb[:, fsl * TAU:(fsl + 1) * TAU],
                             sh[:, fi * TAU:(fi + 1) * TAU]), [(sS, sS.n)])
                    ring_proj([wbf["sa"][:, f * 128:(f + 1) * 128] for f in range(8)],
                              8, lambda e_, s2=s2: sh[:, (s2 * 8 + e_) * TAU:
                                                      (s2 * 8 + e_ + 1) * TAU], sa_epi,
                              extra=[(sS, sS.n)])
                sh2mark = (sV, sV.n)
                for hh in range(2):
                    for e_ in range(8):
                        k.dma("sync", wdn2[hh][:, e_ * 2048:(e_ + 1) * 2048],
                              wbf["sd"][(hh * 8 + e_) * 128:(hh * 8 + e_ + 1) * 128, :],
                              waits=[(sT, dnm[hh])] if e_ == 0 else [], inc=sWd[hh])
                for tt in range(4):
                    trel = tt * 128
                    bs = (tt % 2) * 4
                    for hh in range(2):
                        for e_ in range(8):
                            su = hh * 8 + e_
                            srcb = h1 if su < 8 else h2
                            si = su % 8
                            for d4 in range(4):
                                k.on("tensor", lambda e, si=si, srcb=srcb, e_=e_,
                                     d4=d4, trel=trel, bs=bs, hh=hh:
                                     e.matmul(pool[bs + d4][:],
                                     srcb[:, si * TAU + trel:si * TAU + trel + 128],
                                     wdn2[hh][:, e_ * 2048 + d4 * 512:
                                               e_ * 2048 + d4 * 512 + 512],
                                     start=(hh == 0 and e_ == 0), stop=False),
                                     ([(sWd[0], sWd[0].n), (sWd[1], sWd[1].n),
                                       sh2mark] + dn_guards(bs))
                                     if hh == 0 and e_ == 0 and d4 == 0 else [])
                    for d4 in range(4):
                        k.on("tensor", lambda e, d4=d4, bs=bs: e.matmul(
                             pool[bs + d4][:], onesb[:],
                             bsdrow[:, d4 * 512:(d4 + 1) * 512],
                             start=False, stop=True), [],
                             inc=sT if d4 == 3 else None)
                    for d4 in range(4):
                        a = acc[:, tt * 2048 + d4 * 512:tt * 2048 + (d4 + 1) * 512]
                        k.on("vector", lambda e, a=a, d4=d4, bs=bs:
                             e.tensor_tensor(a, a, pool[bs + d4][:], ALU.add),
                             [(sT, sT.n)] if d4 == 0 else [])
                    projm[("dn", bs)] = (sV, sV.n)
                dnm = [sT.n, sT.n]
                for tt in range(4):
                    trel = tt * 128
                    par = tt % 2
                    k.dma("sync", res[:], xn1_f32[t0 + trel:t0 + trel + 128, :],
                          waits=[(sV, sV.n)], inc=sR)
                    k.on("vector", lambda e, tt=tt, par=par: e.tensor_add(
                         stg[:, par * 2048:(par + 1) * 2048],
                         acc[:, tt * 2048:(tt + 1) * 2048], res[:]),
                         [(sR, sR.n), (sOp[par], sOp[par].n)])
                    k.dma("sync", xout[t0 + trel:t0 + trel + 128, :],
                          stg[:, par * 2048:(par + 1) * 2048],
                          waits=[(sV, sV.n)], inc=sOp[par])
            k.barrier()
            projm.clear()
        k.barrier()


def precast_moe_weights(k, g, nc):
    sPre = k.sem("mo_pre")
    wbf = {"up": [], "sw": [], "dn": []}
    for n in range(6):
        for nm, shp in [("up", (2048, 1024)), ("sw", (1024, 1024)),
                        ("dn", (1024, 2048))]:
            t = dram(nc, f"mo_{nm}{n}_bf", list(shp), BF16)
            src = {"up": "L1_Wup", "sw": "L1_Wsw", "dn": "L1_Wdn"}[nm]
            for r0 in range(0, shp[0], 512):
                k.dma("gpsimd", t[r0:r0 + 512, :], g[src][n][r0:r0 + 512, :],
                      inc=sPre)
            wbf[nm].append(t)
    for nm, src, shp in [("su", "L1_Wsu", (2048, 2048)), ("sa", "L1_Wsa", (1024, 1024)),
                         ("sd", "L1_Wsd", (2048, 2048))]:
        t = dram(nc, f"mo_{nm}_bf", list(shp), BF16)
        for r0 in range(0, shp[0], 512):
            k.dma("gpsimd", t[r0:r0 + 512, :], g[src][r0:r0 + 512, :], inc=sPre)
        wbf[nm] = t
    return wbf


MOE_KEYS = ["L1_Wg", "L1_bg_row", "L1_Wup", "L1_bupc", "L1_Wsw", "L1_bswc",
            "L1_Wdn", "L1_bdn_row", "L1_Wsu", "L1_bsuc", "L1_Wsa", "L1_bsac",
            "L1_Wsd", "L1_bsd_row", "ones_bf"]


def build_l1_moe(w, debug=False):
    nc = bass.Bass("TRN2", target_bir_lowering=False)
    g = declare_weight_inputs(nc, {kk: w[kk] for kk in MOE_KEYS})
    xn1 = dram(nc, "xn1", [T, D], BF16, "ExternalInput")
    xn1f = dram(nc, "xn1f", [T, D], F32, "ExternalInput")
    f1 = dram(nc, "ffn1_tok", [T, D], BF16, "ExternalOutput")
    xt1 = dram(nc, "xt1_tok", [T, D], BF16)
    k = K(nc)
    if debug:
        k._mo_dbg = {
            "wcols": dram(nc, "dbg_wcols", [128, 32], F32, "ExternalOutput"),
            "h1": dram(nc, "dbg_h1", [128, 8 * 512], BF16, "ExternalOutput"),
            "h2": dram(nc, "dbg_h2", [128, 8 * 512], BF16, "ExternalOutput"),
            "acc1": dram(nc, "dbg_acc1", [128, 4 * 2048], F32, "ExternalOutput"),
            "acc2": dram(nc, "dbg_acc2", [128, 4 * 2048], F32, "ExternalOutput"),
            "sh": dram(nc, "dbg_sh", [128, 16 * 512], BF16, "ExternalOutput"),
        }
    wbf = precast_moe_weights(k, g, nc)
    k.barrier()
    emit_l1_moe(k, g, wbf, xn1, xn1f, xt1, f1)
    k.finalize()
    return nc


ALL_WKEYS = (_L0A_KEYS + _L0F_KEYS + _L1A_KEYS + MOE_KEYS)


def build_fused(w, ret_k=False):
    nc = bass.Bass("TRN2", target_bir_lowering=False)
    keys = sorted(set(ALL_WKEYS))
    g = declare_weight_inputs(nc, {kk: w[kk] for kk in keys})
    x0 = dram(nc, "x0", [T, D], F32, "ExternalInput")
    out = dram(nc, "out", [T, D], F32, "ExternalOutput")
    xn0 = dram(nc, "s_xn0", [T, D], BF16)
    a0 = dram(nc, "s_a0fm", [D, T], BF16)
    f0 = dram(nc, "s_f0fm", [D, T], BF16)
    x1 = dram(nc, "s_x1", [T, D], F32)
    xn1 = dram(nc, "s_xn1", [T, D], BF16)
    xn1f = dram(nc, "s_xn1f", [T, D], F32)
    a1 = dram(nc, "s_a1", [T, D], BF16)
    f1 = dram(nc, "s_f1", [T, D], BF16)
    xt1 = dram(nc, "s_xt1", [T, D], BF16)
    kend_dram = dram(nc, "s_kend", [D, T], BF16)
    v_dram = dram(nc, "s_v", [D, T], BF16)
    qk_dram = dram(nc, "s_qk", [2 * D, T], BF16)
    v1_dram = dram(nc, "s_v1", [T, D], BF16)

    k = K(nc)
    sPre = k.sem("precast")
    wbf0 = {}
    for short, nm in [("q", "L0_WqT"), ("k", "L0_WkT"), ("v", "L0_WvT"),
                      ("g", "L0_WgT"), ("o", "L0_WoT")]:
        wbf0[short] = dram(nc, nm + "_bf", [D, D], BF16)
        for r0 in range(0, D, 256):
            k.dma("gpsimd", wbf0[short][r0:r0 + 256, :], g[nm][r0:r0 + 256, :],
                  inc=sPre)
    wbf1 = precast_moe_weights(k, g, nc)
    wbf_f = precast_ffn_weights(k, g, nc)
    k.mark("ln0")
    emit_combine_ln(k, g, x0, [], None, xn0, None)
    k.barrier()
    k.mark("rwkv")
    emit_l0_attn(k, g, xn0, kend_dram, v_dram, a0, wbf0)
    k.mark("ffn0")
    with ExitStack() as st3:
        xnT = st3.enter_context(nc.sbuf_tensor(k._uname("fz_xnT"),
                                               [128, NCHUNK * T], BF16))
        sL = k.sem("fz_ldfm")
        emit_load_fm(k, xnT, xn0, sL)
        k.barrier()
        emit_l0_ffn(k, g, xnT, f0, wbf_f)
    k.mark("comb0")
    emit_combine_ln(k, g, x0, [(a0, "fm_bf"), (f0, "fm_bf")], x1, xn1, xn1f)
    k.mark("mha")
    emit_l1_attn(k, g, xn1, qk_dram, v1_dram, a1)
    k.mark("moe")
    emit_l1_moe(k, g, wbf1, xn1, xn1f, xt1, f1)
    k.mark("comb1")
    emit_combine_ln(k, g, x1, [(a1, "tok_bf"), (f1, "tok_bf")], out, None, None)
    k.mark("end")
    k.finalize()
    if ret_k:
        return nc, k
    return nc


def kernel_unfused(**inputs):
    return _kernel_multi(**inputs)


def kernel(**inputs):
    NC = 8
    cores = list(range(NC))
    w = prep_weights(inputs)
    x_all = np.ascontiguousarray(np.asarray(inputs["L0_attn_hidden_states"],
                                            dtype=np.float32))
    keys = sorted(set(ALL_WKEYS))
    wrep = {kk: w[kk] for kk in keys}
    nc = build_fused(w)
    in_maps = [dict(x0=x_all[c], **wrep) for c in cores]
    res = run_bass_kernel_spmd(nc, in_maps, core_ids=cores)
    return np.stack([np.asarray(res.results[c]["out"], dtype=np.float32)
                     for c in cores])
